# version 01
# baseline (speedup 1.0000x reference)
import numpy as np
from contextlib import ExitStack

import concourse.bass as bass
import concourse.mybir as mybir
from concourse.bass import IndirectOffsetOnAxis
from concourse import library_config as libcfg
from concourse.bass_utils import run_bass_kernel_spmd

F32 = mybir.dt.float32
BF16 = mybir.dt.bfloat16
U32 = mybir.dt.uint32
I16 = mybir.dt.int16
AF = mybir.ActivationFunctionType
OP = mybir.AluOpType
AX = mybir.AxisListType

NCORES = 8
N = 2048
D = 1024
H = 8
TOK = 2 * N
NCH = TOK // 128
SEG = 512
NSEG_B = N // SEG
FF = 4096
CAP = 5120
FFG = 256
NFFG = CAP // FFG


def _stub_out(k, nc, out, es):
    zt = es.enter_context(nc.sbuf_tensor("stub_z", [128, 512], mybir.dt.float32))
    tz = k.op("gpsimd", lambda e: e.memset(zt[:], 0.0))
    k.wait("sync", tz)
    last = None
    for r in range(TOK // 128):
        for cg in range(2):
            last = k.dma("sync", out[r * 128:(r + 1) * 128, cg * 512:(cg + 1) * 512],
                         zt[:], sem="od")
    k.wait("sync", last)
    k.barrier()


class KB:
    ENGINES = ("sync", "tensor", "vector", "scalar", "gpsimd")

    def __init__(self, nc, es, safe=False):
        self.nc = nc
        self.es = es
        self.safe = safe
        self.items = []
        self.semh = {}
        self.semv = {}
        for s in ("ld", "gd", "od", "cc", "t", "v", "s", "g", "ph"):
            self.semh[s] = es.enter_context(nc.semaphore(s))
            self.semv[s] = 0
        self.block = es.enter_context(nc.Block())

    def on(self, eng):
        def deco(fn):
            self.items.append((eng, fn))
        return deco

    def tok(self, sem):
        return (sem, self.semv[sem])

    def dma(self, eng, out, in_, sem=None, transpose=False, slow=False):
        sem = sem or ("gd" if eng == "gpsimd" else "ld")
        self.semv[sem] += 16
        h, v = self.semh[sem], self.semv[sem]

        @self.on(eng)
        def _(e):
            if transpose:
                e.dma_start_transpose(out, in_).then_inc(h, 16)
            elif slow:
                e.dma_start(out=out, in_=in_,
                            allow_slow_non_contiguous=True).then_inc(h, 16)
            else:
                e.dma_start(out=out, in_=in_).then_inc(h, 16)
        return (sem, v)

    def dma16(self, fn):
        self.semv["gd"] += 16
        h, v = self.semh["gd"], self.semv["gd"]

        @self.on("gpsimd")
        def _(e):
            fn(e).then_inc(h, 16)
        return ("gd", v)

    def op(self, eng, fn, inc=None):
        if inc is None:
            inc = {"tensor": "t", "vector": "v", "scalar": "s", "gpsimd": "g",
                   "sync": "g"}[eng]
        self.semv[inc] += 1
        h, v = self.semh[inc], self.semv[inc]
        safe_wait = (h, v - 1) if self.safe else None

        @self.on(eng)
        def _(e):
            if safe_wait is not None and safe_wait[1] > 0:
                e.wait_ge(safe_wait[0], safe_wait[1])
            fn(e).then_inc(h, 1)
        return (inc, v)

    def eop(self, eng, fn, inc=None):
        incs = {"tensor": "t", "vector": "v", "scalar": "s", "gpsimd": "g",
                "sync": "g"}[eng]
        self.wait(eng, (incs, self.semv[incs]))
        return self.op(eng, fn, inc=inc)

    def raw(self, eng, fn):
        if self.safe:
            self.op(eng, fn)
            return

        @self.on(eng)
        def _(e):
            fn(e)

    DMA_SEMS = ("ld", "gd", "od", "cc")

    def wait(self, eng, *tokens):
        toks = []
        seen = {}
        for t in tokens:
            if t is None or t[1] <= 0:
                continue
            s, v = t
            if s in self.DMA_SEMS:
                v = self.semv[s]
            seen[s] = max(seen.get(s, 0), v)
        toks = list(seen.items())
        if not toks:
            return

        @self.on(eng)
        def _(e):
            for s, v in toks:
                e.wait_ge(self.semh[s], v)

    def cc(self, kind, op, ins, outs):
        self.wait("gpsimd", ("cc", self.semv["cc"]))
        self.semv["cc"] += 1
        h, v = self.semh["cc"], self.semv["cc"]

        @self.on("gpsimd")
        def _(e):
            e.collective_compute(kind, op, replica_groups=[list(range(NCORES))],
                                 ins=ins, outs=outs).then_inc(h, 1)
        return ("cc", v)

    def xwait(self, *engines):
        toks = [self.tok(s) for s in ("t", "v", "s", "g", "ld", "gd")]
        for eng in engines:
            self.wait(eng, *toks)

    def barrier(self):
        self.semv["ph"] += len(self.ENGINES)
        h, v = self.semh["ph"], self.semv["ph"]
        for eng in self.ENGINES:
            @self.on(eng)
            def _(e):
                e.sem_inc(h, 1)
                e.wait_ge(h, v)

    def flush(self):
        items = self.items
        self.items = []
        for eng_name, deco in (("sync", self.block.sync), ("tensor", self.block.tensor),
                               ("vector", self.block.vector), ("scalar", self.block.scalar),
                               ("gpsimd", self.block.gpsimd)):
            sub = [fn for e, fn in items if e == eng_name]
            if not sub:
                continue

            def make(sub):
                def body(engine):
                    for fn in sub:
                        fn(engine)
                return body
            deco(make(sub))


def build_nc(debug=False, phases=4, p0sub=99, safe=False):
    nc = bass.Bass("TRN2", target_bir_lowering=False, debug=False,
                   detect_race_conditions=safe)

    def inp(name, shape, dt=F32):
        return nc.dram_tensor(name, shape, dt, kind="ExternalInput").ap()

    def idram(name, shape, dt):
        return nc.dram_tensor(name, shape, dt, kind="Internal").ap()

    x_tm = inp("x_tm", [TOK, D])
    x_fm = inp("x_fm", [D, TOK])
    cT = inp("cT", [128, 16])
    wqT = inp("wqT", [D, D]); wkT = inp("wkT", [D, D])
    wvT = inp("wvT", [D, D]); wuT = inp("wuT", [D, D])
    woT = inp("woT", [D, D])
    wmod = inp("wmod", [D, 6 * D])
    gwT = inp("gwT", [128, 64])
    w1e = inp("w1e", [D, FF]); w2e = inp("w2e", [FF, D])
    bq_fm = inp("bq_fm", [128, 8]); bk_fm = inp("bk_fm", [128, 8])
    bu_fm = inp("bu_fm", [128, 8])
    b1_fm = inp("b1_fm", [128, 32])
    bk_row = inp("bk_row", [1, D]); bv_row = inp("bv_row", [1, D])
    b2_row = inp("b2_row", [1, D]); bo_row = inp("bo_row", [1, D])
    gb_col = inp("gb_col", [8, 1])
    ascale_fm = inp("ascale_fm", [128, 8])
    core_mask = inp("core_mask", [8, 1])
    core_oh = inp("core_oh", [8, 1])
    slotrow128 = inp("slotrow128", [128, 1])
    tokbase = inp("tokbase", [128, 1])

    wqT_b = idram("wqT_b", [D, D], BF16); wkT_b = idram("wkT_b", [D, D], BF16)
    wvT_b = idram("wvT_b", [D, D], BF16); wuT_b = idram("wuT_b", [D, D], BF16)
    woT_b = idram("woT_b", [D, D], BF16)
    wmod_b = idram("wmod_b", [D, 6 * D], BF16)
    gwT_b = idram("gwT_b", [128, 64], BF16)
    w1_b = idram("w1_b", [D, FF], BF16); w2_b = idram("w2_b", [FF, D], BF16)
    xfm_b = idram("xfm_b", [D, TOK], BF16)
    x2_dram = idram("x2_dram", [TOK, D], F32)
    x2_btm = idram("x2_btm", [TOK, D], BF16)
    h2fm = idram("h2fm", [D, TOK], BF16)
    h2_bounce = idram("h2_bounce", [TOK, D], BF16)
    h2_all = idram("h2_all", [NCORES * TOK, D], BF16)
    SLOTS = NCORES * CAP + 128
    slotmap_in = idram("slotmap_in", [SLOTS, 4], F32)
    slotmap_out = idram("slotmap_out", [SLOTS, 4], F32)
    cnt_in = idram("cnt_in", [128, 8], F32)
    cnt_out = idram("cnt_out", [1024, 8], F32)
    eout_bounce = idram("eout_bounce", [CAP, D], F32)
    modrow_dram = idram("modrow_dram", [16, D], F32)
    ffn_tm = [idram(f"ffn_tm{i}", [FFG, D], BF16) for i in range(2)]
    eout_all = idram("eout_all", [NCORES * CAP, D], F32)
    out = nc.dram_tensor("out", [TOK, D], F32, kind="ExternalOutput").ap()
    dbg = {}
    if debug:
        for nm, shape, dt in (("d_x2", [TOK, D], F32), ("d_h2", [TOK, D], BF16),
                              ("d_gate", [128, NCH], F32), ("d_idxg", [128, NCH], F32),
                              ("d_eout", [CAP, D], F32), ("d_cnt", [8, 8], F32),
                              ("d_slotmap", [NCORES * CAP + 128, 4], F32),
                              ("d_oraw", [D, TOK], BF16), ("d_frow", [1, TOK], F32),
                              ("d_qt", [D, TOK], BF16), ("d_kt", [D, TOK], BF16),
                              ("d_vtm", [TOK, D], BF16), ("d_ut", [D, TOK], BF16),
                              ("d_og", [D, TOK], BF16), ("d_wto", [D, D], BF16),
                              ("d_gabo", [128, D], F32), ("d_mx", [128, NCH], F32),
                              ("d_idxf", [128, NCH], F32), ("d_possel", [128, NCH], F32),
                              ("d_keep", [128, NCH], F32), ("d_tpr", [128, NCH], F32),
                              ("d_cum", [128, NCH * 8], F32), ("d_comb", [128, NCH * 8], F32),
                              ("d_scu", [128, NCH], U32), ("d_xh", [D, TOK], BF16),
                              ("d_ab", [2, TOK], F32), ("d_cmat", [8, 8], F32),
                              ("d_tt32", [32, 8], F32), ("d_ttot", [1, NCH * 8], F32),
                              ("d_mv", [128, 64], F32), ("d_scdbg", [128, NCH * 4], F32),
                              ("d_ffng0", [FFG, D], BF16), ("d_slot4", [128, CAP // 128 * 4], F32),
                              ("d_logits", [128, NCH * 8], F32),
                              ("d_csil", [128, 16], F32),
                              ("d_modsb", [2, 6 * D], F32)):
            dbg[nm] = nc.dram_tensor(nm, shape, dt, kind="ExternalOutput").ap()

    es = ExitStack()
    k = KB(nc, es, safe=safe)
    sbg = lambda name, shape, dt: es.enter_context(nc.sbuf_tensor(name, shape, dt))
    bank = [es.enter_context(nc.psum_tensor(f"bank{i}", [128, 512], F32))
            for i in range(8)]

    ones_r_f = sbg("ones_r_f", [1, 128], F32)
    ones_s_b = sbg("ones_s_b", [128, 1], BF16)
    zeros_b = sbg("zeros_b", [128, 512], BF16)
    triu_f = sbg("triu_f", [128, 128], F32)
    triu_b = sbg("triu_b", [128, 128], BF16)
    su32_f = sbg("su32_f", [32, 32], F32)
    ident_f = sbg("ident_f", [128, 128], F32)
    ident_b = sbg("ident_b", [128, 128], BF16)
    ones32_f = sbg("ones32_f", [32, 1], F32)
    ones_1x1 = sbg("ones_1x1", [1, 1], F32)
    bq_s = sbg("bq_s", [128, 8], F32); bk_s = sbg("bk_s", [128, 8], F32)
    bu_s = sbg("bu_s", [128, 8], F32); b1_s = sbg("b1_s", [128, 32], F32)
    asc_s = sbg("asc_s", [128, 8], F32)
    gbc_s = sbg("gbc_s", [8, 1], F32)
    cmask_s = sbg("cmask_s", [8, 1], F32); coh_s = sbg("coh_s", [8, 1], F32)
    onepca = [sbg(f"onepca{b}", [128, 8], F32) for b in range(2)]
    sa_s = [sbg(f"sa_s{b}", [128, 8], F32) for b in range(2)]
    onepcb = [sbg(f"onepcb{b}", [128, 8], F32) for b in range(2)]
    sbv_s = [sbg(f"sbv_s{b}", [128, 8], F32) for b in range(2)]
    logits_tm = sbg("logits_tm", [128, NCH, 8], F32)
    gate_s = sbg("gate_s", [128, NCH], F32)
    idxg_s = sbg("idxg_s", [128, NCH], U32)

    t_wq = k.dma("gpsimd", wqT_b, wqT); t_wk = k.dma("gpsimd", wkT_b, wkT)
    t_wv = k.dma("gpsimd", wvT_b, wvT); t_wu = k.dma("gpsimd", wuT_b, wuT)
    t_wo = k.dma("gpsimd", woT_b, woT)
    t_wm = k.dma("gpsimd", wmod_b, wmod)
    t_gw = k.dma("gpsimd", gwT_b, gwT)
    t_xf = k.dma("gpsimd", xfm_b, x_fm)
    t_w1 = k.dma("gpsimd", w1_b, w1e)
    t_w2 = k.dma("gpsimd", w2_b, w2e)

    if p0sub < 1:
        _stub_out(k, nc, out, es)
        k.flush()
        es.close()
        return nc
    t_small = None
    for dst, src in ((bq_s[:], bq_fm), (bk_s[:], bk_fm), (bu_s[:], bu_fm),
                     (b1_s[:], b1_fm), (asc_s[:], ascale_fm), (gbc_s[:], gb_col),
                     (cmask_s[:], core_mask), (coh_s[:], core_oh)):
        t_small = k.dma("sync", dst, src)

    for t_ in (ones_r_f, ones_s_b, zeros_b, ones32_f, ones_1x1):
        k.op("gpsimd", lambda e, t_=t_: e.memset(t_[:], 1.0 if t_ is not zeros_b else 0.0))
    tg0 = k.op("gpsimd", lambda e: e.memset(triu_f[:], 1.0))
    k.wait("gpsimd", tg0)
    k.op("gpsimd", lambda e: e.affine_select(triu_f[:], triu_f[:], [[1, 128]],
                                             compare_op=OP.is_ge, fill=0.0, base=0,
                                             channel_multiplier=-1))
    tg0 = k.op("gpsimd", lambda e: e.memset(triu_b[:], 1.0))
    k.wait("gpsimd", tg0)
    k.op("gpsimd", lambda e: e.affine_select(triu_b[:], triu_b[:], [[1, 128]],
                                             compare_op=OP.is_ge, fill=0.0, base=0,
                                             channel_multiplier=-1))
    tg0 = k.op("gpsimd", lambda e: e.memset(su32_f[:], 1.0))
    k.wait("gpsimd", tg0)
    k.op("gpsimd", lambda e: e.affine_select(su32_f[:], su32_f[:], [[1, 32]],
                                             compare_op=OP.is_ge, fill=0.0, base=-1,
                                             channel_multiplier=-1))
    tg0 = k.op("gpsimd", lambda e: e.memset(ident_f[:], 1.0))
    k.wait("gpsimd", tg0)
    k.op("gpsimd", lambda e: e.affine_select(ident_f[:], ident_f[:], [[1, 128]],
                                             compare_op=OP.is_equal, fill=0.0, base=0,
                                             channel_multiplier=-1))
    tg0 = k.op("gpsimd", lambda e: e.memset(ident_b[:], 1.0))
    k.wait("gpsimd", tg0)
    t_const = k.op("gpsimd", lambda e: e.affine_select(ident_b[:], ident_b[:], [[1, 128]],
                                                       compare_op=OP.is_equal, fill=0.0,
                                                       base=0, channel_multiplier=-1))

    if p0sub < 2:
        _stub_out(k, nc, out, es)
        k.flush()
        es.close()
        return nc
    rowsrc = sbg("rowsrc", [1, D], F32)

    def replicate_row(row_ap, dst_sb, wait_toks, bank_i):
        last = None
        for g in range(D // 512):
            dslice = dst_sb[:, g * 512:(g + 1) * 512]
            rslice = row_ap[:, g * 512:(g + 1) * 512]
            k.wait("tensor", *wait_toks, t_const)
            tmm = k.op("tensor", lambda e, rs=rslice, bi=bank_i: e.matmul(
                bank[bi][:, 0:512], ones_r_f[:], rs, start=True, stop=True))
            k.wait("vector", tmm)
            last = k.op("vector", lambda e, ds=dslice, bi=bank_i: e.tensor_copy(
                ds, bank[bi][:, 0:512]))
            k.wait("tensor", last)
        return last

    if p0sub < 3:
        _stub_out(k, nc, out, es)
        k.flush()
        es.close()
        return nc
    csil = sbg("csil", [128, 16], F32)
    cT_s = sbg("cT_s", [128, 16], F32)
    t_c = k.dma("sync", cT_s[:], cT)
    k.wait("scalar", t_c)
    t_cs = k.op("scalar", lambda e: e.activation(csil[:], cT_s[:], AF.Silu))

    if p0sub < 4:
        _stub_out(k, nc, out, es)
        k.flush()
        es.close()
        return nc
    with ExitStack() as p0es:
        wmt = [p0es.enter_context(nc.sbuf_tensor(f"wmt{i}", [128, 6 * D], F32))
               for i in range(2)]
        mod_sb = p0es.enter_context(nc.sbuf_tensor("mod_sb", [2, 6 * D], F32))
        mrow_sb = p0es.enter_context(nc.sbuf_tensor("mrow_sb", [1, 6 * D], F32))
        rowtmp = p0es.enter_context(nc.sbuf_tensor("rowtmp", [1, D], F32))
        rowtmp2 = p0es.enter_context(nc.sbuf_tensor("rowtmp2", [1, D], F32))
        browtmp = p0es.enter_context(nc.sbuf_tensor("browtmp", [1, D], F32))

        t_pe = [None, None]
        for half in range(2):
            t_wml = [None, None]
            tmm6 = [None] * 6
            for dc in range(8):
                k.wait("sync", t_wml[dc % 2], t_pe[dc % 2])
                t_wml[dc % 2] = k.dma("sync", wmt[dc % 2][:],
                                      wmod[dc * 128:(dc + 1) * 128, :])
                k.wait("tensor", t_wml[dc % 2], t_cs, t_const)
                lhs = csil[:, dc * 2:dc * 2 + 2]
                for mg6 in range(6):
                    mg = half * 6 + mg6
                    rhs = wmt[dc % 2][:, mg * 512:(mg + 1) * 512]
                    fn = lambda e, lhs=lhs, rhs=rhs, dc=dc, mg6=mg6: e.matmul(
                        bank[mg6][0:2, 0:512], lhs, rhs,
                        start=(dc == 0), stop=(dc == 7))
                    if dc == 7:
                        tmm6[mg6] = k.op("tensor", fn)
                        if mg6 == 5:
                            t_pe[dc % 2] = tmm6[5]
                    elif mg6 == 5:
                        t_pe[dc % 2] = k.op("tensor", fn)
                    else:
                        k.raw("tensor", fn)
            tv = None
            for mg6 in range(6):
                mg = half * 6 + mg6
                k.wait("vector", tmm6[mg6])
                ds = mod_sb[:, mg * 512:(mg + 1) * 512]
                tv = k.op("vector", lambda e, ds=ds, mg6=mg6: e.tensor_copy(
                    ds, bank[mg6][0:2, 0:512]))
            k.wait("tensor", tv)
        if debug:
            k.wait("sync", tv, t_cs)
            k.dma("sync", dbg["d_modsb"][:, half * 3 * D:(half + 1) * 3 * D],
                  mod_sb[:, half * 3 * D:(half + 1) * 3 * D], sem="od")
            k.dma("sync", dbg["d_csil"], csil[:], sem="od")
        if p0sub < 5:
            _stub_out(k, nc, out, es)
            k.flush()
            es.close()
            return nc

        def clamp_row(src_ap, dst_ap):
            k.wait("vector", k.tok("t"), k.tok("s"), k.tok("ld"))
            k.wait("scalar", k.tok("v"), k.tok("t"), k.tok("ld"))
            t0 = k.op("scalar", lambda e, s=src_ap: e.activation(rowtmp2[:], s, AF.Abs))
            k.wait("vector", t0)
            t1 = k.eop("vector", lambda e: e.tensor_scalar_max(rowtmp2[:], rowtmp2[:], 0.25))
            t2 = k.op("scalar", lambda e, s=src_ap: e.sign(browtmp[:], s))
            k.wait("vector", t1, t2)
            return k.eop("vector", lambda e, d=dst_ap: e.tensor_mul(d, rowtmp2[:], browtmp[:]))

        for b in range(2):
            k.xwait("sync")
            tmr = k.dma("sync", mrow_sb[:], mod_sb[b:b + 1, :])
            k.wait("vector", tmr)
            k.wait("scalar", tmr)
            mrow = mrow_sb[0:1, :]
            for jj, (vec_i, dst, add1) in enumerate((
                    (0, sa_s[b][:], False), (3, sbv_s[b][:], False),
                    (1, onepca[b][:], True), (4, onepcb[b][:], True))):
                k.wait("tensor", tmr, k.tok("v"), t_const)
                tcol = None
                for dc in range(8):
                    sl = mrow[:, vec_i * D + dc * 128: vec_i * D + (dc + 1) * 128]
                    tcol = k.op("tensor", lambda e, sl=sl, dc=dc: e.matmul(
                        bank[7][:, dc:dc + 1], sl, ones_1x1[:],
                        start=True, stop=True, skip_group_check=True))
                k.wait("vector", tcol)
                if add1:
                    tvc = k.op("vector", lambda e, dst=dst: e.tensor_scalar_add(
                        dst, bank[7][:, 0:8], 1.0))
                else:
                    tvc = k.op("vector", lambda e, dst=dst: e.tensor_copy(
                        dst, bank[7][:, 0:8]))
                k.wait("tensor", tvc)
            if p0sub < 6:
                break
            tga = clamp_row(mrow[:, 2 * D:3 * D], rowtmp[:])
            k.wait("sync", tga)
            k.dma("sync", modrow_dram[8 + b:9 + b, :], rowtmp[:])
            k.wait("sync", tga)
            t_bo = k.dma("sync", browtmp[:], bo_row)
            k.wait("vector", t_bo, tga)
            tgabo = k.eop("vector", lambda e: e.tensor_mul(rowtmp2[:], rowtmp[:], browtmp[:]))
            k.wait("sync", tgabo)
            k.dma("sync", modrow_dram[10 + b:11 + b, :], rowtmp2[:])
            tgb = clamp_row(mrow[:, 5 * D:6 * D], rowtmp[:])
            k.wait("sync", tgb)
            k.dma("sync", modrow_dram[12 + b:13 + b, :], rowtmp[:])

        if p0sub < 7:
            _stub_out(k, nc, out, es)
            k.flush()
            es.close()
            return nc
        k.barrier()
        k.flush()

    if debug:
        k.xwait("sync")
        for i_, t_ in enumerate((onepca[0], sa_s[0], onepcb[0], sbv_s[0],
                                 onepca[1], sa_s[1], onepcb[1], sbv_s[1])):
            k.dma("sync", dbg["d_mv"][:, i_ * 8:(i_ + 1) * 8], t_[:], sem="od")

    if phases < 1:
        _stub_out(k, nc, out, es)
        k.flush()
        es.close()
        return nc
    with ExitStack() as p1:
        sb1 = lambda name, shape, dt: p1.enter_context(nc.sbuf_tensor(name, shape, dt))
        wt_q = [sb1(f"wt_q{i}", [128, D], BF16) for i in range(8)]
        wt_k = [sb1(f"wt_k{i}", [128, D], BF16) for i in range(8)]
        wt_v = [sb1(f"wt_v{i}", [128, D], BF16) for i in range(8)]
        wt_u = [sb1(f"wt_u{i}", [128, D], BF16) for i in range(8)]
        wt_o = [sb1(f"wt_o{i}", [128, D], BF16) for i in range(8)]
        gw_s = sb1("gw_s", [128, 64], BF16)
        xhat = [sb1(f"xhat{i}", [128, SEG], BF16) for i in range(8)]
        xsq2 = [sb1(f"xsq{i}", [128, SEG], BF16) for i in range(2)]
        qt = [sb1(f"qt{i}", [128, SEG], BF16) for i in range(8)]
        kt = [sb1(f"kt{i}", [128, SEG], BF16) for i in range(8)]
        ut = [sb1(f"ut{i}", [128, SEG], BF16) for i in range(8)]
        ktm = [sb1(f"ktm{i}", [128, D], BF16) for i in range(4)]
        vtm = [sb1(f"vtm{i}", [128, D], BF16) for i in range(4)]
        ofm = [sb1(f"ofm{i}", [128, SEG], BF16) for i in range(8)]
        atm = sb1("atm", [128, 128], BF16)
        ssb = sb1("ssb", [128, 128], BF16)
        e_b1 = sb1("e_b1", [128, SEG], BF16)
        e_b2 = sb1("e_b2", [128, SEG], BF16)
        e_f3 = sb1("e_f3", [128, SEG], F32)
        srow = [sb1(f"srow{i}", [1, SEG], F32) for i in range(3)]
        xf32 = [sb1(f"xf32{i}", [128, D], F32) for i in range(2)]
        x2st = [sb1(f"x2st{i}", [128, D], F32) for i in range(2)]
        h2tmst = [sb1(f"h2tmst{i}", [128, D], BF16) for i in range(2)]
        lgs = sb1("lgs", [8, SEG], F32)
        bkrep = sb1("bkrep", [128, D], F32)
        bvrep = sb1("bvrep", [128, D], F32)
        garep_t = sb1("garep_t", [128, D], F32)
        gaborep_t = sb1("gaborep_t", [128, D], F32)

        t_ld = k.dma("sync", rowsrc[:], bk_row)
        t_r = replicate_row(rowsrc[:], bkrep[:], [t_ld], 0)
        k.wait("sync", t_r)
        t_ld = k.dma("sync", rowsrc[:], bv_row)
        t_r = replicate_row(rowsrc[:], bvrep[:], [t_ld], 0)
        k.wait("sync", t_r)

        k.wait("sync", t_wq, t_wk, t_wv, t_wu, t_gw)
        t_wload = None
        for i in range(8):
            t_wload = k.dma("sync", wt_q[i][:], wqT_b[i * 128:(i + 1) * 128, :])
            t_wload = k.dma("sync", wt_k[i][:], wkT_b[i * 128:(i + 1) * 128, :])
            t_wload = k.dma("sync", wt_v[i][:], wvT_b[i * 128:(i + 1) * 128, :])
            t_wload = k.dma("sync", wt_u[i][:], wuT_b[i * 128:(i + 1) * 128, :])
        t_gwl = k.dma("sync", gw_s[:], gwT_b)

        t_x2w = [None, None]
        t_xl_prev = [None, None]
        t_h2w = [None, None]
        t_h2bounce_all = []

        def ln_stats_and_norm(get_xin_tok, onep_ap_of, shift_ap_of, b):
            prev_pair = [None, None]
            tmm = None
            for dc in range(8):
                xb = xsq2[dc % 2]
                k.wait("scalar", get_xin_tok[dc], prev_pair[dc % 2])
                tsq = k.op("scalar", lambda e, dc=dc, xb=xb: e.activation(
                    xb[:], xhat[dc][:], AF.Square))
                k.wait("tensor", get_xin_tok[dc], tsq, t_small, t_const)
                k.raw("tensor", lambda e, dc=dc: e.matmul(
                    bank[0][0:1, 0:SEG], ones_s_b[:], xhat[dc][:],
                    start=(dc == 0), stop=(dc == 7)))
                tmm = k.op("tensor", lambda e, dc=dc, xb=xb: e.matmul(
                    bank[1][0:1, 0:SEG], ones_s_b[:], xb[:],
                    start=(dc == 0), stop=(dc == 7)))
                prev_pair[dc % 2] = tmm
            k.wait("vector", tmm)
            k.eop("vector", lambda e: e.tensor_scalar_mul(srow[0][:], bank[0][0:1, 0:SEG], 1.0 / D))
            k.eop("vector", lambda e: e.tensor_scalar_mul(srow[1][:], bank[1][0:1, 0:SEG], 1.0 / D))
            k.eop("vector", lambda e: e.tensor_mul(srow[2][:], srow[0][:], srow[0][:]))
            k.eop("vector", lambda e: e.tensor_sub(
                srow[1][:], srow[1][:], srow[2][:]))
            tvv = k.eop("vector", lambda e: e.tensor_scalar_add(srow[1][:], srow[1][:], 1e-5))
            k.wait("scalar", tvv)
            k.eop("scalar", lambda e: e.activation(srow[1][:], srow[1][:], AF.Ln))
            tse = k.eop("scalar", lambda e: e.activation(srow[1][:], srow[1][:], AF.Exp,
                                                         scale=-0.5))
            k.wait("vector", tse)
            tvb = k.eop("vector", lambda e: e.scalar_tensor_tensor(
                srow[2][:], srow[0][:], -1.0, srow[1][:], op0=OP.mult, op1=OP.mult))
            k.wait("tensor", tse, tvb, tmm)
            tra = k.op("tensor", lambda e: e.matmul(bank[2][:, 0:SEG], ones_r_f[:],
                                                    srow[1][:], start=True, stop=True))
            trb = k.op("tensor", lambda e: e.matmul(bank[3][:, 0:SEG], ones_r_f[:],
                                                    srow[2][:], start=True, stop=True))
            txh = None
            k.wait("vector", tra, trb)
            for dc in range(8):
                onep_ap = onep_ap_of(dc)
                shift_ap = shift_ap_of(dc)
                k.op("vector", lambda e, dc=dc: e.tensor_mul(
                    e_f3[:], xhat[dc][:], bank[2][:, 0:SEG]))
                k.op("vector", lambda e, dc=dc: e.tensor_add(
                    e_f3[:], e_f3[:], bank[3][:, 0:SEG]))
                txh = k.op("vector", lambda e, dc=dc, o=onep_ap, s=shift_ap: e.tensor_scalar(
                    xhat[dc][:], e_f3[:], o, s, op0=OP.mult, op1=OP.add))
            k.wait("tensor", txh)
            return txh

        for b in range(2):
            k.xwait("sync")
            t_ld = k.dma("sync", rowsrc[:], modrow_dram[8 + b:9 + b, :])
            t_r = replicate_row(rowsrc[:], garep_t[:], [t_ld], 0)
            k.wait("sync", t_r)
            t_ld = k.dma("sync", rowsrc[:], modrow_dram[10 + b:11 + b, :])
            t_r = replicate_row(rowsrc[:], gaborep_t[:], [t_ld], 0)
            t_wol = []
            k.wait("sync", t_wo)
            for i in range(8):
                t_wol.append(k.dma("sync", wt_o[i][:], woT_b[i * 128:(i + 1) * 128, :]))
            two = None
            garep_b = garep_t[:]
            for i in range(8):
                k.wait("vector", t_wol[i], t_r)
                two = k.op("vector", lambda e, i=i, g=garep_b: e.tensor_mul(
                    wt_o[i][:], wt_o[i][:], g))
            k.wait("tensor", two, k.tok("s"), k.tok("v"))
            k.raw("tensor", lambda e: e.matmul(bank[6][:], zeros_b[:, 0:128], zeros_b[:],
                                               start=True, stop=True))
            tz = k.op("tensor", lambda e: e.matmul(bank[7][:], zeros_b[:, 0:128], zeros_b[:],
                                                   start=True, stop=True))

            if debug and b == 0:
                k.xwait("sync")
                for i in range(8):
                    k.dma("sync", dbg["d_wto"][i * 128:(i + 1) * 128, :], wt_o[i][:],
                          sem="od")
                k.dma("sync", dbg["d_gabo"], gaborep_t[:], sem="od")
            prev_oevac = None
            prev_s_upd = [tz] * H

            for seg in range(NSEG_B):
                t0 = b * N + seg * SEG

                tx = []
                k.xwait("sync")
                k.wait("sync", t_xf)
                for dc in range(8):
                    tx.append(k.dma("sync", xhat[dc][:],
                                    xfm_b[dc * 128:(dc + 1) * 128, t0:t0 + SEG]))
                k.xwait("tensor", "vector", "scalar")
                t_xhat = ln_stats_and_norm(
                    tx, lambda dc: onepca[b][:, dc:dc + 1], lambda dc: sa_s[b][:, dc:dc + 1], b)

                if debug:
                    k.xwait("sync")
                    for dc in range(8):
                        k.dma("sync", dbg["d_xh"][dc * 128:(dc + 1) * 128, t0:t0 + SEG],
                              xhat[dc][:], sem="od")
                    k.dma("sync", dbg["d_ab"][0:1, t0:t0 + SEG], srow[1][:], sem="od")
                    k.dma("sync", dbg["d_ab"][1:2, t0:t0 + SEG], srow[2][:], sem="od")
                def fm_proj(wt, psum_i, hc):
                    k.wait("tensor", t_xhat, t_wload)
                    tret = None
                    for dc in range(8):
                        lhs = wt[dc][:, hc * 128:(hc + 1) * 128]
                        fn = lambda e, lhs=lhs, dc=dc, pi=psum_i: e.matmul(
                            bank[pi][:, 0:SEG], lhs, xhat[dc][:],
                            start=(dc == 0), stop=(dc == 7))
                        if dc < 7:
                            k.raw("tensor", fn)
                        else:
                            tret = k.op("tensor", fn)
                    return tret

                def elu_evac(psum_i, bias_ap, dst_ap):
                    tr_ = k.op("vector", lambda e, pi=psum_i, ba=bias_ap: e.tensor_scalar(
                        e_b1[:], bank[pi][:, 0:SEG], ba, 0.0, op0=OP.add, op1=OP.max))
                    tm_ = k.op("vector", lambda e, pi=psum_i, ba=bias_ap: e.tensor_scalar(
                        e_b2[:], bank[pi][:, 0:SEG], ba, 0.0, op0=OP.add, op1=OP.min))
                    k.wait("scalar", tm_)
                    te_ = k.op("scalar", lambda e: e.activation(e_b2[:], e_b2[:], AF.Exp))
                    k.wait("vector", te_)
                    k.op("vector", lambda e, d=dst_ap: e.tensor_add(d, e_b1[:], e_b2[:]))
                    return k.op("vector", lambda e, d=dst_ap: e.tensor_scalar_add(d, d, -1.0))

                k.xwait("vector", "scalar")
                for dst_list, wt, bias_sb, act in ((qt, wt_q, bq_s, "elu"),
                                                   (kt, wt_k, bk_s, "elu"),
                                                   (ut, wt_u, bu_s, "silu")):
                    for hc in range(8):
                        pi = 2 + (hc % 2)
                        tmm = fm_proj(wt, pi, hc)
                        bias_ap = bias_sb[:, hc:hc + 1]
                        if act == "elu":
                            k.wait("vector", tmm)
                            tev = elu_evac(pi, bias_ap, dst_list[hc][:])
                            k.wait("tensor", tev)
                        else:
                            k.wait("scalar", tmm)
                            tev = k.op("scalar", lambda e, hc=hc, pi=pi, ba=bias_ap,
                                       d=dst_list[hc]: e.activation(
                                d[:], bank[pi][:, 0:SEG], AF.Silu, bias=ba))
                            k.wait("tensor", tev)

                for tc in range(4):
                    for hg in range(2):
                        pi = 4 + hg
                        hsl = slice(hg * 512, (hg + 1) * 512)
                        k.wait("tensor", t_xhat)
                        tkmm = None
                        for dc in range(8):
                            lhs = xhat[dc][:, tc * 128:(tc + 1) * 128]
                            rhs = wt_k[dc][:, hsl]
                            fn = lambda e, lhs=lhs, rhs=rhs, dc=dc, pi=pi: e.matmul(
                                bank[pi][:, 0:512], lhs, rhs, start=(dc == 0), stop=(dc == 7))
                            if dc < 7:
                                k.raw("tensor", fn)
                            else:
                                tkmm = k.op("tensor", fn)
                        k.wait("vector", tkmm)
                        tv1 = k.op("vector", lambda e, pi=pi, hsl=hsl: e.tensor_add(
                            e_f3[:], bank[pi][:, 0:512], bkrep[:, hsl]))
                        tv2 = k.op("vector", lambda e: e.tensor_scalar_min(
                            e_b2[:], e_f3[:], 0.0))
                        k.wait("scalar", tv2)
                        te = k.op("scalar", lambda e: e.activation(e_b2[:], e_b2[:], AF.Exp))
                        tv3 = k.op("vector", lambda e: e.tensor_scalar_max(
                            e_b1[:], e_f3[:], 0.0))
                        k.wait("vector", te)
                        kdst = ktm[tc][:, hsl]
                        k.op("vector", lambda e, d=kdst: e.tensor_add(d, e_b1[:], e_b2[:]))
                        tv5 = k.op("vector", lambda e, d=kdst: e.tensor_scalar_add(d, d, -1.0))
                        k.wait("tensor", tv1, tv3)
                        tvmm = None
                        for dc in range(8):
                            lhs = xhat[dc][:, tc * 128:(tc + 1) * 128]
                            rhs = wt_v[dc][:, hsl]
                            fn = lambda e, lhs=lhs, rhs=rhs, dc=dc, pi=pi: e.matmul(
                                bank[pi][:, 0:512], lhs, rhs, start=(dc == 0), stop=(dc == 7))
                            if dc < 7:
                                k.raw("tensor", fn)
                            else:
                                tvmm = k.op("tensor", fn)
                        k.wait("vector", tvmm)
                        tv6 = k.op("vector", lambda e, pi=pi, hsl=hsl: e.tensor_add(
                            e_f3[:], bank[pi][:, 0:512], bvrep[:, hsl]))
                        k.wait("scalar", tv6)
                        vdst = vtm[tc][:, hsl]
                        ts7 = k.op("scalar", lambda e, d=vdst: e.activation(
                            d, e_f3[:], AF.Silu))
                        k.wait("tensor", tv6, ts7)

                k.xwait("tensor", "vector", "scalar")
                for h in range(H):
                    sview = bank[6 + h // 4][:, (h % 4) * 128:(h % 4) * 128 + 128]
                    for cc in range(4):
                        gc = seg * 4 + cc
                        csl = slice(cc * 128, cc * 128 + 128)
                        k.wait("tensor", prev_oevac)
                        tat = k.op("tensor", lambda e, h=h, csl=csl: e.matmul(
                            bank[0][:, 0:128], kt[h][:, csl], qt[h][:, csl],
                            start=True, stop=True))
                        k.wait("vector", tat)
                        tatm = k.op("vector", lambda e: e.tensor_mul(
                            atm[:], bank[0][:, 0:128], triu_f[:]))
                        tsc = None
                        if gc > 0:
                            k.wait("vector", prev_s_upd[h])
                            tsc = k.op("vector", lambda e, sview=sview: e.tensor_copy(
                                ssb[:], sview))
                        k.wait("tensor", tatm)
                        fn_intra = lambda e, cc=cc, h=h, gc=gc: e.matmul(
                            bank[1][:, 0:128], vtm[cc][:, h * 128:(h + 1) * 128], atm[:],
                            start=True, stop=(gc == 0))
                        if gc > 0:
                            k.raw("tensor", fn_intra)
                            k.wait("tensor", tsc)
                            t_o = k.op("tensor", lambda e, h=h, csl=csl: e.matmul(
                                bank[1][:, 0:128], ssb[:], qt[h][:, csl],
                                start=False, stop=True))
                        else:
                            t_o = k.op("tensor", fn_intra)
                        if tsc is not None:
                            k.wait("tensor", tsc)
                        t_s = k.op("tensor", lambda e, cc=cc, h=h, sview=sview: e.matmul(
                            sview, ktm[cc][:, h * 128:(h + 1) * 128],
                            vtm[cc][:, h * 128:(h + 1) * 128], start=False, stop=False,
                            skip_group_check=True))
                        prev_s_upd[h] = t_s
                        k.wait("scalar", t_o)
                        prev_oevac = k.op("scalar", lambda e, h=h, csl=csl: e.copy(
                            ofm[h][:, csl], bank[1][:, 0:128]))

                if debug:
                    k.xwait("sync")
                    for hc in range(8):
                        k.dma("sync", dbg["d_oraw"][hc * 128:(hc + 1) * 128, t0:t0 + SEG],
                              ofm[hc][:], sem="od")
                        k.dma("sync", dbg["d_ut"][hc * 128:(hc + 1) * 128, t0:t0 + SEG],
                              ut[hc][:], sem="od")
                        k.dma("sync", dbg["d_qt"][hc * 128:(hc + 1) * 128, t0:t0 + SEG],
                              qt[hc][:], sem="od")
                        k.dma("sync", dbg["d_kt"][hc * 128:(hc + 1) * 128, t0:t0 + SEG],
                              kt[hc][:], sem="od")
                    for tc in range(4):
                        k.dma("sync", dbg["d_vtm"][t0 + tc * 128:t0 + (tc + 1) * 128, :],
                              vtm[tc][:], sem="od")
                k.xwait("tensor", "vector", "scalar")
                k.wait("scalar", prev_oevac)
                prev_pair_r = [None, None]
                tmr = None
                for hc in range(8):
                    xb = xsq2[hc % 2]
                    k.wait("scalar", prev_pair_r[hc % 2])
                    tsq = k.op("scalar", lambda e, hc=hc, xb=xb: e.activation(
                        xb[:], ofm[hc][:], AF.Square))
                    k.wait("tensor", tsq, prev_oevac)
                    tmr = k.op("tensor", lambda e, hc=hc, xb=xb: e.matmul(
                        bank[4][0:1, 0:SEG], ones_s_b[:], xb[:],
                        start=(hc == 0), stop=(hc == 7)))
                    prev_pair_r[hc % 2] = tmr
                k.wait("vector", tmr)
                tvr = k.eop("vector", lambda e: e.tensor_scalar_mul(
                    srow[0][:], bank[4][0:1, 0:SEG], 1.0 / D))
                k.wait("scalar", tvr)
                k.eop("scalar", lambda e: e.activation(srow[0][:], srow[0][:], AF.Ln))
                tse = k.eop("scalar", lambda e: e.activation(srow[0][:], srow[0][:], AF.Exp,
                                                             scale=0.5))
                k.wait("vector", tse)
                tva = k.eop("vector", lambda e: e.tensor_scalar_add(srow[0][:], srow[0][:], 1e-8))
                k.wait("scalar", tva)
                k.eop("scalar", lambda e: e.activation(srow[0][:], srow[0][:], AF.Ln))
                tse2 = k.eop("scalar", lambda e: e.activation(srow[0][:], srow[0][:], AF.Exp,
                                                              scale=-1.0))
                if debug:
                    k.wait("sync", tse2)
                    k.dma("sync", dbg["d_frow"][:, t0:t0 + SEG], srow[0][:], sem="od")
                k.wait("tensor", tse2, tmr)
                trf = k.op("tensor", lambda e: e.matmul(bank[5][:, 0:SEG], ones_r_f[:],
                                                        srow[0][:], start=True, stop=True))
                tog = None
                k.wait("vector", trf)
                for hc in range(8):
                    k.op("vector", lambda e, hc=hc: e.tensor_mul(
                        e_f3[:], ofm[hc][:], bank[5][:, 0:SEG]))
                    tog = k.op("vector", lambda e, hc=hc: e.scalar_tensor_tensor(
                        ofm[hc][:], e_f3[:], asc_s[:, hc:hc + 1], ut[hc][:],
                        op0=OP.mult, op1=OP.mult))
                if debug:
                    k.xwait("sync")
                    for hc in range(8):
                        k.dma("sync", dbg["d_og"][hc * 128:(hc + 1) * 128, t0:t0 + SEG],
                              ofm[hc][:], sem="od")
                gaborep_b = gaborep_t[:]
                prev_eread = [None, None]
                for tc in range(4):
                    tcur = t0 + tc * 128
                    xbuf = xf32[tc % 2]
                    x2buf = x2st[tc % 2]
                    k.wait("sync", t_xl_prev[tc % 2])
                    txl = k.dma("sync", xbuf[:], x_tm[tcur:tcur + 128, :])
                    tx2 = None
                    for eg in range(2):
                        pi = 2 + eg
                        esl = slice(eg * 512, (eg + 1) * 512)
                        k.wait("tensor", tog, t_x2w[tc % 2], prev_eread[eg])
                        tpo = None
                        for hc in range(8):
                            lhs = ofm[hc][:, tc * 128:(tc + 1) * 128]
                            rhs = wt_o[hc][:, esl]
                            fn = lambda e, lhs=lhs, rhs=rhs, hc=hc, pi=pi: e.matmul(
                                bank[pi][:, 0:512], lhs, rhs, start=(hc == 0), stop=(hc == 7))
                            if hc < 7:
                                k.raw("tensor", fn)
                            else:
                                tpo = k.op("tensor", fn)
                        k.wait("vector", tpo, txl)
                        k.op("vector", lambda e, pi=pi, esl=esl, g=gaborep_b: e.tensor_add(
                            e_f3[:], bank[pi][:, 0:512], g[:, esl]))
                        tx2 = k.op("vector", lambda e, x2buf=x2buf, esl=esl, xbuf=xbuf:
                                   e.tensor_add(x2buf[:, esl], e_f3[:], xbuf[:, esl]))
                        prev_eread[eg] = tx2
                    k.wait("sync", tx2)
                    t1_ = k.dma("sync", x2_dram[tcur:tcur + 128, :], x2buf[:])
                    k.wait("gpsimd", tx2)
                    t2_ = k.dma("gpsimd", x2_btm[tcur:tcur + 128, :], x2buf[:])
                    t_x2w[tc % 2] = t2_
                    t_xl_prev[tc % 2] = tx2
                    k.wait("vector", t1_)
                t_x2_seg = (t1_, t2_)

                tx2f = []
                k.xwait("sync")
                k.wait("sync", *t_x2w)
                for dc in range(8):
                    tx2f.append(k.dma("sync", xhat[dc][:],
                                      x2_btm[t0:t0 + SEG, dc * 128:(dc + 1) * 128],
                                      transpose=True))
                k.xwait("tensor", "vector", "scalar")
                t_h2 = ln_stats_and_norm(
                    tx2f, lambda dc: onepcb[b][:, dc:dc + 1], lambda dc: sbv_s[b][:, dc:dc + 1], b)
                th2w = []
                for dc in range(8):
                    k.wait("sync", t_h2)
                    th2w.append(k.dma("sync", h2fm[dc * 128:(dc + 1) * 128, t0:t0 + SEG],
                                      xhat[dc][:]))
                for tc in range(4):
                    tcur = t0 + tc * 128
                    hbuf = h2tmst[tc % 2]
                    k.wait("sync", th2w[-1], t_h2w[tc % 2])
                    tr = k.dma("sync", hbuf[:], h2fm[:, tcur:tcur + 128], transpose=True)
                    k.wait("sync", tr)
                    tb = k.dma("sync", h2_bounce[tcur:tcur + 128, :], hbuf[:])
                    t_h2w[tc % 2] = tb
                    t_h2bounce_all.append(tb)
                k.wait("tensor", t_h2, t_gwl)
                tlm = None
                for dc in range(8):
                    lhs = gw_s[:, dc * 8:dc * 8 + 8]
                    fn = lambda e, lhs=lhs, dc=dc: e.matmul(
                        bank[2][0:8, 0:SEG], lhs, xhat[dc][:],
                        start=(dc == 0), stop=(dc == 7))
                    if dc < 7:
                        k.raw("tensor", fn)
                    else:
                        tlm = k.op("tensor", fn)
                k.wait("scalar", tlm)
                tle = k.op("scalar", lambda e: e.activation(
                    lgs[:], bank[2][0:8, 0:SEG], AF.Identity, bias=gbc_s[:]))
                k.wait("tensor", tle)
                for tc in range(4):
                    ch = (t0 + tc * 128) // 128
                    src = lgs[:, tc * 128:(tc + 1) * 128]
                    ttr = k.op("tensor", lambda e, src=src: e.transpose(
                        bank[3][0:128, 0:8], src, ident_f[0:8, 0:8]))
                    k.wait("vector", ttr)
                    dstl = logits_tm[:, ch, :]
                    tlv = k.op("vector", lambda e, d=dstl: e.tensor_copy(
                        d, bank[3][0:128, 0:8]))
                    k.wait("tensor", tlv)

        k.barrier()
        k.flush()

    if phases < 2:
        _stub_out(k, nc, out, es)
        k.flush()
        es.close()
        return nc
    t_ag_h2 = k.cc("AllGather", OP.bypass, [h2_bounce.opt()], [h2_all.opt()])

    with ExitStack() as p2:
        sb2 = lambda name, shape, dt: p2.enter_context(nc.sbuf_tensor(name, shape, dt))
        mx = sb2("mx", [128, NCH], F32)
        ex = sb2("ex", [128, NCH * 8], F32)
        iot = sb2("iot", [128, NCH * 8], F32)
        cand = sb2("cand", [128, NCH * 8], F32)
        idxf = sb2("idxf", [128, NCH], F32)
        oh_f = sb2("oh_f", [128, NCH * 8], F32)
        oh_b = sb2("oh_b", [128, NCH * 8], BF16)
        cum_sb = sb2("cum_sb", [128, NCH * 8], F32)
        tpr = sb2("tpr", [128, NCH], F32)
        ttot = sb2("ttot", [1, NCH * 8], F32)
        ttot32 = sb2("ttot32", [32, 8], F32)
        exrow = sb2("exrow", [32, 8], F32)
        offrow = sb2("offrow", [1, 8], F32)
        cmat = sb2("cmat", [8, 8], F32)
        cmatbig = sb2("cmatbig", [128, 8], F32)
        comb = sb2("comb", [32, 8], F32)
        combrow = sb2("combrow", [1, NCH * 8], F32)
        possel = sb2("possel", [128, NCH], F32)
        keep_s = sb2("keep_s", [128, NCH], F32)
        zf32 = sb2("zf32", [128, 428], F32)
        iota_p = sb2("iota_p", [128, 1], F32)
        sc_r = sb2("sc_r", [128, NCH], F32)
        sc_q = sb2("sc_q", [128, NCH], F32)
        sc_j = sb2("sc_j", [128, NCH], F32)
        sc_s = sb2("sc_s", [128, NCH], F32)
        sc_d = sb2("sc_d", [128, 1], F32)
        sc_u = sb2("sc_u", [128, NCH], U32)
        sc_t = sb2("sc_t", [128, NCH], U32)
        sc_d2 = sb2("sc_d2", [128, NCH], F32)
        payall4 = sb2("payall4", [128, NCH, 4], F32)
        tokb_s = sb2("tokb_s", [128, 1], F32)

        lg3 = logits_tm[:]
        i3 = iot[:].rearrange("p (a b) -> p a b", b=8)
        e3 = ex[:].rearrange("p (a b) -> p a b", b=8)
        o3 = oh_f[:].rearrange("p (a b) -> p a b", b=8)
        c3 = cand[:].rearrange("p (a b) -> p a b", b=8)

        tio = k.op("gpsimd", lambda e: e.iota(
            i3, [[0, NCH], [1, 8]], base=0, channel_multiplier=0,
            allow_small_or_imprecise_dtypes=True))
        tmx = k.eop("vector", lambda e: e.tensor_reduce(
            mx[:].unsqueeze(2), lg3, axis=AX.X, op=OP.max))
        mxb = mx[:].unsqueeze(2).broadcast_to([128, NCH, 8])
        k.wait("vector", tmx)
        tex = k.eop("vector", lambda e: e.tensor_tensor(e3, lg3, mxb, op=OP.subtract))
        k.wait("scalar", tex)
        tee = k.eop("scalar", lambda e: e.activation(ex[:], ex[:], AF.Exp))
        k.wait("vector", tee)
        tsum = k.eop("vector", lambda e: e.tensor_reduce(
            tpr[:].unsqueeze(2), e3, axis=AX.X, op=OP.add))
        k.wait("scalar", tsum)
        k.eop("scalar", lambda e: e.activation(tpr[:], tpr[:], AF.Ln))
        ttp = k.eop("scalar", lambda e: e.activation(tpr[:], tpr[:], AF.Exp, scale=-1.0))
        k.wait("vector", tio)
        teq = k.eop("vector", lambda e: e.tensor_tensor(o3, lg3, mxb, op=OP.is_equal))
        k.wait("vector", teq)
        k.eop("vector", lambda e: e.tensor_scalar_add(cand[:], iot[:], -1000.0))
        k.eop("vector", lambda e: e.tensor_mul(cand[:], cand[:], oh_f[:]))
        k.eop("vector", lambda e: e.tensor_scalar_add(cand[:], cand[:], 1000.0))
        tix = k.eop("vector", lambda e: e.tensor_reduce(
            idxf[:].unsqueeze(2), c3, axis=AX.X, op=OP.min))
        ixb = idxf[:].unsqueeze(2).broadcast_to([128, NCH, 8])
        k.wait("vector", tix)
        toh = k.eop("vector", lambda e: e.tensor_tensor(o3, i3, ixb, op=OP.is_equal))
        k.wait("vector", toh)
        tohb = k.eop("vector", lambda e: e.tensor_copy(oh_b[:], oh_f[:]))
        k.wait("tensor", tohb, t_const)
        tcm = None
        for ch in range(NCH):
            sl = slice(ch * 8, ch * 8 + 8)
            fn = lambda e, sl=sl, ch=ch: e.matmul(
                bank[0][:, sl], triu_b[:], oh_b[:, sl],
                start=(ch == 0), stop=True, skip_group_check=True)
            if ch < NCH - 1:
                k.raw("tensor", fn)
            else:
                tcm = k.op("tensor", fn)
        k.wait("vector", tcm)
        tcums = k.eop("vector", lambda e: e.tensor_copy(cum_sb[:], bank[0][:, 0:NCH * 8]))
        k.wait("sync", tcums)
        ttt = k.dma("sync", modrow_dram[0:1, 0:NCH * 8], cum_sb[127:128, :])
        k.wait("sync", ttt)
        trr = k.dma("sync", ttot32[:],
                    modrow_dram[0:1, 0:NCH * 8].rearrange("o (a b) -> (o a) b", b=8))
        k.wait("tensor", trr)
        tmm1 = k.op("tensor", lambda e: e.matmul(
            bank[1][0:32, 0:8], su32_f[:], ttot32[:], start=True, stop=True))
        tmm2 = k.op("tensor", lambda e: e.matmul(
            bank[2][0:1, 0:8], ones32_f[:], ttot32[:], start=True, stop=True))
        k.wait("vector", tmm1, tmm2)
        texr = k.eop("vector", lambda e: e.tensor_copy(exrow[:], bank[1][0:32, 0:8]))
        tctr = k.eop("vector", lambda e: e.tensor_copy(offrow[:], bank[2][0:1, 0:8]))
        k.wait("tensor", tctr)
        tcrepb = k.op("tensor", lambda e: e.matmul(
            bank[3][:, 0:8], ones_r_f[:], offrow[:], start=True, stop=True))
        k.wait("vector", tcrepb)
        tcmatb = k.eop("vector", lambda e: e.tensor_copy(cmatbig[:], bank[3][:, 0:8]))
        k.wait("sync", tcmatb)
        tcin = k.dma("sync", cnt_in, cmatbig[:])
        if debug:
            k.dma("sync", dbg["d_tt32"], ttot32[:], sem="od")
            k.dma("sync", dbg["d_ttot"], modrow_dram[0:1, 0:NCH * 8], sem="od")
        k.wait("gpsimd", tcin)
        t_ar = k.cc("AllGather", OP.bypass, [cnt_in.opt()], [cnt_out.opt()])
        k.wait("sync", t_ar)
        tcl = k.dma("sync", cmat[:], cnt_out.rearrange("(c r) e -> c r e", r=128)[:, 0, :])
        k.wait("tensor", tcl)
        toff = k.op("tensor", lambda e: e.matmul(
            bank[4][0:1, 0:8], cmask_s[:], cmat[:], start=True, stop=True))
        k.wait("vector", toff)
        toffr = k.eop("vector", lambda e: e.tensor_copy(offrow[:], bank[4][0:1, 0:8]))
        k.wait("tensor", toffr)
        torep = k.op("tensor", lambda e: e.matmul(
            bank[5][0:32, 0:8], ones_r_f[:, 0:32], offrow[:], start=True, stop=True))
        k.wait("vector", torep, texr)
        tcomb = k.eop("vector", lambda e: e.tensor_add(comb[:], exrow[:], bank[5][0:32, 0:8]))
        k.wait("sync", tcomb)
        tcw = k.dma("sync", modrow_dram[1:2, 0:NCH * 8].rearrange("o (a b) -> (o a) b", b=8),
                    comb[:])
        k.wait("sync", tcw)
        tcrow = k.dma("sync", combrow[:], modrow_dram[1:2, 0:NCH * 8])
        k.wait("tensor", tcrow)
        tcrep2 = k.op("tensor", lambda e: e.matmul(
            bank[5][:, 0:NCH * 8], ones_r_f[:], combrow[:], start=True, stop=True))
        k.wait("vector", tcrep2, tcums, toh)
        k.eop("vector", lambda e: e.tensor_add(cand[:], cum_sb[:], bank[5][:, 0:NCH * 8]))
        k.eop("vector", lambda e: e.tensor_mul(cand[:], cand[:], oh_f[:]))
        tps = k.eop("vector", lambda e: e.tensor_reduce(
            possel[:].unsqueeze(2), c3, axis=AX.X, op=OP.add))
        k.wait("vector", tps, ttp)
        tkp = k.eop("vector", lambda e: e.tensor_scalar(
            keep_s[:], possel[:], float(CAP), None, op0=OP.is_le))
        k.wait("vector", tkp)
        tgt = k.eop("vector", lambda e: e.tensor_mul(gate_s[:], tpr[:], keep_s[:]))
        k.eop("vector", lambda e: e.tensor_mul(cand[:, 0:NCH], possel[:], keep_s[:]))
        k.eop("vector", lambda e: e.tensor_scalar(
            cand[:, 0:NCH], cand[:, 0:NCH], -1.0, 0.0, op0=OP.add, op1=OP.max))
        k.eop("vector", lambda e: e.scalar_tensor_tensor(
            cand[:, NCH:2 * NCH], idxf[:], float(CAP), cand[:, 0:NCH],
            op0=OP.mult, op1=OP.add))
        k.eop("vector", lambda e: e.tensor_scalar_max(
            cand[:, NCH:2 * NCH], cand[:, NCH:2 * NCH], 0.0))
        tg2 = k.eop("vector", lambda e: e.tensor_scalar_min(
            cand[:, NCH:2 * NCH], cand[:, NCH:2 * NCH], float(NCORES * CAP - 1)))
        k.wait("gpsimd", tg2)
        tidxg = k.dma("gpsimd", idxg_s[:], cand[:, NCH:2 * NCH])
        tz0 = k.op("gpsimd", lambda e: e.memset(zf32[:], 0.0))
        k.wait("sync", tz0)
        tzw = None
        zview = slotmap_in.rearrange("(p c) w -> p (c w)", p=128)
        for off in range(0, 1284, 428):
            w = min(428, 1284 - off)
            tzw = k.dma("sync", zview[:, off:off + w], zf32[:, 0:w])
        tip = k.op("gpsimd", lambda e: e.iota(
            iota_p[:], [[0, 1]], base=0, channel_multiplier=1,
            allow_small_or_imprecise_dtypes=True))
        k.wait("vector", tip, tkp)
        tr1 = k.eop("vector", lambda e: e.tensor_scalar_add(sc_r[:], possel[:], -1.0))
        k.eop("vector", lambda e: e.tensor_scalar_mul(sc_j[:], sc_r[:], 1.0 / 128.0))
        k.eop("vector", lambda e: e.tensor_scalar_add(sc_j[:], sc_j[:], 8388608.0))
        k.eop("vector", lambda e: e.tensor_scalar_add(sc_j[:], sc_j[:], -8388608.0))
        k.eop("vector", lambda e: e.scalar_tensor_tensor(
            sc_q[:], sc_j[:], -128.0, sc_r[:], op0=OP.mult, op1=OP.add))
        k.eop("vector", lambda e: e.tensor_scalar(
            sc_d2[:], sc_q[:], 0.0, None, op0=OP.is_lt))
        k.eop("vector", lambda e: e.tensor_sub(sc_j[:], sc_j[:], sc_d2[:]))
        k.eop("vector", lambda e: e.scalar_tensor_tensor(
            sc_q[:], sc_d2[:], 128.0, sc_q[:], op0=OP.mult, op1=OP.add))
        k.eop("vector", lambda e: e.scalar_tensor_tensor(
            sc_s[:], sc_q[:], 40.0, sc_j[:], op0=OP.mult, op1=OP.add))
        k.eop("vector", lambda e: e.scalar_tensor_tensor(
            sc_s[:], idxf[:], float(CAP), sc_s[:], op0=OP.mult, op1=OP.add))
        k.eop("vector", lambda e: e.tensor_scalar_add(sc_d[:], iota_p[:], float(NCORES * CAP)))
        k.eop("vector", lambda e: e.tensor_sub(sc_s[:], sc_s[:], sc_d[:].broadcast_to([128, NCH])))
        k.eop("vector", lambda e: e.tensor_mul(sc_s[:], sc_s[:], keep_s[:]))
        k.eop("vector", lambda e: e.tensor_add(
            sc_s[:], sc_s[:], sc_d[:].broadcast_to([128, NCH])))
        k.eop("vector", lambda e: e.tensor_scalar_max(sc_s[:], sc_s[:], 0.0))
        tslot = k.eop("vector", lambda e: e.tensor_scalar_min(
            sc_s[:], sc_s[:], float(NCORES * CAP + 127)))
        if debug:
            k.wait("sync", tslot)
            k.dma("sync", dbg["d_scdbg"][:, 0:NCH], sc_r[:], sem="od")
            k.dma("sync", dbg["d_scdbg"][:, NCH:2 * NCH], sc_j[:], sem="od")
            k.dma("sync", dbg["d_scdbg"][:, 2 * NCH:3 * NCH], sc_q[:], sem="od")
            k.dma("sync", dbg["d_scdbg"][:, 3 * NCH:4 * NCH], sc_s[:], sem="od")
        k.wait("gpsimd", tslot)
        tsc_u = k.dma("gpsimd", sc_u[:], sc_s[:])
        tpay0 = k.op("gpsimd", lambda e: e.iota(
            payall4[:], [[128, NCH], [0, 4]], base=1, channel_multiplier=1,
            allow_small_or_imprecise_dtypes=True))
        tb_ld = k.dma("sync", tokb_s[:], tokbase)
        k.wait("vector", tpay0, tb_ld)
        tpay = k.eop("vector", lambda e: e.tensor_scalar_add(
            payall4[:].rearrange("p a b -> p (a b)"),
            payall4[:].rearrange("p a b -> p (a b)"), tokb_s[:]))
        k.wait("gpsimd", tsc_u, tzw, tpay)
        tscat = None
        for ch in range(NCH):
            tscat = k.dma16(lambda e, ch=ch: e.indirect_dma_start(
                out=slotmap_in, out_offset=IndirectOffsetOnAxis(sc_u[:, ch:ch + 1], 0),
                in_=payall4[:, ch, :], in_offset=None))
        k.wait("gpsimd", tscat)
        t_ar_slot = k.cc("AllReduce", OP.add, [slotmap_in.opt()], [slotmap_out.opt()])

        if debug:
            k.wait("sync", tgt, tidxg, tmx, tix, tps, tkp, ttp, tcums)
            k.dma("sync", dbg["d_mx"], mx[:], sem="od")
            k.dma("sync", dbg["d_idxf"], idxf[:], sem="od")
            k.dma("sync", dbg["d_possel"], possel[:], sem="od")
            k.dma("sync", dbg["d_keep"], keep_s[:], sem="od")
            k.dma("sync", dbg["d_tpr"], tpr[:], sem="od")
            k.dma("sync", dbg["d_cum"], cum_sb[:], sem="od")
            k.dma("sync", dbg["d_gate"], gate_s[:], sem="od")
            k.wait("sync", tscat)
            k.dma("sync", dbg["d_scu"], sc_u[:], sem="od")
            k.dma("sync", dbg["d_idxg"], cand[:, NCH:2 * NCH], sem="od")
            k.dma("sync", dbg["d_cnt"], cnt_out.rearrange("(c r) e -> c r e", r=128)[:, 0, :], sem="od")
            k.dma("sync", dbg["d_logits"], logits_tm[:].rearrange("p a b -> p (a b)"),
                  sem="od")
            k.dma("sync", dbg["d_x2"], x2_dram, sem="od")
            k.dma("sync", dbg["d_h2"], h2_bounce, sem="od")

        k.barrier()
        k.flush()

    if phases < 3:
        _stub_out(k, nc, out, es)
        k.flush()
        es.close()
        return nc
    with ExitStack() as p3:
        sb3 = lambda name, shape, dt: p3.enter_context(nc.sbuf_tensor(name, shape, dt))
        slot4 = sb3("slot4", [128, CAP // 128 * 4], F32)
        srowf = sb3("srowf", [128, 1], F32)
        srowu = sb3("srowu", [128, 1], U32)
        slot4u = sb3("slot4u", [128, CAP // 128 * 4], U32)
        btm = [sb3(f"btm{i}", [128, 1024], BF16) for i in range(4)]
        w1t = [sb3(f"w1t{i}", [128, FF], BF16) for i in range(8)]
        w2t = [sb3(f"w2t{i}", [128, D], BF16) for i in range(32)]
        bufg = [sb3(f"bufg{i}", [128, 8, FFG], BF16) for i in range(2)]
        y1c = [sb3(f"y1c{i}", [128, FFG], BF16) for i in range(2)]
        eost = [sb3(f"eost{i}", [128, D], F32) for i in range(2)]
        b2rep = sb3("b2rep", [128, D], F32)

        t_ld = k.dma("sync", rowsrc[:], b2_row)
        replicate_row(rowsrc[:], b2rep[:], [t_ld], 6)

        k.wait("sync", t_w1, t_w2)
        t_wf = None
        for i in range(8):
            t_wf = k.dma("sync", w1t[i][:], w1_b[i * 128:(i + 1) * 128, :])
        for i in range(32):
            t_wf = k.dma("sync", w2t[i][:], w2_b[i * 128:(i + 1) * 128, :])

        tsr = k.dma("sync", srowf[:], slotrow128)
        k.wait("gpsimd", tsr)
        tsru = k.dma("gpsimd", srowu[:], srowf[:])
        k.wait("gpsimd", tsru, t_ar_slot)
        tsl = k.dma16(lambda e: e.indirect_dma_start(
            out=slot4[:], out_offset=None,
            in_=slotmap_out, in_offset=IndirectOffsetOnAxis(srowu[:], 0)))
        k.wait("vector", tsl)
        k.eop("vector", lambda e: e.tensor_scalar(
            slot4[:], slot4[:], -1.0, 0.0, op0=OP.add, op1=OP.max))
        tsi0 = k.eop("vector", lambda e: e.tensor_scalar_min(
            slot4[:], slot4[:], float(NCORES * TOK - 1)))
        k.wait("gpsimd", tsi0)
        tsi = k.dma("gpsimd", slot4u[:], slot4[:])

        k.wait("gpsimd", t_ag_h2, tsi)

        gather_state = {"tgi": [None, None], "tw": {}, "tlast": [None, None]}

        def gather(g, gb, waitmm=None):
            toks = []
            for half in range(2):
                col = 2 * g + half
                iu = slot4u[:, 4 * col:4 * col + 1]
                k.wait("gpsimd", tsi, gather_state["tw"].get((gb, half)))
                tgi = k.dma16(lambda e, iu=iu, gb=gb, half=half: e.indirect_dma_start(
                    out=btm[2 * gb + half][:], out_offset=None,
                    in_=h2_all, in_offset=IndirectOffsetOnAxis(iu[:], 0)))
                gather_state["tgi"][half] = tgi
                k.wait("sync", tgi, gather_state["tlast"][gb])
                tw = k.dma("sync", ffn_tm[gb][half * 128:(half + 1) * 128, :],
                           btm[2 * gb + half][:])
                gather_state["tw"][(gb, half)] = tw
                toks.append(tw)
            k.wait("sync", *toks)
            if waitmm is not None:
                k.wait("sync", waitmm)
            tlast = None
            for dc in range(8):
                tlast = k.dma("sync", bufg[gb][:, dc, :],
                              ffn_tm[gb][:, dc * 128:(dc + 1) * 128], transpose=True)
            gather_state["tlast"][gb] = tlast
            if debug and g == 0:
                k.dma("sync", dbg["d_ffng0"], ffn_tm[0], sem="od")
            return tlast

        tga = [gather(0, 0), None]
        t_eout_w = [None, None]
        ts1_prev = [None, None]
        for g in range(NFFG):
            gb = g % 2
            k.wait("tensor", tga[gb], t_wf)
            last_mm1 = None
            for fc in range(32):
                pb = fc % 2
                k.wait("tensor", ts1_prev[pb])
                tmm1 = None
                for dc in range(8):
                    lhs = w1t[dc][:, fc * 128:(fc + 1) * 128]
                    rhs = bufg[gb][:, dc, :]
                    fn = lambda e, lhs=lhs, rhs=rhs, dc=dc, pb=pb: e.matmul(
                        bank[pb][:, 0:FFG], lhs, rhs, start=(dc == 0), stop=(dc == 7))
                    if dc < 7:
                        k.raw("tensor", fn)
                    else:
                        tmm1 = k.op("tensor", fn)
                k.wait("scalar", tmm1)
                b1ap = b1_s[:, fc:fc + 1]
                ts1 = k.op("scalar", lambda e, pb=pb, b1ap=b1ap: e.activation(
                    y1c[pb][:], bank[pb][:, 0:FFG], AF.Relu, bias=b1ap))
                ts1_prev[pb] = ts1
                k.wait("tensor", ts1)
                for tc in range(2):
                    lhs = y1c[pb][:, tc * 128:(tc + 1) * 128]
                    for dg in range(2):
                        rhs = w2t[fc][:, dg * 512:(dg + 1) * 512]
                        bi = 2 + tc * 2 + dg
                        fn = lambda e, lhs=lhs, rhs=rhs, bi=bi, fc=fc: e.matmul(
                            bank[bi][:, 0:512], lhs, rhs,
                            start=(fc == 0), stop=(fc == 31))
                        if fc == 31 and tc == 1 and dg == 1:
                            tmm2 = k.op("tensor", fn)
                        else:
                            k.raw("tensor", fn)
                last_mm1 = tmm1
            if g + 1 < NFFG:
                tga[(g + 1) % 2] = gather(g + 1, (g + 1) % 2, waitmm=last_mm1)
            for tc in range(2):
                ebuf = eost[tc]
                k.wait("vector", tmm2, t_eout_w[tc])
                tve = None
                for dg in range(2):
                    bi = 2 + tc * 2 + dg
                    dsl = slice(dg * 512, (dg + 1) * 512)
                    tve = k.op("vector", lambda e, bi=bi, dsl=dsl, ebuf=ebuf: e.tensor_add(
                        ebuf[:, dsl], bank[bi][:, 0:512], b2rep[:, dsl]))
                k.wait("sync", tve)
                r0 = g * FFG + tc * 128
                tst = k.dma("sync", eout_bounce[r0:r0 + 128, :], ebuf[:])
                t_eout_w[tc] = tst
                k.wait("tensor", tve)
        k.wait("gpsimd", *t_eout_w)
        t_ag_eo = k.cc("AllGather", OP.bypass, [eout_bounce.opt()], [eout_all.opt()])

        if debug:
            k.wait("sync", *t_eout_w)
            k.dma("sync", dbg["d_eout"], eout_bounce, sem="od")
            k.dma("sync", dbg["d_slotmap"], slotmap_out, sem="od")
            k.wait("sync", tsi)
            k.dma("sync", dbg["d_slot4"], slot4[:], sem="od")

        k.barrier()
        k.flush()

    if phases < 4:
        _stub_out(k, nc, out, es)
        k.flush()
        es.close()
        return nc
    with ExitStack() as p4:
        sb4 = lambda name, shape, dt: p4.enter_context(nc.sbuf_tensor(name, shape, dt))
        egst = [sb4(f"egst{i}", [128, D], F32) for i in range(2)]
        x2l = [sb4(f"x2l{i}", [128, D], F32) for i in range(2)]
        otile = [sb4(f"otile{i}", [128, D], F32) for i in range(2)]
        ftmp = sb4("ftmp", [128, D], F32)
        gbrep_t = [sb4(f"gbrep_t{i}", [128, D], F32) for i in range(2)]

        for b in range(2):
            t_ld = k.dma("sync", rowsrc[:], modrow_dram[12 + b:13 + b, :])
            t_r = replicate_row(rowsrc[:], gbrep_t[b][:], [t_ld], 0)
            k.wait("sync", t_r)

        k.wait("gpsimd", t_ag_eo)
        k.wait("sync", t_ag_eo)
        tprev = [None, None]
        tso_prev = [None, None]
        tout_all = []
        for ch in range(NCH):
            sl = ch % 2
            b = ch // 16
            idxcol = idxg_s[:, ch:ch + 1]
            k.wait("gpsimd", tprev[sl])
            tg_ = k.dma16(lambda e, idxcol=idxcol, sl=sl: e.indirect_dma_start(
                out=egst[sl][:], out_offset=None,
                in_=eout_all, in_offset=IndirectOffsetOnAxis(idxcol, 0)))
            k.wait("sync", tprev[sl])
            tx_ = k.dma("sync", x2l[sl][:], x2_dram[ch * 128:(ch + 1) * 128, :])
            gate_ap = gate_s[:, ch:ch + 1]
            gb_ap = gbrep_t[b][:]
            k.wait("vector", tg_, tx_, tso_prev[sl])
            k.op("vector", lambda e, sl=sl, gate_ap=gate_ap, gb_ap=gb_ap:
                 e.scalar_tensor_tensor(ftmp[:], egst[sl][:], gate_ap, gb_ap,
                                        op0=OP.mult, op1=OP.mult))
            tvo = k.op("vector", lambda e, sl=sl: e.tensor_add(
                otile[sl][:], ftmp[:], x2l[sl][:]))
            tprev[sl] = tvo
            k.wait("sync", tvo)
            tso = k.dma("sync", out[ch * 128:(ch + 1) * 128, :], otile[sl][:], sem="od")
            tso_prev[sl] = tso
            tout_all.append(tso)
        k.wait("sync", tout_all[-1])
        k.barrier()
        k.flush()

    es.close()
    return nc


_CACHE = {}


def _get_nc(debug=False):
    if debug not in _CACHE:
        _CACHE[debug] = build_nc(debug)
    return _CACHE[debug]


def kernel(**inputs):
    return _run(inputs, debug=False)[0]


def _run(inputs, debug=False):
    f32 = np.float32
    x = np.asarray(inputs["L0_attn_x"], f32)
    cvec = np.asarray(inputs["L0_c"], f32)
    Wq = np.asarray(inputs["L0_attn_Wq"], f32); bq = np.asarray(inputs["L0_attn_bq"], f32)
    Wk = np.asarray(inputs["L0_attn_Wk"], f32); bk = np.asarray(inputs["L0_attn_bk"], f32)
    Wv = np.asarray(inputs["L0_attn_Wv"], f32); bv = np.asarray(inputs["L0_attn_bv"], f32)
    Wu = np.asarray(inputs["L0_attn_Wu"], f32); bu = np.asarray(inputs["L0_attn_bu"], f32)
    Wo = np.asarray(inputs["L0_attn_Wo"], f32); bo = np.asarray(inputs["L0_attn_bo"], f32)
    ascale = np.asarray(inputs["L0_attn_scale"], f32)
    gw = np.asarray(inputs["L0_ffn_gate_w"], f32); gb_ = np.asarray(inputs["L0_ffn_gate_b"], f32)
    w1 = np.asarray(inputs["L0_ffn_w1"], f32); b1 = np.asarray(inputs["L0_ffn_b1"], f32)
    w2 = np.asarray(inputs["L0_ffn_w2"], f32); b2 = np.asarray(inputs["L0_ffn_b2"], f32)
    wmod_ = np.asarray(inputs["L0_wmod"], f32)

    C = np.ascontiguousarray

    def fmvec(v):
        return C(v.reshape(8, 128).T)

    shared = {
        "wqT": C(Wq.T), "wkT": C(Wk.T), "wvT": C(Wv.T), "wuT": C(Wu.T),
        "woT": C(Wo.T), "wmod": C(wmod_),
        "gwT": C(gw.T.reshape(8, 128, 8).transpose(1, 0, 2).reshape(128, 64)),
        "bq_fm": fmvec(bq), "bk_fm": fmvec(bk), "bu_fm": fmvec(bu),
        "bk_row": bk.reshape(1, D), "bv_row": bv.reshape(1, D),
        "b2_row": None, "bo_row": bo.reshape(1, D),
        "gb_col": gb_.reshape(8, 1),
        "ascale_fm": fmvec(ascale),
    }
    in_maps = []
    for c in range(NCORES):
        xs = x[2 * c:2 * c + 2].reshape(TOK, D)
        m = dict(shared)
        m["x_tm"] = C(xs)
        m["x_fm"] = C(xs.T)
        cc_ = cvec[2 * c:2 * c + 2]
        m["cT"] = C(cc_.T.reshape(8, 128, 2).transpose(1, 0, 2).reshape(128, 16))
        m["w1e"] = C(w1[c]); m["w2e"] = C(w2[c])
        m["b1_fm"] = C(b1[c].reshape(32, 128).T)
        m["b2_row"] = b2[c].reshape(1, D)
        cm = np.zeros((8, 1), f32); cm[:c, 0] = 1.0
        m["core_mask"] = cm
        co = np.zeros((8, 1), f32); co[c, 0] = 1.0
        m["core_oh"] = co
        m["slotrow128"] = (float(c * CAP) + np.arange(128, dtype=f32)[:, None] * 40.0)
        m["tokbase"] = np.full((128, 1), float(TOK * c), f32)
        in_maps.append(m)

    nc = _get_nc(debug)
    res = run_bass_kernel_spmd(nc, in_maps, core_ids=list(range(NCORES)))
    outs = np.concatenate([res.results[c]["out"].reshape(2, N, D)
                           for c in range(NCORES)], axis=0)
    return outs, res


# version 02
# speedup vs baseline: 1.0048x; 1.0048x over previous
import numpy as np
from contextlib import ExitStack

import concourse.bass as bass
import concourse.mybir as mybir
from concourse.bass import IndirectOffsetOnAxis
from concourse import library_config as libcfg
from concourse.bass_utils import run_bass_kernel_spmd

F32 = mybir.dt.float32
BF16 = mybir.dt.bfloat16
U32 = mybir.dt.uint32
I16 = mybir.dt.int16
AF = mybir.ActivationFunctionType
OP = mybir.AluOpType
AX = mybir.AxisListType

NCORES = 8
N = 2048
D = 1024
H = 8
TOK = 2 * N
NCH = TOK // 128
SEG = 512
NSEG_B = N // SEG
FF = 4096
CAP = 5120
FFG = 256
NFFG = CAP // FFG


def _stub_out(k, nc, out, es):
    zt = es.enter_context(nc.sbuf_tensor("stub_z", [128, 512], mybir.dt.float32))
    tz = k.op("gpsimd", lambda e: e.memset(zt[:], 0.0))
    k.wait("sync", tz)
    last = None
    for r in range(TOK // 128):
        for cg in range(2):
            last = k.dma("sync", out[r * 128:(r + 1) * 128, cg * 512:(cg + 1) * 512],
                         zt[:], sem="od")
    k.wait("sync", last)
    k.barrier()


class KB:
    ENGINES = ("sync", "tensor", "vector", "scalar", "gpsimd")

    def __init__(self, nc, es, safe=False):
        self.nc = nc
        self.es = es
        self.safe = safe
        self.items = []
        self.semh = {}
        self.semv = {}
        for s in ("ld", "gd", "od", "cc", "t", "v", "s", "g", "ph"):
            self.semh[s] = es.enter_context(nc.semaphore(s))
            self.semv[s] = 0
        self.block = es.enter_context(nc.Block())

    def on(self, eng):
        def deco(fn):
            self.items.append((eng, fn))
        return deco

    def tok(self, sem):
        return (sem, self.semv[sem])

    def dma(self, eng, out, in_, sem=None, transpose=False, slow=False):
        sem = sem or ("gd" if eng == "gpsimd" else "ld")
        self.semv[sem] += 16
        h, v = self.semh[sem], self.semv[sem]

        @self.on(eng)
        def _(e):
            if transpose:
                e.dma_start_transpose(out, in_).then_inc(h, 16)
            elif slow:
                e.dma_start(out=out, in_=in_,
                            allow_slow_non_contiguous=True).then_inc(h, 16)
            else:
                e.dma_start(out=out, in_=in_).then_inc(h, 16)
        return (sem, v)

    def dma16(self, fn):
        self.semv["gd"] += 16
        h, v = self.semh["gd"], self.semv["gd"]

        @self.on("gpsimd")
        def _(e):
            fn(e).then_inc(h, 16)
        return ("gd", v)

    def op(self, eng, fn, inc=None):
        if inc is None:
            inc = {"tensor": "t", "vector": "v", "scalar": "s", "gpsimd": "g",
                   "sync": "g"}[eng]
        self.semv[inc] += 1
        h, v = self.semh[inc], self.semv[inc]
        safe_wait = (h, v - 1) if self.safe else None

        @self.on(eng)
        def _(e):
            if safe_wait is not None and safe_wait[1] > 0:
                e.wait_ge(safe_wait[0], safe_wait[1])
            fn(e).then_inc(h, 1)
        return (inc, v)

    def eop(self, eng, fn, inc=None):
        incs = {"tensor": "t", "vector": "v", "scalar": "s", "gpsimd": "g",
                "sync": "g"}[eng]
        self.wait(eng, (incs, self.semv[incs]))
        return self.op(eng, fn, inc=inc)

    def raw(self, eng, fn):
        if self.safe:
            self.op(eng, fn)
            return

        @self.on(eng)
        def _(e):
            fn(e)

    DMA_SEMS = ("ld", "gd", "od", "cc")

    def wait(self, eng, *tokens):
        toks = []
        seen = {}
        for t in tokens:
            if t is None or t[1] <= 0:
                continue
            s, v = t
            if s in self.DMA_SEMS:
                v = self.semv[s]
            seen[s] = max(seen.get(s, 0), v)
        toks = list(seen.items())
        if not toks:
            return

        @self.on(eng)
        def _(e):
            for s, v in toks:
                e.wait_ge(self.semh[s], v)

    def cc(self, kind, op, ins, outs):
        self.wait("gpsimd", ("cc", self.semv["cc"]))
        self.semv["cc"] += 1
        h, v = self.semh["cc"], self.semv["cc"]

        @self.on("gpsimd")
        def _(e):
            e.collective_compute(kind, op, replica_groups=[list(range(NCORES))],
                                 ins=ins, outs=outs).then_inc(h, 1)
        return ("cc", v)

    def xwait(self, *engines):
        toks = [self.tok(s) for s in ("t", "v", "s", "g", "ld", "gd")]
        for eng in engines:
            self.wait(eng, *toks)

    def barrier(self):
        self.semv["ph"] += len(self.ENGINES)
        h, v = self.semh["ph"], self.semv["ph"]
        for eng in self.ENGINES:
            @self.on(eng)
            def _(e):
                e.sem_inc(h, 1)
                e.wait_ge(h, v)

    def flush(self):
        items = self.items
        self.items = []
        for eng_name, deco in (("sync", self.block.sync), ("tensor", self.block.tensor),
                               ("vector", self.block.vector), ("scalar", self.block.scalar),
                               ("gpsimd", self.block.gpsimd)):
            sub = [fn for e, fn in items if e == eng_name]
            if not sub:
                continue

            def make(sub):
                def body(engine):
                    for fn in sub:
                        fn(engine)
                return body
            deco(make(sub))


def build_nc(debug=False, phases=4, p0sub=99, safe=False):
    nc = bass.Bass("TRN2", target_bir_lowering=False, debug=False,
                   detect_race_conditions=safe)

    def inp(name, shape, dt=F32):
        return nc.dram_tensor(name, shape, dt, kind="ExternalInput").ap()

    def idram(name, shape, dt):
        return nc.dram_tensor(name, shape, dt, kind="Internal").ap()

    x_tm = inp("x_tm", [TOK, D])
    x_fm = inp("x_fm", [D, TOK])
    cT = inp("cT", [128, 16])
    wqT = inp("wqT", [D, D]); wkT = inp("wkT", [D, D])
    wvT = inp("wvT", [D, D]); wuT = inp("wuT", [D, D])
    woT = inp("woT", [D, D])
    wmod = inp("wmod", [D, 6 * D])
    gwT = inp("gwT", [128, 64])
    w1e = inp("w1e", [D, FF]); w2e = inp("w2e", [FF, D])
    bq_fm = inp("bq_fm", [128, 8]); bk_fm = inp("bk_fm", [128, 8])
    bu_fm = inp("bu_fm", [128, 8])
    b1_fm = inp("b1_fm", [128, 32])
    bk_row = inp("bk_row", [1, D]); bv_row = inp("bv_row", [1, D])
    b2_row = inp("b2_row", [1, D]); bo_row = inp("bo_row", [1, D])
    gb_col = inp("gb_col", [8, 1])
    ascale_fm = inp("ascale_fm", [128, 8])
    core_mask = inp("core_mask", [8, 1])
    core_oh = inp("core_oh", [8, 1])
    slotrow128 = inp("slotrow128", [128, 1])
    tokbase = inp("tokbase", [128, 1])

    wqT_b = idram("wqT_b", [D, D], BF16); wkT_b = idram("wkT_b", [D, D], BF16)
    wvT_b = idram("wvT_b", [D, D], BF16); wuT_b = idram("wuT_b", [D, D], BF16)
    woT_b = idram("woT_b", [D, D], BF16)
    wmod_b = idram("wmod_b", [D, 6 * D], BF16)
    gwT_b = idram("gwT_b", [128, 64], BF16)
    w1_b = idram("w1_b", [D, FF], BF16); w2_b = idram("w2_b", [FF, D], BF16)
    xfm_b = idram("xfm_b", [D, TOK], BF16)
    x2_dram = idram("x2_dram", [TOK, D], F32)
    x2_btm = idram("x2_btm", [TOK, D], BF16)
    h2fm = idram("h2fm", [D, TOK], BF16)
    h2_bounce = idram("h2_bounce", [TOK, D], BF16)
    h2_all = idram("h2_all", [NCORES * TOK, D], BF16)
    SLOTS = NCORES * CAP + 128
    slotmap_in = idram("slotmap_in", [SLOTS, 4], F32)
    slotmap_out = idram("slotmap_out", [SLOTS, 4], F32)
    cnt_in = idram("cnt_in", [128, 8], F32)
    cnt_out = idram("cnt_out", [1024, 8], F32)
    eout_bounce = idram("eout_bounce", [CAP, D], F32)
    modrow_dram = idram("modrow_dram", [16, D], F32)
    ffn_tm = [idram(f"ffn_tm{i}", [FFG, D], BF16) for i in range(2)]
    eout_all = idram("eout_all", [NCORES * CAP, D], F32)
    out = nc.dram_tensor("out", [TOK, D], F32, kind="ExternalOutput").ap()
    dbg = {}
    if debug:
        for nm, shape, dt in (("d_x2", [TOK, D], F32), ("d_h2", [TOK, D], BF16),
                              ("d_gate", [128, NCH], F32), ("d_idxg", [128, NCH], F32),
                              ("d_eout", [CAP, D], F32), ("d_cnt", [8, 8], F32),
                              ("d_slotmap", [NCORES * CAP + 128, 4], F32),
                              ("d_oraw", [D, TOK], BF16), ("d_frow", [1, TOK], F32),
                              ("d_qt", [D, TOK], BF16), ("d_kt", [D, TOK], BF16),
                              ("d_vtm", [TOK, D], BF16), ("d_ut", [D, TOK], BF16),
                              ("d_og", [D, TOK], BF16), ("d_wto", [D, D], BF16),
                              ("d_gabo", [128, D], F32), ("d_mx", [128, NCH], F32),
                              ("d_idxf", [128, NCH], F32), ("d_possel", [128, NCH], F32),
                              ("d_keep", [128, NCH], F32), ("d_tpr", [128, NCH], F32),
                              ("d_cum", [128, NCH * 8], F32), ("d_comb", [128, NCH * 8], F32),
                              ("d_scu", [128, NCH], U32), ("d_xh", [D, TOK], BF16),
                              ("d_ab", [2, TOK], F32), ("d_cmat", [8, 8], F32),
                              ("d_tt32", [32, 8], F32), ("d_ttot", [1, NCH * 8], F32),
                              ("d_mv", [128, 64], F32), ("d_scdbg", [128, NCH * 4], F32),
                              ("d_ffng0", [FFG, D], BF16), ("d_slot4", [128, CAP // 128 * 4], F32),
                              ("d_logits", [128, NCH * 8], F32),
                              ("d_csil", [128, 16], F32),
                              ("d_modsb", [2, 6 * D], F32)):
            dbg[nm] = nc.dram_tensor(nm, shape, dt, kind="ExternalOutput").ap()

    es = ExitStack()
    k = KB(nc, es, safe=safe)
    sbg = lambda name, shape, dt: es.enter_context(nc.sbuf_tensor(name, shape, dt))
    bank = [es.enter_context(nc.psum_tensor(f"bank{i}", [128, 512], F32))
            for i in range(8)]

    ones_r_f = sbg("ones_r_f", [1, 128], F32)
    ones_s_b = sbg("ones_s_b", [128, 1], BF16)
    zeros_b = sbg("zeros_b", [128, 512], BF16)
    triu_f = sbg("triu_f", [128, 128], F32)
    triu_b = sbg("triu_b", [128, 128], BF16)
    su32_f = sbg("su32_f", [32, 32], F32)
    ident_f = sbg("ident_f", [128, 128], F32)
    ident_b = sbg("ident_b", [128, 128], BF16)
    ones32_f = sbg("ones32_f", [32, 1], F32)
    ones_1x1 = sbg("ones_1x1", [1, 1], F32)
    bq_s = sbg("bq_s", [128, 8], F32); bk_s = sbg("bk_s", [128, 8], F32)
    bu_s = sbg("bu_s", [128, 8], F32); b1_s = sbg("b1_s", [128, 32], F32)
    asc_s = sbg("asc_s", [128, 8], F32)
    gbc_s = sbg("gbc_s", [8, 1], F32)
    cmask_s = sbg("cmask_s", [8, 1], F32); coh_s = sbg("coh_s", [8, 1], F32)
    onepca = [sbg(f"onepca{b}", [128, 8], F32) for b in range(2)]
    sa_s = [sbg(f"sa_s{b}", [128, 8], F32) for b in range(2)]
    onepcb = [sbg(f"onepcb{b}", [128, 8], F32) for b in range(2)]
    sbv_s = [sbg(f"sbv_s{b}", [128, 8], F32) for b in range(2)]
    logits_tm = sbg("logits_tm", [128, NCH, 8], F32)
    gate_s = sbg("gate_s", [128, NCH], F32)
    idxg_s = sbg("idxg_s", [128, NCH], U32)

    t_wq = k.dma("gpsimd", wqT_b, wqT); t_wk = k.dma("gpsimd", wkT_b, wkT)
    t_wv = k.dma("gpsimd", wvT_b, wvT); t_wu = k.dma("gpsimd", wuT_b, wuT)
    t_wo = k.dma("gpsimd", woT_b, woT)
    t_wm = k.dma("gpsimd", wmod_b, wmod)
    t_gw = k.dma("gpsimd", gwT_b, gwT)
    t_xf = k.dma("gpsimd", xfm_b, x_fm)
    t_w1 = k.dma("gpsimd", w1_b, w1e)
    t_w2 = k.dma("gpsimd", w2_b, w2e)

    if p0sub < 1:
        _stub_out(k, nc, out, es)
        k.flush()
        es.close()
        return nc
    t_small = None
    for dst, src in ((bq_s[:], bq_fm), (bk_s[:], bk_fm), (bu_s[:], bu_fm),
                     (b1_s[:], b1_fm), (asc_s[:], ascale_fm), (gbc_s[:], gb_col),
                     (cmask_s[:], core_mask), (coh_s[:], core_oh)):
        t_small = k.dma("sync", dst, src)

    for t_ in (ones_r_f, ones_s_b, zeros_b, ones32_f, ones_1x1):
        k.op("gpsimd", lambda e, t_=t_: e.memset(t_[:], 1.0 if t_ is not zeros_b else 0.0))
    tg0 = k.op("gpsimd", lambda e: e.memset(triu_f[:], 1.0))
    k.wait("gpsimd", tg0)
    k.op("gpsimd", lambda e: e.affine_select(triu_f[:], triu_f[:], [[1, 128]],
                                             compare_op=OP.is_ge, fill=0.0, base=0,
                                             channel_multiplier=-1))
    tg0 = k.op("gpsimd", lambda e: e.memset(triu_b[:], 1.0))
    k.wait("gpsimd", tg0)
    k.op("gpsimd", lambda e: e.affine_select(triu_b[:], triu_b[:], [[1, 128]],
                                             compare_op=OP.is_ge, fill=0.0, base=0,
                                             channel_multiplier=-1))
    tg0 = k.op("gpsimd", lambda e: e.memset(su32_f[:], 1.0))
    k.wait("gpsimd", tg0)
    k.op("gpsimd", lambda e: e.affine_select(su32_f[:], su32_f[:], [[1, 32]],
                                             compare_op=OP.is_ge, fill=0.0, base=-1,
                                             channel_multiplier=-1))
    tg0 = k.op("gpsimd", lambda e: e.memset(ident_f[:], 1.0))
    k.wait("gpsimd", tg0)
    k.op("gpsimd", lambda e: e.affine_select(ident_f[:], ident_f[:], [[1, 128]],
                                             compare_op=OP.is_equal, fill=0.0, base=0,
                                             channel_multiplier=-1))
    tg0 = k.op("gpsimd", lambda e: e.memset(ident_b[:], 1.0))
    k.wait("gpsimd", tg0)
    t_const = k.op("gpsimd", lambda e: e.affine_select(ident_b[:], ident_b[:], [[1, 128]],
                                                       compare_op=OP.is_equal, fill=0.0,
                                                       base=0, channel_multiplier=-1))

    if p0sub < 2:
        _stub_out(k, nc, out, es)
        k.flush()
        es.close()
        return nc
    rowsrc = sbg("rowsrc", [1, D], F32)

    def replicate_row(row_ap, dst_sb, wait_toks, bank_i):
        last = None
        for g in range(D // 512):
            dslice = dst_sb[:, g * 512:(g + 1) * 512]
            rslice = row_ap[:, g * 512:(g + 1) * 512]
            k.wait("tensor", *wait_toks, t_const)
            tmm = k.op("tensor", lambda e, rs=rslice, bi=bank_i: e.matmul(
                bank[bi][:, 0:512], ones_r_f[:], rs, start=True, stop=True))
            k.wait("vector", tmm)
            last = k.op("vector", lambda e, ds=dslice, bi=bank_i: e.tensor_copy(
                ds, bank[bi][:, 0:512]))
            k.wait("tensor", last)
        return last

    if p0sub < 3:
        _stub_out(k, nc, out, es)
        k.flush()
        es.close()
        return nc
    csil = sbg("csil", [128, 16], F32)
    cT_s = sbg("cT_s", [128, 16], F32)
    t_c = k.dma("sync", cT_s[:], cT)
    k.wait("scalar", t_c)
    t_cs = k.op("scalar", lambda e: e.activation(csil[:], cT_s[:], AF.Silu))

    if p0sub < 4:
        _stub_out(k, nc, out, es)
        k.flush()
        es.close()
        return nc
    with ExitStack() as p0es:
        wmt = [p0es.enter_context(nc.sbuf_tensor(f"wmt{i}", [128, 6 * D], F32))
               for i in range(2)]
        mod_sb = p0es.enter_context(nc.sbuf_tensor("mod_sb", [2, 6 * D], F32))
        mrow_sb = p0es.enter_context(nc.sbuf_tensor("mrow_sb", [1, 6 * D], F32))
        rowtmp = p0es.enter_context(nc.sbuf_tensor("rowtmp", [1, D], F32))
        rowtmp2 = p0es.enter_context(nc.sbuf_tensor("rowtmp2", [1, D], F32))
        browtmp = p0es.enter_context(nc.sbuf_tensor("browtmp", [1, D], F32))

        k.barrier()
        t_pe = [None, None]
        for half in range(2):
            t_wml = [None, None]
            tmm6 = [None] * 6
            for dc in range(8):
                k.wait("sync", t_wml[dc % 2], t_pe[dc % 2])
                t_wml[dc % 2] = k.dma("sync", wmt[dc % 2][:],
                                      wmod[dc * 128:(dc + 1) * 128, :])
                k.wait("tensor", t_wml[dc % 2], t_cs, t_const)
                lhs = csil[:, dc * 2:dc * 2 + 2]
                for mg6 in range(6):
                    mg = half * 6 + mg6
                    rhs = wmt[dc % 2][:, mg * 512:(mg + 1) * 512]
                    fn = lambda e, lhs=lhs, rhs=rhs, dc=dc, mg6=mg6: e.matmul(
                        bank[mg6][0:2, 0:512], lhs, rhs,
                        start=(dc == 0), stop=(dc == 7))
                    tmm6[mg6] = k.op("tensor", fn)
                    if mg6 == 5:
                        t_pe[dc % 2] = tmm6[5]
            tv = None
            for mg6 in range(6):
                mg = half * 6 + mg6
                k.wait("vector", tmm6[mg6])
                ds = mod_sb[:, mg * 512:(mg + 1) * 512]
                tv = k.eop("vector", lambda e, ds=ds, mg6=mg6: e.tensor_copy(
                    ds, bank[mg6][0:2, 0:512]))
            k.wait("tensor", tv)
        if debug:
            k.wait("sync", tv, t_cs)
            k.dma("sync", dbg["d_modsb"][:, half * 3 * D:(half + 1) * 3 * D],
                  mod_sb[:, half * 3 * D:(half + 1) * 3 * D], sem="od")
            k.dma("sync", dbg["d_csil"], csil[:], sem="od")
        if p0sub < 5:
            _stub_out(k, nc, out, es)
            k.flush()
            es.close()
            return nc

        def clamp_row(src_ap, dst_ap):
            k.wait("vector", k.tok("t"), k.tok("s"), k.tok("ld"))
            k.wait("scalar", k.tok("v"), k.tok("t"), k.tok("ld"))
            t0 = k.op("scalar", lambda e, s=src_ap: e.activation(rowtmp2[:], s, AF.Abs))
            k.wait("vector", t0)
            t1 = k.eop("vector", lambda e: e.tensor_scalar_max(rowtmp2[:], rowtmp2[:], 0.25))
            t2 = k.op("scalar", lambda e, s=src_ap: e.sign(browtmp[:], s))
            k.wait("vector", t1, t2)
            return k.eop("vector", lambda e, d=dst_ap: e.tensor_mul(d, rowtmp2[:], browtmp[:]))

        for b in range(2):
            k.xwait("sync")
            tmr = k.dma("sync", mrow_sb[:], mod_sb[b:b + 1, :])
            k.wait("vector", tmr)
            k.wait("scalar", tmr)
            mrow = mrow_sb[0:1, :]
            for jj, (vec_i, dst, add1) in enumerate((
                    (0, sa_s[b][:], False), (3, sbv_s[b][:], False),
                    (1, onepca[b][:], True), (4, onepcb[b][:], True))):
                k.wait("tensor", tmr, k.tok("v"), t_const)
                tcol = None
                for dc in range(8):
                    sl = mrow[:, vec_i * D + dc * 128: vec_i * D + (dc + 1) * 128]
                    tcol = k.op("tensor", lambda e, sl=sl, dc=dc: e.matmul(
                        bank[7][:, dc:dc + 1], sl, ones_1x1[:],
                        start=True, stop=True, skip_group_check=True))
                k.wait("vector", tcol)
                if add1:
                    tvc = k.eop("vector", lambda e, dst=dst: e.tensor_scalar_add(
                        dst, bank[7][:, 0:8], 1.0))
                else:
                    tvc = k.eop("vector", lambda e, dst=dst: e.tensor_copy(
                        dst, bank[7][:, 0:8]))
                k.wait("tensor", tvc)
            if p0sub < 6:
                break
            tga = clamp_row(mrow[:, 2 * D:3 * D], rowtmp[:])
            k.wait("sync", tga)
            k.dma("sync", modrow_dram[8 + b:9 + b, :], rowtmp[:])
            k.wait("sync", tga)
            t_bo = k.dma("sync", browtmp[:], bo_row)
            k.wait("vector", t_bo, tga)
            tgabo = k.eop("vector", lambda e: e.tensor_mul(rowtmp2[:], rowtmp[:], browtmp[:]))
            k.wait("sync", tgabo)
            k.dma("sync", modrow_dram[10 + b:11 + b, :], rowtmp2[:])
            tgb = clamp_row(mrow[:, 5 * D:6 * D], rowtmp[:])
            k.wait("sync", tgb)
            k.dma("sync", modrow_dram[12 + b:13 + b, :], rowtmp[:])

        if p0sub < 7:
            _stub_out(k, nc, out, es)
            k.flush()
            es.close()
            return nc
        k.barrier()
        k.flush()

    if debug:
        k.xwait("sync")
        for i_, t_ in enumerate((onepca[0], sa_s[0], onepcb[0], sbv_s[0],
                                 onepca[1], sa_s[1], onepcb[1], sbv_s[1])):
            k.dma("sync", dbg["d_mv"][:, i_ * 8:(i_ + 1) * 8], t_[:], sem="od")

    if phases < 1:
        _stub_out(k, nc, out, es)
        k.flush()
        es.close()
        return nc
    with ExitStack() as p1:
        sb1 = lambda name, shape, dt: p1.enter_context(nc.sbuf_tensor(name, shape, dt))
        wt_q = [sb1(f"wt_q{i}", [128, D], BF16) for i in range(8)]
        wt_k = [sb1(f"wt_k{i}", [128, D], BF16) for i in range(8)]
        wt_v = [sb1(f"wt_v{i}", [128, D], BF16) for i in range(8)]
        wt_u = [sb1(f"wt_u{i}", [128, D], BF16) for i in range(8)]
        wt_o = [sb1(f"wt_o{i}", [128, D], BF16) for i in range(8)]
        gw_s = sb1("gw_s", [128, 64], BF16)
        xhat = [sb1(f"xhat{i}", [128, SEG], BF16) for i in range(8)]
        xsq2 = [sb1(f"xsq{i}", [128, SEG], BF16) for i in range(2)]
        qt = [sb1(f"qt{i}", [128, SEG], BF16) for i in range(8)]
        kt = [sb1(f"kt{i}", [128, SEG], BF16) for i in range(8)]
        ut = [sb1(f"ut{i}", [128, SEG], BF16) for i in range(8)]
        ktm = [sb1(f"ktm{i}", [128, D], BF16) for i in range(4)]
        vtm = [sb1(f"vtm{i}", [128, D], BF16) for i in range(4)]
        ofm = [sb1(f"ofm{i}", [128, SEG], BF16) for i in range(8)]
        atm = sb1("atm", [128, 128], BF16)
        ssb = sb1("ssb", [128, 128], BF16)
        e_b1 = sb1("e_b1", [128, SEG], BF16)
        e_b2 = sb1("e_b2", [128, SEG], BF16)
        e_f3 = sb1("e_f3", [128, SEG], F32)
        srow = [sb1(f"srow{i}", [1, SEG], F32) for i in range(3)]
        xf32 = [sb1(f"xf32{i}", [128, D], F32) for i in range(2)]
        x2st = [sb1(f"x2st{i}", [128, D], F32) for i in range(2)]
        h2tmst = [sb1(f"h2tmst{i}", [128, D], BF16) for i in range(2)]
        lgs = sb1("lgs", [8, SEG], F32)
        bkrep = sb1("bkrep", [128, D], F32)
        bvrep = sb1("bvrep", [128, D], F32)
        garep_t = sb1("garep_t", [128, D], F32)
        gaborep_t = sb1("gaborep_t", [128, D], F32)

        t_ld = k.dma("sync", rowsrc[:], bk_row)
        t_r = replicate_row(rowsrc[:], bkrep[:], [t_ld], 0)
        k.wait("sync", t_r)
        t_ld = k.dma("sync", rowsrc[:], bv_row)
        t_r = replicate_row(rowsrc[:], bvrep[:], [t_ld], 0)
        k.wait("sync", t_r)

        k.wait("sync", t_wq, t_wk, t_wv, t_wu, t_gw)
        t_wload = None
        for i in range(8):
            t_wload = k.dma("sync", wt_q[i][:], wqT_b[i * 128:(i + 1) * 128, :])
            t_wload = k.dma("sync", wt_k[i][:], wkT_b[i * 128:(i + 1) * 128, :])
            t_wload = k.dma("sync", wt_v[i][:], wvT_b[i * 128:(i + 1) * 128, :])
            t_wload = k.dma("sync", wt_u[i][:], wuT_b[i * 128:(i + 1) * 128, :])
        t_gwl = k.dma("sync", gw_s[:], gwT_b)

        t_x2w = [None, None]
        t_xl_prev = [None, None]
        t_h2w = [None, None]
        t_h2bounce_all = []

        def ln_stats_and_norm(get_xin_tok, onep_ap_of, shift_ap_of, b):
            prev_pair = [None, None]
            tmm = None
            for dc in range(8):
                xb = xsq2[dc % 2]
                k.wait("scalar", get_xin_tok[dc], prev_pair[dc % 2])
                tsq = k.op("scalar", lambda e, dc=dc, xb=xb: e.activation(
                    xb[:], xhat[dc][:], AF.Square))
                k.wait("tensor", get_xin_tok[dc], tsq, t_small, t_const)
                k.raw("tensor", lambda e, dc=dc: e.matmul(
                    bank[0][0:1, 0:SEG], ones_s_b[:], xhat[dc][:],
                    start=(dc == 0), stop=(dc == 7)))
                tmm = k.op("tensor", lambda e, dc=dc, xb=xb: e.matmul(
                    bank[1][0:1, 0:SEG], ones_s_b[:], xb[:],
                    start=(dc == 0), stop=(dc == 7)))
                prev_pair[dc % 2] = tmm
            k.wait("vector", tmm)
            k.eop("vector", lambda e: e.tensor_scalar_mul(srow[0][:], bank[0][0:1, 0:SEG], 1.0 / D))
            k.eop("vector", lambda e: e.tensor_scalar_mul(srow[1][:], bank[1][0:1, 0:SEG], 1.0 / D))
            k.eop("vector", lambda e: e.tensor_mul(srow[2][:], srow[0][:], srow[0][:]))
            k.eop("vector", lambda e: e.tensor_sub(
                srow[1][:], srow[1][:], srow[2][:]))
            tvv = k.eop("vector", lambda e: e.tensor_scalar_add(srow[1][:], srow[1][:], 1e-5))
            k.wait("scalar", tvv)
            k.eop("scalar", lambda e: e.activation(srow[1][:], srow[1][:], AF.Ln))
            tse = k.eop("scalar", lambda e: e.activation(srow[1][:], srow[1][:], AF.Exp,
                                                         scale=-0.5))
            k.wait("vector", tse)
            tvb = k.eop("vector", lambda e: e.scalar_tensor_tensor(
                srow[2][:], srow[0][:], -1.0, srow[1][:], op0=OP.mult, op1=OP.mult))
            k.wait("tensor", tse, tvb, tmm)
            tra = k.op("tensor", lambda e: e.matmul(bank[2][:, 0:SEG], ones_r_f[:],
                                                    srow[1][:], start=True, stop=True))
            trb = k.op("tensor", lambda e: e.matmul(bank[3][:, 0:SEG], ones_r_f[:],
                                                    srow[2][:], start=True, stop=True))
            txh = None
            k.wait("vector", tra, trb)
            for dc in range(8):
                onep_ap = onep_ap_of(dc)
                shift_ap = shift_ap_of(dc)
                k.op("vector", lambda e, dc=dc: e.tensor_mul(
                    e_f3[:], xhat[dc][:], bank[2][:, 0:SEG]))
                k.op("vector", lambda e, dc=dc: e.tensor_add(
                    e_f3[:], e_f3[:], bank[3][:, 0:SEG]))
                txh = k.op("vector", lambda e, dc=dc, o=onep_ap, s=shift_ap: e.tensor_scalar(
                    xhat[dc][:], e_f3[:], o, s, op0=OP.mult, op1=OP.add))
            k.wait("tensor", txh)
            return txh

        for b in range(2):
            k.xwait("sync")
            t_ld = k.dma("sync", rowsrc[:], modrow_dram[8 + b:9 + b, :])
            t_r = replicate_row(rowsrc[:], garep_t[:], [t_ld], 0)
            k.wait("sync", t_r)
            t_ld = k.dma("sync", rowsrc[:], modrow_dram[10 + b:11 + b, :])
            t_r = replicate_row(rowsrc[:], gaborep_t[:], [t_ld], 0)
            t_wol = []
            k.wait("sync", t_wo)
            for i in range(8):
                t_wol.append(k.dma("sync", wt_o[i][:], woT_b[i * 128:(i + 1) * 128, :]))
            two = None
            garep_b = garep_t[:]
            for i in range(8):
                k.wait("vector", t_wol[i], t_r)
                two = k.op("vector", lambda e, i=i, g=garep_b: e.tensor_mul(
                    wt_o[i][:], wt_o[i][:], g))
            k.wait("tensor", two, k.tok("s"), k.tok("v"))
            k.raw("tensor", lambda e: e.matmul(bank[6][:], zeros_b[:, 0:128], zeros_b[:],
                                               start=True, stop=True))
            tz = k.op("tensor", lambda e: e.matmul(bank[7][:], zeros_b[:, 0:128], zeros_b[:],
                                                   start=True, stop=True))

            if debug and b == 0:
                k.xwait("sync")
                for i in range(8):
                    k.dma("sync", dbg["d_wto"][i * 128:(i + 1) * 128, :], wt_o[i][:],
                          sem="od")
                k.dma("sync", dbg["d_gabo"], gaborep_t[:], sem="od")
            prev_oevac = None
            prev_s_upd = [tz] * H

            for seg in range(NSEG_B):
                t0 = b * N + seg * SEG

                tx = []
                k.xwait("sync")
                k.wait("sync", t_xf)
                for dc in range(8):
                    tx.append(k.dma("sync", xhat[dc][:],
                                    xfm_b[dc * 128:(dc + 1) * 128, t0:t0 + SEG]))
                k.xwait("tensor", "vector", "scalar")
                t_xhat = ln_stats_and_norm(
                    tx, lambda dc: onepca[b][:, dc:dc + 1], lambda dc: sa_s[b][:, dc:dc + 1], b)

                if debug:
                    k.xwait("sync")
                    for dc in range(8):
                        k.dma("sync", dbg["d_xh"][dc * 128:(dc + 1) * 128, t0:t0 + SEG],
                              xhat[dc][:], sem="od")
                    k.dma("sync", dbg["d_ab"][0:1, t0:t0 + SEG], srow[1][:], sem="od")
                    k.dma("sync", dbg["d_ab"][1:2, t0:t0 + SEG], srow[2][:], sem="od")
                def fm_proj(wt, psum_i, hc):
                    k.wait("tensor", t_xhat, t_wload)
                    tret = None
                    for dc in range(8):
                        lhs = wt[dc][:, hc * 128:(hc + 1) * 128]
                        fn = lambda e, lhs=lhs, dc=dc, pi=psum_i: e.matmul(
                            bank[pi][:, 0:SEG], lhs, xhat[dc][:],
                            start=(dc == 0), stop=(dc == 7))
                        if dc < 7:
                            k.raw("tensor", fn)
                        else:
                            tret = k.op("tensor", fn)
                    return tret

                def elu_evac(psum_i, bias_ap, dst_ap):
                    tr_ = k.op("vector", lambda e, pi=psum_i, ba=bias_ap: e.tensor_scalar(
                        e_b1[:], bank[pi][:, 0:SEG], ba, 0.0, op0=OP.add, op1=OP.max))
                    tm_ = k.op("vector", lambda e, pi=psum_i, ba=bias_ap: e.tensor_scalar(
                        e_b2[:], bank[pi][:, 0:SEG], ba, 0.0, op0=OP.add, op1=OP.min))
                    k.wait("scalar", tm_)
                    te_ = k.op("scalar", lambda e: e.activation(e_b2[:], e_b2[:], AF.Exp))
                    k.wait("vector", te_)
                    k.op("vector", lambda e, d=dst_ap: e.tensor_add(d, e_b1[:], e_b2[:]))
                    return k.op("vector", lambda e, d=dst_ap: e.tensor_scalar_add(d, d, -1.0))

                k.xwait("vector", "scalar")
                for dst_list, wt, bias_sb, act in ((qt, wt_q, bq_s, "elu"),
                                                   (kt, wt_k, bk_s, "elu"),
                                                   (ut, wt_u, bu_s, "silu")):
                    for hc in range(8):
                        pi = 2 + (hc % 2)
                        tmm = fm_proj(wt, pi, hc)
                        bias_ap = bias_sb[:, hc:hc + 1]
                        if act == "elu":
                            k.wait("vector", tmm)
                            tev = elu_evac(pi, bias_ap, dst_list[hc][:])
                            k.wait("tensor", tev)
                        else:
                            k.wait("scalar", tmm)
                            tev = k.op("scalar", lambda e, hc=hc, pi=pi, ba=bias_ap,
                                       d=dst_list[hc]: e.activation(
                                d[:], bank[pi][:, 0:SEG], AF.Silu, bias=ba))
                            k.wait("tensor", tev)

                for tc in range(4):
                    for hg in range(2):
                        pi = 4 + hg
                        hsl = slice(hg * 512, (hg + 1) * 512)
                        k.wait("tensor", t_xhat)
                        tkmm = None
                        for dc in range(8):
                            lhs = xhat[dc][:, tc * 128:(tc + 1) * 128]
                            rhs = wt_k[dc][:, hsl]
                            fn = lambda e, lhs=lhs, rhs=rhs, dc=dc, pi=pi: e.matmul(
                                bank[pi][:, 0:512], lhs, rhs, start=(dc == 0), stop=(dc == 7))
                            if dc < 7:
                                k.raw("tensor", fn)
                            else:
                                tkmm = k.op("tensor", fn)
                        k.wait("vector", tkmm)
                        tv1 = k.op("vector", lambda e, pi=pi, hsl=hsl: e.tensor_add(
                            e_f3[:], bank[pi][:, 0:512], bkrep[:, hsl]))
                        tv2 = k.op("vector", lambda e: e.tensor_scalar_min(
                            e_b2[:], e_f3[:], 0.0))
                        k.wait("scalar", tv2)
                        te = k.op("scalar", lambda e: e.activation(e_b2[:], e_b2[:], AF.Exp))
                        tv3 = k.op("vector", lambda e: e.tensor_scalar_max(
                            e_b1[:], e_f3[:], 0.0))
                        k.wait("vector", te)
                        kdst = ktm[tc][:, hsl]
                        k.op("vector", lambda e, d=kdst: e.tensor_add(d, e_b1[:], e_b2[:]))
                        tv5 = k.op("vector", lambda e, d=kdst: e.tensor_scalar_add(d, d, -1.0))
                        k.wait("tensor", tv1, tv3)
                        tvmm = None
                        for dc in range(8):
                            lhs = xhat[dc][:, tc * 128:(tc + 1) * 128]
                            rhs = wt_v[dc][:, hsl]
                            fn = lambda e, lhs=lhs, rhs=rhs, dc=dc, pi=pi: e.matmul(
                                bank[pi][:, 0:512], lhs, rhs, start=(dc == 0), stop=(dc == 7))
                            if dc < 7:
                                k.raw("tensor", fn)
                            else:
                                tvmm = k.op("tensor", fn)
                        k.wait("vector", tvmm)
                        tv6 = k.op("vector", lambda e, pi=pi, hsl=hsl: e.tensor_add(
                            e_f3[:], bank[pi][:, 0:512], bvrep[:, hsl]))
                        k.wait("scalar", tv6)
                        vdst = vtm[tc][:, hsl]
                        ts7 = k.op("scalar", lambda e, d=vdst: e.activation(
                            d, e_f3[:], AF.Silu))
                        k.wait("tensor", tv6, ts7)

                k.xwait("tensor", "vector", "scalar")
                for h in range(H):
                    sview = bank[6 + h // 4][:, (h % 4) * 128:(h % 4) * 128 + 128]
                    for cc in range(4):
                        gc = seg * 4 + cc
                        csl = slice(cc * 128, cc * 128 + 128)
                        k.wait("tensor", prev_oevac)
                        tat = k.op("tensor", lambda e, h=h, csl=csl: e.matmul(
                            bank[0][:, 0:128], kt[h][:, csl], qt[h][:, csl],
                            start=True, stop=True))
                        k.wait("vector", tat)
                        tatm = k.op("vector", lambda e: e.tensor_mul(
                            atm[:], bank[0][:, 0:128], triu_f[:]))
                        tsc = None
                        if gc > 0:
                            k.wait("vector", prev_s_upd[h])
                            tsc = k.op("vector", lambda e, sview=sview: e.tensor_copy(
                                ssb[:], sview))
                        k.wait("tensor", tatm)
                        fn_intra = lambda e, cc=cc, h=h, gc=gc: e.matmul(
                            bank[1][:, 0:128], vtm[cc][:, h * 128:(h + 1) * 128], atm[:],
                            start=True, stop=(gc == 0))
                        if gc > 0:
                            k.raw("tensor", fn_intra)
                            k.wait("tensor", tsc)
                            t_o = k.op("tensor", lambda e, h=h, csl=csl: e.matmul(
                                bank[1][:, 0:128], ssb[:], qt[h][:, csl],
                                start=False, stop=True))
                        else:
                            t_o = k.op("tensor", fn_intra)
                        if tsc is not None:
                            k.wait("tensor", tsc)
                        t_s = k.op("tensor", lambda e, cc=cc, h=h, sview=sview: e.matmul(
                            sview, ktm[cc][:, h * 128:(h + 1) * 128],
                            vtm[cc][:, h * 128:(h + 1) * 128], start=False, stop=False,
                            skip_group_check=True))
                        prev_s_upd[h] = t_s
                        k.wait("scalar", t_o)
                        prev_oevac = k.op("scalar", lambda e, h=h, csl=csl: e.copy(
                            ofm[h][:, csl], bank[1][:, 0:128]))

                if debug:
                    k.xwait("sync")
                    for hc in range(8):
                        k.dma("sync", dbg["d_oraw"][hc * 128:(hc + 1) * 128, t0:t0 + SEG],
                              ofm[hc][:], sem="od")
                        k.dma("sync", dbg["d_ut"][hc * 128:(hc + 1) * 128, t0:t0 + SEG],
                              ut[hc][:], sem="od")
                        k.dma("sync", dbg["d_qt"][hc * 128:(hc + 1) * 128, t0:t0 + SEG],
                              qt[hc][:], sem="od")
                        k.dma("sync", dbg["d_kt"][hc * 128:(hc + 1) * 128, t0:t0 + SEG],
                              kt[hc][:], sem="od")
                    for tc in range(4):
                        k.dma("sync", dbg["d_vtm"][t0 + tc * 128:t0 + (tc + 1) * 128, :],
                              vtm[tc][:], sem="od")
                k.xwait("tensor", "vector", "scalar")
                k.wait("scalar", prev_oevac)
                prev_pair_r = [None, None]
                tmr = None
                for hc in range(8):
                    xb = xsq2[hc % 2]
                    k.wait("scalar", prev_pair_r[hc % 2])
                    tsq = k.op("scalar", lambda e, hc=hc, xb=xb: e.activation(
                        xb[:], ofm[hc][:], AF.Square))
                    k.wait("tensor", tsq, prev_oevac)
                    tmr = k.op("tensor", lambda e, hc=hc, xb=xb: e.matmul(
                        bank[4][0:1, 0:SEG], ones_s_b[:], xb[:],
                        start=(hc == 0), stop=(hc == 7)))
                    prev_pair_r[hc % 2] = tmr
                k.wait("vector", tmr)
                tvr = k.eop("vector", lambda e: e.tensor_scalar_mul(
                    srow[0][:], bank[4][0:1, 0:SEG], 1.0 / D))
                k.wait("scalar", tvr)
                k.eop("scalar", lambda e: e.activation(srow[0][:], srow[0][:], AF.Ln))
                tse = k.eop("scalar", lambda e: e.activation(srow[0][:], srow[0][:], AF.Exp,
                                                             scale=0.5))
                k.wait("vector", tse)
                tva = k.eop("vector", lambda e: e.tensor_scalar_add(srow[0][:], srow[0][:], 1e-8))
                k.wait("scalar", tva)
                k.eop("scalar", lambda e: e.activation(srow[0][:], srow[0][:], AF.Ln))
                tse2 = k.eop("scalar", lambda e: e.activation(srow[0][:], srow[0][:], AF.Exp,
                                                              scale=-1.0))
                if debug:
                    k.wait("sync", tse2)
                    k.dma("sync", dbg["d_frow"][:, t0:t0 + SEG], srow[0][:], sem="od")
                k.wait("tensor", tse2, tmr)
                trf = k.op("tensor", lambda e: e.matmul(bank[5][:, 0:SEG], ones_r_f[:],
                                                        srow[0][:], start=True, stop=True))
                tog = None
                k.wait("vector", trf)
                for hc in range(8):
                    k.op("vector", lambda e, hc=hc: e.tensor_mul(
                        e_f3[:], ofm[hc][:], bank[5][:, 0:SEG]))
                    tog = k.op("vector", lambda e, hc=hc: e.scalar_tensor_tensor(
                        ofm[hc][:], e_f3[:], asc_s[:, hc:hc + 1], ut[hc][:],
                        op0=OP.mult, op1=OP.mult))
                if debug:
                    k.xwait("sync")
                    for hc in range(8):
                        k.dma("sync", dbg["d_og"][hc * 128:(hc + 1) * 128, t0:t0 + SEG],
                              ofm[hc][:], sem="od")
                gaborep_b = gaborep_t[:]
                prev_eread = [None, None]
                for tc in range(4):
                    tcur = t0 + tc * 128
                    xbuf = xf32[tc % 2]
                    x2buf = x2st[tc % 2]
                    k.wait("sync", t_xl_prev[tc % 2])
                    txl = k.dma("sync", xbuf[:], x_tm[tcur:tcur + 128, :])
                    tx2 = None
                    for eg in range(2):
                        pi = 2 + eg
                        esl = slice(eg * 512, (eg + 1) * 512)
                        k.wait("tensor", tog, t_x2w[tc % 2], prev_eread[eg])
                        tpo = None
                        for hc in range(8):
                            lhs = ofm[hc][:, tc * 128:(tc + 1) * 128]
                            rhs = wt_o[hc][:, esl]
                            fn = lambda e, lhs=lhs, rhs=rhs, hc=hc, pi=pi: e.matmul(
                                bank[pi][:, 0:512], lhs, rhs, start=(hc == 0), stop=(hc == 7))
                            if hc < 7:
                                k.raw("tensor", fn)
                            else:
                                tpo = k.op("tensor", fn)
                        k.wait("vector", tpo, txl)
                        k.op("vector", lambda e, pi=pi, esl=esl, g=gaborep_b: e.tensor_add(
                            e_f3[:], bank[pi][:, 0:512], g[:, esl]))
                        tx2 = k.op("vector", lambda e, x2buf=x2buf, esl=esl, xbuf=xbuf:
                                   e.tensor_add(x2buf[:, esl], e_f3[:], xbuf[:, esl]))
                        prev_eread[eg] = tx2
                    k.wait("sync", tx2)
                    t1_ = k.dma("sync", x2_dram[tcur:tcur + 128, :], x2buf[:])
                    k.wait("gpsimd", tx2)
                    t2_ = k.dma("gpsimd", x2_btm[tcur:tcur + 128, :], x2buf[:])
                    t_x2w[tc % 2] = t2_
                    t_xl_prev[tc % 2] = tx2
                    k.wait("vector", t1_)
                t_x2_seg = (t1_, t2_)

                tx2f = []
                k.xwait("sync")
                k.wait("sync", *t_x2w)
                for dc in range(8):
                    tx2f.append(k.dma("sync", xhat[dc][:],
                                      x2_btm[t0:t0 + SEG, dc * 128:(dc + 1) * 128],
                                      transpose=True))
                k.xwait("tensor", "vector", "scalar")
                t_h2 = ln_stats_and_norm(
                    tx2f, lambda dc: onepcb[b][:, dc:dc + 1], lambda dc: sbv_s[b][:, dc:dc + 1], b)
                th2w = []
                for dc in range(8):
                    k.wait("sync", t_h2)
                    th2w.append(k.dma("sync", h2fm[dc * 128:(dc + 1) * 128, t0:t0 + SEG],
                                      xhat[dc][:]))
                for tc in range(4):
                    tcur = t0 + tc * 128
                    hbuf = h2tmst[tc % 2]
                    k.wait("sync", th2w[-1], t_h2w[tc % 2])
                    tr = k.dma("sync", hbuf[:], h2fm[:, tcur:tcur + 128], transpose=True)
                    k.wait("sync", tr)
                    tb = k.dma("sync", h2_bounce[tcur:tcur + 128, :], hbuf[:])
                    t_h2w[tc % 2] = tb
                    t_h2bounce_all.append(tb)
                k.wait("tensor", t_h2, t_gwl)
                tlm = None
                for dc in range(8):
                    lhs = gw_s[:, dc * 8:dc * 8 + 8]
                    fn = lambda e, lhs=lhs, dc=dc: e.matmul(
                        bank[2][0:8, 0:SEG], lhs, xhat[dc][:],
                        start=(dc == 0), stop=(dc == 7))
                    if dc < 7:
                        k.raw("tensor", fn)
                    else:
                        tlm = k.op("tensor", fn)
                k.wait("scalar", tlm)
                tle = k.op("scalar", lambda e: e.activation(
                    lgs[:], bank[2][0:8, 0:SEG], AF.Identity, bias=gbc_s[:]))
                k.wait("tensor", tle)
                for tc in range(4):
                    ch = (t0 + tc * 128) // 128
                    src = lgs[:, tc * 128:(tc + 1) * 128]
                    ttr = k.op("tensor", lambda e, src=src: e.transpose(
                        bank[3][0:128, 0:8], src, ident_f[0:8, 0:8]))
                    k.wait("vector", ttr)
                    dstl = logits_tm[:, ch, :]
                    tlv = k.op("vector", lambda e, d=dstl: e.tensor_copy(
                        d, bank[3][0:128, 0:8]))
                    k.wait("tensor", tlv)

        k.barrier()
        k.flush()

    if phases < 2:
        _stub_out(k, nc, out, es)
        k.flush()
        es.close()
        return nc
    t_ag_h2 = k.cc("AllGather", OP.bypass, [h2_bounce.opt()], [h2_all.opt()])

    with ExitStack() as p2:
        sb2 = lambda name, shape, dt: p2.enter_context(nc.sbuf_tensor(name, shape, dt))
        mx = sb2("mx", [128, NCH], F32)
        ex = sb2("ex", [128, NCH * 8], F32)
        iot = sb2("iot", [128, NCH * 8], F32)
        cand = sb2("cand", [128, NCH * 8], F32)
        idxf = sb2("idxf", [128, NCH], F32)
        oh_f = sb2("oh_f", [128, NCH * 8], F32)
        oh_b = sb2("oh_b", [128, NCH * 8], BF16)
        cum_sb = sb2("cum_sb", [128, NCH * 8], F32)
        tpr = sb2("tpr", [128, NCH], F32)
        ttot = sb2("ttot", [1, NCH * 8], F32)
        ttot32 = sb2("ttot32", [32, 8], F32)
        exrow = sb2("exrow", [32, 8], F32)
        offrow = sb2("offrow", [1, 8], F32)
        cmat = sb2("cmat", [8, 8], F32)
        cmatbig = sb2("cmatbig", [128, 8], F32)
        comb = sb2("comb", [32, 8], F32)
        combrow = sb2("combrow", [1, NCH * 8], F32)
        possel = sb2("possel", [128, NCH], F32)
        keep_s = sb2("keep_s", [128, NCH], F32)
        zf32 = sb2("zf32", [128, 428], F32)
        iota_p = sb2("iota_p", [128, 1], F32)
        sc_r = sb2("sc_r", [128, NCH], F32)
        sc_q = sb2("sc_q", [128, NCH], F32)
        sc_j = sb2("sc_j", [128, NCH], F32)
        sc_s = sb2("sc_s", [128, NCH], F32)
        sc_d = sb2("sc_d", [128, 1], F32)
        sc_u = sb2("sc_u", [128, NCH], U32)
        sc_t = sb2("sc_t", [128, NCH], U32)
        sc_d2 = sb2("sc_d2", [128, NCH], F32)
        payall4 = sb2("payall4", [128, NCH, 4], F32)
        tokb_s = sb2("tokb_s", [128, 1], F32)

        lg3 = logits_tm[:]
        i3 = iot[:].rearrange("p (a b) -> p a b", b=8)
        e3 = ex[:].rearrange("p (a b) -> p a b", b=8)
        o3 = oh_f[:].rearrange("p (a b) -> p a b", b=8)
        c3 = cand[:].rearrange("p (a b) -> p a b", b=8)

        tio = k.op("gpsimd", lambda e: e.iota(
            i3, [[0, NCH], [1, 8]], base=0, channel_multiplier=0,
            allow_small_or_imprecise_dtypes=True))
        tmx = k.eop("vector", lambda e: e.tensor_reduce(
            mx[:].unsqueeze(2), lg3, axis=AX.X, op=OP.max))
        mxb = mx[:].unsqueeze(2).broadcast_to([128, NCH, 8])
        k.wait("vector", tmx)
        tex = k.eop("vector", lambda e: e.tensor_tensor(e3, lg3, mxb, op=OP.subtract))
        k.wait("scalar", tex)
        tee = k.eop("scalar", lambda e: e.activation(ex[:], ex[:], AF.Exp))
        k.wait("vector", tee)
        tsum = k.eop("vector", lambda e: e.tensor_reduce(
            tpr[:].unsqueeze(2), e3, axis=AX.X, op=OP.add))
        k.wait("scalar", tsum)
        k.eop("scalar", lambda e: e.activation(tpr[:], tpr[:], AF.Ln))
        ttp = k.eop("scalar", lambda e: e.activation(tpr[:], tpr[:], AF.Exp, scale=-1.0))
        k.wait("vector", tio)
        teq = k.eop("vector", lambda e: e.tensor_tensor(o3, lg3, mxb, op=OP.is_equal))
        k.wait("vector", teq)
        k.eop("vector", lambda e: e.tensor_scalar_add(cand[:], iot[:], -1000.0))
        k.eop("vector", lambda e: e.tensor_mul(cand[:], cand[:], oh_f[:]))
        k.eop("vector", lambda e: e.tensor_scalar_add(cand[:], cand[:], 1000.0))
        tix = k.eop("vector", lambda e: e.tensor_reduce(
            idxf[:].unsqueeze(2), c3, axis=AX.X, op=OP.min))
        ixb = idxf[:].unsqueeze(2).broadcast_to([128, NCH, 8])
        k.wait("vector", tix)
        toh = k.eop("vector", lambda e: e.tensor_tensor(o3, i3, ixb, op=OP.is_equal))
        k.wait("vector", toh)
        tohb = k.eop("vector", lambda e: e.tensor_copy(oh_b[:], oh_f[:]))
        k.wait("tensor", tohb, t_const)
        tcm = None
        for ch in range(NCH):
            sl = slice(ch * 8, ch * 8 + 8)
            fn = lambda e, sl=sl, ch=ch: e.matmul(
                bank[0][:, sl], triu_b[:], oh_b[:, sl],
                start=(ch == 0), stop=True, skip_group_check=True)
            if ch < NCH - 1:
                k.raw("tensor", fn)
            else:
                tcm = k.op("tensor", fn)
        k.wait("vector", tcm)
        tcums = k.eop("vector", lambda e: e.tensor_copy(cum_sb[:], bank[0][:, 0:NCH * 8]))
        k.wait("sync", tcums)
        ttt = k.dma("sync", modrow_dram[0:1, 0:NCH * 8], cum_sb[127:128, :])
        k.wait("sync", ttt)
        trr = k.dma("sync", ttot32[:],
                    modrow_dram[0:1, 0:NCH * 8].rearrange("o (a b) -> (o a) b", b=8))
        k.wait("tensor", trr)
        tmm1 = k.op("tensor", lambda e: e.matmul(
            bank[1][0:32, 0:8], su32_f[:], ttot32[:], start=True, stop=True))
        tmm2 = k.op("tensor", lambda e: e.matmul(
            bank[2][0:1, 0:8], ones32_f[:], ttot32[:], start=True, stop=True))
        k.wait("vector", tmm1, tmm2)
        texr = k.eop("vector", lambda e: e.tensor_copy(exrow[:], bank[1][0:32, 0:8]))
        tctr = k.eop("vector", lambda e: e.tensor_copy(offrow[:], bank[2][0:1, 0:8]))
        k.wait("tensor", tctr)
        tcrepb = k.op("tensor", lambda e: e.matmul(
            bank[3][:, 0:8], ones_r_f[:], offrow[:], start=True, stop=True))
        k.wait("vector", tcrepb)
        tcmatb = k.eop("vector", lambda e: e.tensor_copy(cmatbig[:], bank[3][:, 0:8]))
        k.wait("sync", tcmatb)
        tcin = k.dma("sync", cnt_in, cmatbig[:])
        if debug:
            k.dma("sync", dbg["d_tt32"], ttot32[:], sem="od")
            k.dma("sync", dbg["d_ttot"], modrow_dram[0:1, 0:NCH * 8], sem="od")
        k.wait("gpsimd", tcin)
        t_ar = k.cc("AllGather", OP.bypass, [cnt_in.opt()], [cnt_out.opt()])
        k.wait("sync", t_ar)
        tcl = k.dma("sync", cmat[:], cnt_out.rearrange("(c r) e -> c r e", r=128)[:, 0, :])
        k.wait("tensor", tcl)
        toff = k.op("tensor", lambda e: e.matmul(
            bank[4][0:1, 0:8], cmask_s[:], cmat[:], start=True, stop=True))
        k.wait("vector", toff)
        toffr = k.eop("vector", lambda e: e.tensor_copy(offrow[:], bank[4][0:1, 0:8]))
        k.wait("tensor", toffr)
        torep = k.op("tensor", lambda e: e.matmul(
            bank[5][0:32, 0:8], ones_r_f[:, 0:32], offrow[:], start=True, stop=True))
        k.wait("vector", torep, texr)
        tcomb = k.eop("vector", lambda e: e.tensor_add(comb[:], exrow[:], bank[5][0:32, 0:8]))
        k.wait("sync", tcomb)
        tcw = k.dma("sync", modrow_dram[1:2, 0:NCH * 8].rearrange("o (a b) -> (o a) b", b=8),
                    comb[:])
        k.wait("sync", tcw)
        tcrow = k.dma("sync", combrow[:], modrow_dram[1:2, 0:NCH * 8])
        k.wait("tensor", tcrow)
        tcrep2 = k.op("tensor", lambda e: e.matmul(
            bank[5][:, 0:NCH * 8], ones_r_f[:], combrow[:], start=True, stop=True))
        k.wait("vector", tcrep2, tcums, toh)
        k.eop("vector", lambda e: e.tensor_add(cand[:], cum_sb[:], bank[5][:, 0:NCH * 8]))
        k.eop("vector", lambda e: e.tensor_mul(cand[:], cand[:], oh_f[:]))
        tps = k.eop("vector", lambda e: e.tensor_reduce(
            possel[:].unsqueeze(2), c3, axis=AX.X, op=OP.add))
        k.wait("vector", tps, ttp)
        tkp = k.eop("vector", lambda e: e.tensor_scalar(
            keep_s[:], possel[:], float(CAP), None, op0=OP.is_le))
        k.wait("vector", tkp)
        tgt = k.eop("vector", lambda e: e.tensor_mul(gate_s[:], tpr[:], keep_s[:]))
        k.eop("vector", lambda e: e.tensor_mul(cand[:, 0:NCH], possel[:], keep_s[:]))
        k.eop("vector", lambda e: e.tensor_scalar(
            cand[:, 0:NCH], cand[:, 0:NCH], -1.0, 0.0, op0=OP.add, op1=OP.max))
        k.eop("vector", lambda e: e.scalar_tensor_tensor(
            cand[:, NCH:2 * NCH], idxf[:], float(CAP), cand[:, 0:NCH],
            op0=OP.mult, op1=OP.add))
        k.eop("vector", lambda e: e.tensor_scalar_max(
            cand[:, NCH:2 * NCH], cand[:, NCH:2 * NCH], 0.0))
        tg2 = k.eop("vector", lambda e: e.tensor_scalar_min(
            cand[:, NCH:2 * NCH], cand[:, NCH:2 * NCH], float(NCORES * CAP - 1)))
        k.wait("gpsimd", tg2)
        tidxg = k.dma("gpsimd", idxg_s[:], cand[:, NCH:2 * NCH])
        tz0 = k.op("gpsimd", lambda e: e.memset(zf32[:], 0.0))
        k.wait("sync", tz0)
        tzw = None
        zview = slotmap_in.rearrange("(p c) w -> p (c w)", p=128)
        for off in range(0, 1284, 428):
            w = min(428, 1284 - off)
            tzw = k.dma("sync", zview[:, off:off + w], zf32[:, 0:w])
        tip = k.op("gpsimd", lambda e: e.iota(
            iota_p[:], [[0, 1]], base=0, channel_multiplier=1,
            allow_small_or_imprecise_dtypes=True))
        k.wait("vector", tip, tkp)
        tr1 = k.eop("vector", lambda e: e.tensor_scalar_add(sc_r[:], possel[:], -1.0))
        k.eop("vector", lambda e: e.tensor_scalar_mul(sc_j[:], sc_r[:], 1.0 / 128.0))
        k.eop("vector", lambda e: e.tensor_scalar_add(sc_j[:], sc_j[:], 8388608.0))
        k.eop("vector", lambda e: e.tensor_scalar_add(sc_j[:], sc_j[:], -8388608.0))
        k.eop("vector", lambda e: e.scalar_tensor_tensor(
            sc_q[:], sc_j[:], -128.0, sc_r[:], op0=OP.mult, op1=OP.add))
        k.eop("vector", lambda e: e.tensor_scalar(
            sc_d2[:], sc_q[:], 0.0, None, op0=OP.is_lt))
        k.eop("vector", lambda e: e.tensor_sub(sc_j[:], sc_j[:], sc_d2[:]))
        k.eop("vector", lambda e: e.scalar_tensor_tensor(
            sc_q[:], sc_d2[:], 128.0, sc_q[:], op0=OP.mult, op1=OP.add))
        k.eop("vector", lambda e: e.scalar_tensor_tensor(
            sc_s[:], sc_q[:], 40.0, sc_j[:], op0=OP.mult, op1=OP.add))
        k.eop("vector", lambda e: e.scalar_tensor_tensor(
            sc_s[:], idxf[:], float(CAP), sc_s[:], op0=OP.mult, op1=OP.add))
        k.eop("vector", lambda e: e.tensor_scalar_add(sc_d[:], iota_p[:], float(NCORES * CAP)))
        k.eop("vector", lambda e: e.tensor_sub(sc_s[:], sc_s[:], sc_d[:].broadcast_to([128, NCH])))
        k.eop("vector", lambda e: e.tensor_mul(sc_s[:], sc_s[:], keep_s[:]))
        k.eop("vector", lambda e: e.tensor_add(
            sc_s[:], sc_s[:], sc_d[:].broadcast_to([128, NCH])))
        k.eop("vector", lambda e: e.tensor_scalar_max(sc_s[:], sc_s[:], 0.0))
        tslot = k.eop("vector", lambda e: e.tensor_scalar_min(
            sc_s[:], sc_s[:], float(NCORES * CAP + 127)))
        if debug:
            k.wait("sync", tslot)
            k.dma("sync", dbg["d_scdbg"][:, 0:NCH], sc_r[:], sem="od")
            k.dma("sync", dbg["d_scdbg"][:, NCH:2 * NCH], sc_j[:], sem="od")
            k.dma("sync", dbg["d_scdbg"][:, 2 * NCH:3 * NCH], sc_q[:], sem="od")
            k.dma("sync", dbg["d_scdbg"][:, 3 * NCH:4 * NCH], sc_s[:], sem="od")
        k.wait("gpsimd", tslot)
        tsc_u = k.dma("gpsimd", sc_u[:], sc_s[:])
        tpay0 = k.op("gpsimd", lambda e: e.iota(
            payall4[:], [[128, NCH], [0, 4]], base=1, channel_multiplier=1,
            allow_small_or_imprecise_dtypes=True))
        tb_ld = k.dma("sync", tokb_s[:], tokbase)
        k.wait("vector", tpay0, tb_ld)
        tpay = k.eop("vector", lambda e: e.tensor_scalar_add(
            payall4[:].rearrange("p a b -> p (a b)"),
            payall4[:].rearrange("p a b -> p (a b)"), tokb_s[:]))
        k.wait("gpsimd", tsc_u, tzw, tpay)
        tscat = None
        for ch in range(NCH):
            tscat = k.dma16(lambda e, ch=ch: e.indirect_dma_start(
                out=slotmap_in, out_offset=IndirectOffsetOnAxis(sc_u[:, ch:ch + 1], 0),
                in_=payall4[:, ch, :], in_offset=None))
        k.wait("gpsimd", tscat)
        t_ar_slot = k.cc("AllReduce", OP.add, [slotmap_in.opt()], [slotmap_out.opt()])

        if debug:
            k.wait("sync", tgt, tidxg, tmx, tix, tps, tkp, ttp, tcums)
            k.dma("sync", dbg["d_mx"], mx[:], sem="od")
            k.dma("sync", dbg["d_idxf"], idxf[:], sem="od")
            k.dma("sync", dbg["d_possel"], possel[:], sem="od")
            k.dma("sync", dbg["d_keep"], keep_s[:], sem="od")
            k.dma("sync", dbg["d_tpr"], tpr[:], sem="od")
            k.dma("sync", dbg["d_cum"], cum_sb[:], sem="od")
            k.dma("sync", dbg["d_gate"], gate_s[:], sem="od")
            k.wait("sync", tscat)
            k.dma("sync", dbg["d_scu"], sc_u[:], sem="od")
            k.dma("sync", dbg["d_idxg"], cand[:, NCH:2 * NCH], sem="od")
            k.dma("sync", dbg["d_cnt"], cnt_out.rearrange("(c r) e -> c r e", r=128)[:, 0, :], sem="od")
            k.dma("sync", dbg["d_logits"], logits_tm[:].rearrange("p a b -> p (a b)"),
                  sem="od")
            k.dma("sync", dbg["d_x2"], x2_dram, sem="od")
            k.dma("sync", dbg["d_h2"], h2_bounce, sem="od")

        k.barrier()
        k.flush()

    if phases < 3:
        _stub_out(k, nc, out, es)
        k.flush()
        es.close()
        return nc
    with ExitStack() as p3:
        sb3 = lambda name, shape, dt: p3.enter_context(nc.sbuf_tensor(name, shape, dt))
        slot4 = sb3("slot4", [128, CAP // 128 * 4], F32)
        srowf = sb3("srowf", [128, 1], F32)
        srowu = sb3("srowu", [128, 1], U32)
        slot4u = sb3("slot4u", [128, CAP // 128 * 4], U32)
        btm = [sb3(f"btm{i}", [128, 1024], BF16) for i in range(4)]
        w1t = [sb3(f"w1t{i}", [128, FF], BF16) for i in range(8)]
        w2t = [sb3(f"w2t{i}", [128, D], BF16) for i in range(32)]
        bufg = [sb3(f"bufg{i}", [128, 8, FFG], BF16) for i in range(2)]
        y1c = [sb3(f"y1c{i}", [128, FFG], BF16) for i in range(2)]
        eost = [sb3(f"eost{i}", [128, D], F32) for i in range(2)]
        b2rep = sb3("b2rep", [128, D], F32)

        t_ld = k.dma("sync", rowsrc[:], b2_row)
        replicate_row(rowsrc[:], b2rep[:], [t_ld], 6)

        k.wait("sync", t_w1, t_w2)
        t_wf = None
        for i in range(8):
            t_wf = k.dma("sync", w1t[i][:], w1_b[i * 128:(i + 1) * 128, :])
        for i in range(32):
            t_wf = k.dma("sync", w2t[i][:], w2_b[i * 128:(i + 1) * 128, :])

        tsr = k.dma("sync", srowf[:], slotrow128)
        k.wait("gpsimd", tsr)
        tsru = k.dma("gpsimd", srowu[:], srowf[:])
        k.wait("gpsimd", tsru, t_ar_slot)
        tsl = k.dma16(lambda e: e.indirect_dma_start(
            out=slot4[:], out_offset=None,
            in_=slotmap_out, in_offset=IndirectOffsetOnAxis(srowu[:], 0)))
        k.wait("vector", tsl)
        k.eop("vector", lambda e: e.tensor_scalar(
            slot4[:], slot4[:], -1.0, 0.0, op0=OP.add, op1=OP.max))
        tsi0 = k.eop("vector", lambda e: e.tensor_scalar_min(
            slot4[:], slot4[:], float(NCORES * TOK - 1)))
        k.wait("gpsimd", tsi0)
        tsi = k.dma("gpsimd", slot4u[:], slot4[:])

        k.wait("gpsimd", t_ag_h2, tsi)

        gather_state = {"tgi": [None, None], "tw": {}, "tlast": [None, None]}

        def gather(g, gb, waitmm=None):
            toks = []
            for half in range(2):
                col = 2 * g + half
                iu = slot4u[:, 4 * col:4 * col + 1]
                k.wait("gpsimd", tsi, gather_state["tw"].get((gb, half)))
                tgi = k.dma16(lambda e, iu=iu, gb=gb, half=half: e.indirect_dma_start(
                    out=btm[2 * gb + half][:], out_offset=None,
                    in_=h2_all, in_offset=IndirectOffsetOnAxis(iu[:], 0)))
                gather_state["tgi"][half] = tgi
                k.wait("sync", tgi, gather_state["tlast"][gb])
                tw = k.dma("sync", ffn_tm[gb][half * 128:(half + 1) * 128, :],
                           btm[2 * gb + half][:])
                gather_state["tw"][(gb, half)] = tw
                toks.append(tw)
            k.wait("sync", *toks)
            if waitmm is not None:
                k.wait("sync", waitmm)
            tlast = None
            for dc in range(8):
                tlast = k.dma("sync", bufg[gb][:, dc, :],
                              ffn_tm[gb][:, dc * 128:(dc + 1) * 128], transpose=True)
            gather_state["tlast"][gb] = tlast
            if debug and g == 0:
                k.dma("sync", dbg["d_ffng0"], ffn_tm[0], sem="od")
            return tlast

        tga = [gather(0, 0), None]
        t_eout_w = [None, None]
        ts1_prev = [None, None]
        for g in range(NFFG):
            gb = g % 2
            k.wait("tensor", tga[gb], t_wf)
            last_mm1 = None
            for fc in range(32):
                pb = fc % 2
                k.wait("tensor", ts1_prev[pb])
                tmm1 = None
                for dc in range(8):
                    lhs = w1t[dc][:, fc * 128:(fc + 1) * 128]
                    rhs = bufg[gb][:, dc, :]
                    fn = lambda e, lhs=lhs, rhs=rhs, dc=dc, pb=pb: e.matmul(
                        bank[pb][:, 0:FFG], lhs, rhs, start=(dc == 0), stop=(dc == 7))
                    if dc < 7:
                        k.raw("tensor", fn)
                    else:
                        tmm1 = k.op("tensor", fn)
                k.wait("scalar", tmm1)
                b1ap = b1_s[:, fc:fc + 1]
                ts1 = k.op("scalar", lambda e, pb=pb, b1ap=b1ap: e.activation(
                    y1c[pb][:], bank[pb][:, 0:FFG], AF.Relu, bias=b1ap))
                ts1_prev[pb] = ts1
                k.wait("tensor", ts1)
                for tc in range(2):
                    lhs = y1c[pb][:, tc * 128:(tc + 1) * 128]
                    for dg in range(2):
                        rhs = w2t[fc][:, dg * 512:(dg + 1) * 512]
                        bi = 2 + tc * 2 + dg
                        fn = lambda e, lhs=lhs, rhs=rhs, bi=bi, fc=fc: e.matmul(
                            bank[bi][:, 0:512], lhs, rhs,
                            start=(fc == 0), stop=(fc == 31))
                        if fc == 31 and tc == 1 and dg == 1:
                            tmm2 = k.op("tensor", fn)
                        else:
                            k.raw("tensor", fn)
                last_mm1 = tmm1
            if g + 1 < NFFG:
                tga[(g + 1) % 2] = gather(g + 1, (g + 1) % 2, waitmm=last_mm1)
            for tc in range(2):
                ebuf = eost[tc]
                k.wait("vector", tmm2, t_eout_w[tc])
                tve = None
                for dg in range(2):
                    bi = 2 + tc * 2 + dg
                    dsl = slice(dg * 512, (dg + 1) * 512)
                    tve = k.op("vector", lambda e, bi=bi, dsl=dsl, ebuf=ebuf: e.tensor_add(
                        ebuf[:, dsl], bank[bi][:, 0:512], b2rep[:, dsl]))
                k.wait("sync", tve)
                r0 = g * FFG + tc * 128
                tst = k.dma("sync", eout_bounce[r0:r0 + 128, :], ebuf[:])
                t_eout_w[tc] = tst
                k.wait("tensor", tve)
        k.wait("gpsimd", *t_eout_w)
        t_ag_eo = k.cc("AllGather", OP.bypass, [eout_bounce.opt()], [eout_all.opt()])

        if debug:
            k.wait("sync", *t_eout_w)
            k.dma("sync", dbg["d_eout"], eout_bounce, sem="od")
            k.dma("sync", dbg["d_slotmap"], slotmap_out, sem="od")
            k.wait("sync", tsi)
            k.dma("sync", dbg["d_slot4"], slot4[:], sem="od")

        k.barrier()
        k.flush()

    if phases < 4:
        _stub_out(k, nc, out, es)
        k.flush()
        es.close()
        return nc
    with ExitStack() as p4:
        sb4 = lambda name, shape, dt: p4.enter_context(nc.sbuf_tensor(name, shape, dt))
        egst = [sb4(f"egst{i}", [128, D], F32) for i in range(2)]
        x2l = [sb4(f"x2l{i}", [128, D], F32) for i in range(2)]
        otile = [sb4(f"otile{i}", [128, D], F32) for i in range(2)]
        ftmp = sb4("ftmp", [128, D], F32)
        gbrep_t = [sb4(f"gbrep_t{i}", [128, D], F32) for i in range(2)]

        for b in range(2):
            t_ld = k.dma("sync", rowsrc[:], modrow_dram[12 + b:13 + b, :])
            t_r = replicate_row(rowsrc[:], gbrep_t[b][:], [t_ld], 0)
            k.wait("sync", t_r)

        k.wait("gpsimd", t_ag_eo)
        k.wait("sync", t_ag_eo)
        tprev = [None, None]
        tso_prev = [None, None]
        tout_all = []
        for ch in range(NCH):
            sl = ch % 2
            b = ch // 16
            idxcol = idxg_s[:, ch:ch + 1]
            k.wait("gpsimd", tprev[sl])
            tg_ = k.dma16(lambda e, idxcol=idxcol, sl=sl: e.indirect_dma_start(
                out=egst[sl][:], out_offset=None,
                in_=eout_all, in_offset=IndirectOffsetOnAxis(idxcol, 0)))
            k.wait("sync", tprev[sl])
            tx_ = k.dma("sync", x2l[sl][:], x2_dram[ch * 128:(ch + 1) * 128, :])
            gate_ap = gate_s[:, ch:ch + 1]
            gb_ap = gbrep_t[b][:]
            k.wait("vector", tg_, tx_, tso_prev[sl])
            k.op("vector", lambda e, sl=sl, gate_ap=gate_ap, gb_ap=gb_ap:
                 e.scalar_tensor_tensor(ftmp[:], egst[sl][:], gate_ap, gb_ap,
                                        op0=OP.mult, op1=OP.mult))
            tvo = k.op("vector", lambda e, sl=sl: e.tensor_add(
                otile[sl][:], ftmp[:], x2l[sl][:]))
            tprev[sl] = tvo
            k.wait("sync", tvo)
            tso = k.dma("sync", out[ch * 128:(ch + 1) * 128, :], otile[sl][:], sem="od")
            tso_prev[sl] = tso
            tout_all.append(tso)
        k.wait("sync", tout_all[-1])
        k.barrier()
        k.flush()

    es.close()
    return nc


_CACHE = {}


def _get_nc(debug=False):
    if debug not in _CACHE:
        _CACHE[debug] = build_nc(debug)
    return _CACHE[debug]


def kernel(**inputs):
    return _run(inputs, debug=False)[0]


def _run(inputs, debug=False):
    f32 = np.float32
    x = np.asarray(inputs["L0_attn_x"], f32)
    cvec = np.asarray(inputs["L0_c"], f32)
    Wq = np.asarray(inputs["L0_attn_Wq"], f32); bq = np.asarray(inputs["L0_attn_bq"], f32)
    Wk = np.asarray(inputs["L0_attn_Wk"], f32); bk = np.asarray(inputs["L0_attn_bk"], f32)
    Wv = np.asarray(inputs["L0_attn_Wv"], f32); bv = np.asarray(inputs["L0_attn_bv"], f32)
    Wu = np.asarray(inputs["L0_attn_Wu"], f32); bu = np.asarray(inputs["L0_attn_bu"], f32)
    Wo = np.asarray(inputs["L0_attn_Wo"], f32); bo = np.asarray(inputs["L0_attn_bo"], f32)
    ascale = np.asarray(inputs["L0_attn_scale"], f32)
    gw = np.asarray(inputs["L0_ffn_gate_w"], f32); gb_ = np.asarray(inputs["L0_ffn_gate_b"], f32)
    w1 = np.asarray(inputs["L0_ffn_w1"], f32); b1 = np.asarray(inputs["L0_ffn_b1"], f32)
    w2 = np.asarray(inputs["L0_ffn_w2"], f32); b2 = np.asarray(inputs["L0_ffn_b2"], f32)
    wmod_ = np.asarray(inputs["L0_wmod"], f32)

    C = np.ascontiguousarray

    def fmvec(v):
        return C(v.reshape(8, 128).T)

    shared = {
        "wqT": C(Wq.T), "wkT": C(Wk.T), "wvT": C(Wv.T), "wuT": C(Wu.T),
        "woT": C(Wo.T), "wmod": C(wmod_),
        "gwT": C(gw.T.reshape(8, 128, 8).transpose(1, 0, 2).reshape(128, 64)),
        "bq_fm": fmvec(bq), "bk_fm": fmvec(bk), "bu_fm": fmvec(bu),
        "bk_row": bk.reshape(1, D), "bv_row": bv.reshape(1, D),
        "b2_row": None, "bo_row": bo.reshape(1, D),
        "gb_col": gb_.reshape(8, 1),
        "ascale_fm": fmvec(ascale),
    }
    in_maps = []
    for c in range(NCORES):
        xs = x[2 * c:2 * c + 2].reshape(TOK, D)
        m = dict(shared)
        m["x_tm"] = C(xs)
        m["x_fm"] = C(xs.T)
        cc_ = cvec[2 * c:2 * c + 2]
        m["cT"] = C(cc_.T.reshape(8, 128, 2).transpose(1, 0, 2).reshape(128, 16))
        m["w1e"] = C(w1[c]); m["w2e"] = C(w2[c])
        m["b1_fm"] = C(b1[c].reshape(32, 128).T)
        m["b2_row"] = b2[c].reshape(1, D)
        cm = np.zeros((8, 1), f32); cm[:c, 0] = 1.0
        m["core_mask"] = cm
        co = np.zeros((8, 1), f32); co[c, 0] = 1.0
        m["core_oh"] = co
        m["slotrow128"] = (float(c * CAP) + np.arange(128, dtype=f32)[:, None] * 40.0)
        m["tokbase"] = np.full((128, 1), float(TOK * c), f32)
        in_maps.append(m)

    nc = _get_nc(debug)
    res = run_bass_kernel_spmd(nc, in_maps, core_ids=list(range(NCORES)))
    outs = np.concatenate([res.results[c]["out"].reshape(2, N, D)
                           for c in range(NCORES)], axis=0)
    return outs, res


# version 03
# speedup vs baseline: 1.0694x; 1.0643x over previous
import numpy as np
from contextlib import ExitStack

import concourse.bass as bass
import concourse.mybir as mybir
from concourse.bass import IndirectOffsetOnAxis
from concourse import library_config as libcfg
from concourse.bass_utils import run_bass_kernel_spmd

F32 = mybir.dt.float32
BF16 = mybir.dt.bfloat16
U32 = mybir.dt.uint32
I16 = mybir.dt.int16
AF = mybir.ActivationFunctionType
OP = mybir.AluOpType
AX = mybir.AxisListType

NCORES = 8
N = 2048
D = 1024
H = 8
TOK = 2 * N
NCH = TOK // 128
SEG = 512
NSEG_B = N // SEG
FF = 4096
CAP = 5120
FFG = 256
NFFG = CAP // FFG


def _stub_out(k, nc, out, es):
    zt = es.enter_context(nc.sbuf_tensor("stub_z", [128, 512], mybir.dt.float32))
    tz = k.op("gpsimd", lambda e: e.memset(zt[:], 0.0))
    k.wait("sync", tz)
    last = None
    for r in range(TOK // 128):
        for cg in range(2):
            last = k.dma("sync", out[r * 128:(r + 1) * 128, cg * 512:(cg + 1) * 512],
                         zt[:], sem="od")
    k.wait("sync", last)
    k.barrier()


class KB:
    ENGINES = ("sync", "tensor", "vector", "scalar", "gpsimd")

    def __init__(self, nc, es, safe=False):
        self.nc = nc
        self.es = es
        self.safe = safe
        self.items = []
        self.semh = {}
        self.semv = {}
        for s in ("ld", "gd", "od", "cc", "t", "v", "s", "g", "ph"):
            self.semh[s] = es.enter_context(nc.semaphore(s))
            self.semv[s] = 0
        self.block = es.enter_context(nc.Block())

    def on(self, eng):
        def deco(fn):
            self.items.append((eng, fn))
        return deco

    def tok(self, sem):
        return (sem, self.semv[sem])

    def dma(self, eng, out, in_, sem=None, transpose=False, slow=False):
        sem = sem or ("gd" if eng == "gpsimd" else "ld")
        self.semv[sem] += 16
        h, v = self.semh[sem], self.semv[sem]

        @self.on(eng)
        def _(e):
            if transpose:
                e.dma_start_transpose(out, in_).then_inc(h, 16)
            elif slow:
                e.dma_start(out=out, in_=in_,
                            allow_slow_non_contiguous=True).then_inc(h, 16)
            else:
                e.dma_start(out=out, in_=in_).then_inc(h, 16)
        return (sem, v)

    def dma16(self, fn):
        self.semv["gd"] += 16
        h, v = self.semh["gd"], self.semv["gd"]

        @self.on("gpsimd")
        def _(e):
            fn(e).then_inc(h, 16)
        return ("gd", v)

    def op(self, eng, fn, inc=None):
        if inc is None:
            inc = {"tensor": "t", "vector": "v", "scalar": "s", "gpsimd": "g",
                   "sync": "g"}[eng]
        self.semv[inc] += 1
        h, v = self.semh[inc], self.semv[inc]
        safe_wait = (h, v - 1) if self.safe else None

        @self.on(eng)
        def _(e):
            if safe_wait is not None and safe_wait[1] > 0:
                e.wait_ge(safe_wait[0], safe_wait[1])
            fn(e).then_inc(h, 1)
        return (inc, v)

    def eop(self, eng, fn, inc=None):
        incs = {"tensor": "t", "vector": "v", "scalar": "s", "gpsimd": "g",
                "sync": "g"}[eng]
        self.wait(eng, (incs, self.semv[incs]))
        return self.op(eng, fn, inc=inc)

    def raw(self, eng, fn):
        if self.safe:
            self.op(eng, fn)
            return

        @self.on(eng)
        def _(e):
            fn(e)

    DMA_SEMS = ("ld", "gd", "od", "cc")

    def wait(self, eng, *tokens):
        toks = []
        seen = {}
        for t in tokens:
            if t is None or t[1] <= 0:
                continue
            s, v = t
            if s in self.DMA_SEMS:
                v = self.semv[s]
            seen[s] = max(seen.get(s, 0), v)
        toks = list(seen.items())
        if not toks:
            return

        @self.on(eng)
        def _(e):
            for s, v in toks:
                e.wait_ge(self.semh[s], v)

    def cc(self, kind, op, ins, outs):
        self.wait("gpsimd", ("cc", self.semv["cc"]))
        self.semv["cc"] += 1
        h, v = self.semh["cc"], self.semv["cc"]

        @self.on("gpsimd")
        def _(e):
            e.collective_compute(kind, op, replica_groups=[list(range(NCORES))],
                                 ins=ins, outs=outs).then_inc(h, 1)
        return ("cc", v)

    def xwait(self, *engines):
        toks = [self.tok(s) for s in ("t", "v", "s", "g", "ld", "gd")]
        for eng in engines:
            self.wait(eng, *toks)

    def barrier(self):
        self.semv["ph"] += len(self.ENGINES)
        h, v = self.semh["ph"], self.semv["ph"]
        for eng in self.ENGINES:
            @self.on(eng)
            def _(e):
                e.sem_inc(h, 1)
                e.wait_ge(h, v)

    def flush(self):
        items = self.items
        self.items = []
        for eng_name, deco in (("sync", self.block.sync), ("tensor", self.block.tensor),
                               ("vector", self.block.vector), ("scalar", self.block.scalar),
                               ("gpsimd", self.block.gpsimd)):
            sub = [fn for e, fn in items if e == eng_name]
            if not sub:
                continue

            def make(sub):
                def body(engine):
                    for fn in sub:
                        fn(engine)
                return body
            deco(make(sub))


def build_nc(debug=False, phases=4, p0sub=99, safe=False):
    nc = bass.Bass("TRN2", target_bir_lowering=False, debug=False,
                   detect_race_conditions=safe)

    def inp(name, shape, dt=F32):
        return nc.dram_tensor(name, shape, dt, kind="ExternalInput").ap()

    def idram(name, shape, dt):
        return nc.dram_tensor(name, shape, dt, kind="Internal").ap()

    x_tm = inp("x_tm", [TOK, D])
    x_fm = inp("x_fm", [D, TOK])
    cT = inp("cT", [128, 16])
    wqT = inp("wqT", [D, D]); wkT = inp("wkT", [D, D])
    wvT = inp("wvT", [D, D]); wuT = inp("wuT", [D, D])
    woT = inp("woT", [D, D])
    wmod = inp("wmod", [D, 6 * D])
    gwT = inp("gwT", [128, 64])
    w1e = inp("w1e", [D, FF]); w2e = inp("w2e", [FF, D])
    bq_fm = inp("bq_fm", [128, 8]); bk_fm = inp("bk_fm", [128, 8])
    bu_fm = inp("bu_fm", [128, 8])
    b1_fm = inp("b1_fm", [128, 32])
    bk_row = inp("bk_row", [1, D]); bv_row = inp("bv_row", [1, D])
    b2_row = inp("b2_row", [1, D]); bo_row = inp("bo_row", [1, D])
    gb_col = inp("gb_col", [8, 1])
    ascale_fm = inp("ascale_fm", [128, 8])
    core_mask = inp("core_mask", [8, 1])
    core_oh = inp("core_oh", [8, 1])
    slotrow128 = inp("slotrow128", [128, 1])
    tokbase = inp("tokbase", [128, 1])

    wqT_b = idram("wqT_b", [D, D], BF16); wkT_b = idram("wkT_b", [D, D], BF16)
    wvT_b = idram("wvT_b", [D, D], BF16); wuT_b = idram("wuT_b", [D, D], BF16)
    woT_b = idram("woT_b", [D, D], BF16)
    wmod_b = idram("wmod_b", [D, 6 * D], BF16)
    gwT_b = idram("gwT_b", [128, 64], BF16)
    w1_b = idram("w1_b", [D, FF], BF16); w2_b = idram("w2_b", [FF, D], BF16)
    xfm_b = idram("xfm_b", [D, TOK], BF16)
    x2_dram = idram("x2_dram", [TOK, D], F32)
    x2_btm = idram("x2_btm", [TOK, D], BF16)
    h2fm = idram("h2fm", [D, TOK], BF16)
    h2_bounce = idram("h2_bounce", [TOK, D], BF16)
    h2_all = idram("h2_all", [NCORES * TOK, D], BF16)
    SLOTS = NCORES * CAP + 128
    slotmap_in = idram("slotmap_in", [SLOTS, 4], F32)
    slotmap_out = idram("slotmap_out", [SLOTS, 4], F32)
    cnt_in = idram("cnt_in", [128, 8], F32)
    cnt_out = idram("cnt_out", [1024, 8], F32)
    eout_bounce = idram("eout_bounce", [CAP, D], BF16)
    modrow_dram = idram("modrow_dram", [16, D], F32)
    ffn_tm = [idram(f"ffn_tm{i}", [FFG, D], BF16) for i in range(2)]
    eout_all = idram("eout_all", [NCORES * CAP, D], BF16)
    out = nc.dram_tensor("out", [TOK, D], F32, kind="ExternalOutput").ap()
    dbg = {}
    if debug:
        for nm, shape, dt in (("d_x2", [TOK, D], F32), ("d_h2", [TOK, D], BF16),
                              ("d_gate", [128, NCH], F32), ("d_idxg", [128, NCH], F32),
                              ("d_eout", [CAP, D], BF16), ("d_cnt", [8, 8], F32),
                              ("d_slotmap", [NCORES * CAP + 128, 4], F32),
                              ("d_oraw", [D, TOK], BF16), ("d_frow", [1, TOK], F32),
                              ("d_qt", [D, TOK], BF16), ("d_kt", [D, TOK], BF16),
                              ("d_vtm", [TOK, D], BF16), ("d_ut", [D, TOK], BF16),
                              ("d_og", [D, TOK], BF16), ("d_wto", [D, D], BF16),
                              ("d_gabo", [128, D], F32), ("d_mx", [128, NCH], F32),
                              ("d_idxf", [128, NCH], F32), ("d_possel", [128, NCH], F32),
                              ("d_keep", [128, NCH], F32), ("d_tpr", [128, NCH], F32),
                              ("d_cum", [128, NCH * 8], F32), ("d_comb", [128, NCH * 8], F32),
                              ("d_scu", [128, NCH], U32), ("d_xh", [D, TOK], BF16),
                              ("d_ab", [2, TOK], F32), ("d_cmat", [8, 8], F32),
                              ("d_tt32", [32, 8], F32), ("d_ttot", [1, NCH * 8], F32),
                              ("d_mv", [128, 64], F32), ("d_scdbg", [128, NCH * 4], F32),
                              ("d_ffng0", [FFG, D], BF16), ("d_slot4", [128, CAP // 128 * 4], F32),
                              ("d_logits", [128, NCH * 8], F32),
                              ("d_csil", [128, 16], F32),
                              ("d_modsb", [2, 6 * D], F32)):
            dbg[nm] = nc.dram_tensor(nm, shape, dt, kind="ExternalOutput").ap()

    es = ExitStack()
    k = KB(nc, es, safe=safe)
    sbg = lambda name, shape, dt: es.enter_context(nc.sbuf_tensor(name, shape, dt))
    bank = [es.enter_context(nc.psum_tensor(f"bank{i}", [128, 512], F32))
            for i in range(8)]

    ones_r_f = sbg("ones_r_f", [1, 128], F32)
    ones_s_b = sbg("ones_s_b", [128, 1], BF16)
    zeros_b = sbg("zeros_b", [128, 512], BF16)
    triu_f = sbg("triu_f", [128, 128], F32)
    triu_b = sbg("triu_b", [128, 128], BF16)
    su32_f = sbg("su32_f", [32, 32], F32)
    ident_f = sbg("ident_f", [128, 128], F32)
    ident_b = sbg("ident_b", [128, 128], BF16)
    ones32_f = sbg("ones32_f", [32, 1], F32)
    ones_1x1 = sbg("ones_1x1", [1, 1], F32)
    bq_s = sbg("bq_s", [128, 8], F32); bk_s = sbg("bk_s", [128, 8], F32)
    bu_s = sbg("bu_s", [128, 8], F32); b1_s = sbg("b1_s", [128, 32], F32)
    asc_s = sbg("asc_s", [128, 8], F32)
    gbc_s = sbg("gbc_s", [8, 1], F32)
    cmask_s = sbg("cmask_s", [8, 1], F32); coh_s = sbg("coh_s", [8, 1], F32)
    onepca = [sbg(f"onepca{b}", [128, 8], F32) for b in range(2)]
    sa_s = [sbg(f"sa_s{b}", [128, 8], F32) for b in range(2)]
    onepcb = [sbg(f"onepcb{b}", [128, 8], F32) for b in range(2)]
    sbv_s = [sbg(f"sbv_s{b}", [128, 8], F32) for b in range(2)]
    logits_tm = sbg("logits_tm", [128, NCH, 8], F32)
    gate_s = sbg("gate_s", [128, NCH], F32)
    idxg_s = sbg("idxg_s", [128, NCH], U32)

    t_wq = k.dma("gpsimd", wqT_b, wqT); t_wk = k.dma("gpsimd", wkT_b, wkT)
    t_wv = k.dma("gpsimd", wvT_b, wvT); t_wu = k.dma("gpsimd", wuT_b, wuT)
    t_wo = k.dma("gpsimd", woT_b, woT)
    t_wm = k.dma("gpsimd", wmod_b, wmod)
    t_gw = k.dma("gpsimd", gwT_b, gwT)
    t_xf = k.dma("gpsimd", xfm_b, x_fm)
    t_w1 = k.dma("gpsimd", w1_b, w1e)
    t_w2 = k.dma("gpsimd", w2_b, w2e)

    if p0sub < 1:
        _stub_out(k, nc, out, es)
        k.flush()
        es.close()
        return nc
    t_small = None
    for dst, src in ((bq_s[:], bq_fm), (bk_s[:], bk_fm), (bu_s[:], bu_fm),
                     (b1_s[:], b1_fm), (asc_s[:], ascale_fm), (gbc_s[:], gb_col),
                     (cmask_s[:], core_mask), (coh_s[:], core_oh)):
        t_small = k.dma("sync", dst, src)

    for t_ in (ones_r_f, ones_s_b, zeros_b, ones32_f, ones_1x1):
        k.op("gpsimd", lambda e, t_=t_: e.memset(t_[:], 1.0 if t_ is not zeros_b else 0.0))
    tg0 = k.op("gpsimd", lambda e: e.memset(triu_f[:], 1.0))
    k.wait("gpsimd", tg0)
    k.op("gpsimd", lambda e: e.affine_select(triu_f[:], triu_f[:], [[1, 128]],
                                             compare_op=OP.is_ge, fill=0.0, base=0,
                                             channel_multiplier=-1))
    tg0 = k.op("gpsimd", lambda e: e.memset(triu_b[:], 1.0))
    k.wait("gpsimd", tg0)
    k.op("gpsimd", lambda e: e.affine_select(triu_b[:], triu_b[:], [[1, 128]],
                                             compare_op=OP.is_ge, fill=0.0, base=0,
                                             channel_multiplier=-1))
    tg0 = k.op("gpsimd", lambda e: e.memset(su32_f[:], 1.0))
    k.wait("gpsimd", tg0)
    k.op("gpsimd", lambda e: e.affine_select(su32_f[:], su32_f[:], [[1, 32]],
                                             compare_op=OP.is_ge, fill=0.0, base=-1,
                                             channel_multiplier=-1))
    tg0 = k.op("gpsimd", lambda e: e.memset(ident_f[:], 1.0))
    k.wait("gpsimd", tg0)
    k.op("gpsimd", lambda e: e.affine_select(ident_f[:], ident_f[:], [[1, 128]],
                                             compare_op=OP.is_equal, fill=0.0, base=0,
                                             channel_multiplier=-1))
    tg0 = k.op("gpsimd", lambda e: e.memset(ident_b[:], 1.0))
    k.wait("gpsimd", tg0)
    t_const = k.op("gpsimd", lambda e: e.affine_select(ident_b[:], ident_b[:], [[1, 128]],
                                                       compare_op=OP.is_equal, fill=0.0,
                                                       base=0, channel_multiplier=-1))

    if p0sub < 2:
        _stub_out(k, nc, out, es)
        k.flush()
        es.close()
        return nc
    rowsrc = sbg("rowsrc", [1, D], F32)

    def replicate_row(row_ap, dst_sb, wait_toks, bank_i):
        last = None
        for g in range(D // 512):
            dslice = dst_sb[:, g * 512:(g + 1) * 512]
            rslice = row_ap[:, g * 512:(g + 1) * 512]
            k.wait("tensor", *wait_toks, t_const)
            tmm = k.op("tensor", lambda e, rs=rslice, bi=bank_i: e.matmul(
                bank[bi][:, 0:512], ones_r_f[:], rs, start=True, stop=True))
            k.wait("vector", tmm)
            last = k.op("vector", lambda e, ds=dslice, bi=bank_i: e.tensor_copy(
                ds, bank[bi][:, 0:512]))
            k.wait("tensor", last)
        return last

    if p0sub < 3:
        _stub_out(k, nc, out, es)
        k.flush()
        es.close()
        return nc
    csil = sbg("csil", [128, 16], F32)
    cT_s = sbg("cT_s", [128, 16], F32)
    t_c = k.dma("sync", cT_s[:], cT)
    k.wait("scalar", t_c)
    t_cs = k.op("scalar", lambda e: e.activation(csil[:], cT_s[:], AF.Silu))

    if p0sub < 4:
        _stub_out(k, nc, out, es)
        k.flush()
        es.close()
        return nc
    with ExitStack() as p0es:
        wmt = [p0es.enter_context(nc.sbuf_tensor(f"wmt{i}", [128, 6 * D], F32))
               for i in range(2)]
        mod_sb = p0es.enter_context(nc.sbuf_tensor("mod_sb", [2, 6 * D], F32))
        mrow_sb = p0es.enter_context(nc.sbuf_tensor("mrow_sb", [1, 6 * D], F32))
        rowtmp = p0es.enter_context(nc.sbuf_tensor("rowtmp", [1, D], F32))
        rowtmp2 = p0es.enter_context(nc.sbuf_tensor("rowtmp2", [1, D], F32))
        browtmp = p0es.enter_context(nc.sbuf_tensor("browtmp", [1, D], F32))

        k.barrier()
        t_pe = [None, None]
        for half in range(2):
            t_wml = [None, None]
            tmm6 = [None] * 6
            for dc in range(8):
                k.wait("sync", t_wml[dc % 2], t_pe[dc % 2])
                t_wml[dc % 2] = k.dma("sync", wmt[dc % 2][:],
                                      wmod[dc * 128:(dc + 1) * 128, :])
                k.wait("tensor", t_wml[dc % 2], t_cs, t_const)
                lhs = csil[:, dc * 2:dc * 2 + 2]
                for mg6 in range(6):
                    mg = half * 6 + mg6
                    rhs = wmt[dc % 2][:, mg * 512:(mg + 1) * 512]
                    fn = lambda e, lhs=lhs, rhs=rhs, dc=dc, mg6=mg6: e.matmul(
                        bank[mg6][0:2, 0:512], lhs, rhs,
                        start=(dc == 0), stop=(dc == 7))
                    tmm6[mg6] = k.op("tensor", fn)
                    if mg6 == 5:
                        t_pe[dc % 2] = tmm6[5]
            tv = None
            for mg6 in range(6):
                mg = half * 6 + mg6
                k.wait("vector", tmm6[mg6])
                ds = mod_sb[:, mg * 512:(mg + 1) * 512]
                tv = k.eop("vector", lambda e, ds=ds, mg6=mg6: e.tensor_copy(
                    ds, bank[mg6][0:2, 0:512]))
            k.wait("tensor", tv)
        if debug:
            k.wait("sync", tv, t_cs)
            k.dma("sync", dbg["d_modsb"][:, half * 3 * D:(half + 1) * 3 * D],
                  mod_sb[:, half * 3 * D:(half + 1) * 3 * D], sem="od")
            k.dma("sync", dbg["d_csil"], csil[:], sem="od")
        if p0sub < 5:
            _stub_out(k, nc, out, es)
            k.flush()
            es.close()
            return nc

        def clamp_row(src_ap, dst_ap):
            k.wait("vector", k.tok("t"), k.tok("s"), k.tok("ld"))
            k.wait("scalar", k.tok("v"), k.tok("t"), k.tok("ld"))
            t0 = k.op("scalar", lambda e, s=src_ap: e.activation(rowtmp2[:], s, AF.Abs))
            k.wait("vector", t0)
            t1 = k.eop("vector", lambda e: e.tensor_scalar_max(rowtmp2[:], rowtmp2[:], 0.25))
            t2 = k.op("scalar", lambda e, s=src_ap: e.sign(browtmp[:], s))
            k.wait("vector", t1, t2)
            return k.eop("vector", lambda e, d=dst_ap: e.tensor_mul(d, rowtmp2[:], browtmp[:]))

        for b in range(2):
            k.xwait("sync")
            tmr = k.dma("sync", mrow_sb[:], mod_sb[b:b + 1, :])
            k.wait("vector", tmr)
            k.wait("scalar", tmr)
            mrow = mrow_sb[0:1, :]
            for jj, (vec_i, dst, add1) in enumerate((
                    (0, sa_s[b][:], False), (3, sbv_s[b][:], False),
                    (1, onepca[b][:], True), (4, onepcb[b][:], True))):
                k.wait("tensor", tmr, k.tok("v"), t_const)
                tcol = None
                for dc in range(8):
                    sl = mrow[:, vec_i * D + dc * 128: vec_i * D + (dc + 1) * 128]
                    tcol = k.op("tensor", lambda e, sl=sl, dc=dc: e.matmul(
                        bank[7][:, dc:dc + 1], sl, ones_1x1[:],
                        start=True, stop=True, skip_group_check=True))
                k.wait("vector", tcol)
                if add1:
                    tvc = k.eop("vector", lambda e, dst=dst: e.tensor_scalar_add(
                        dst, bank[7][:, 0:8], 1.0))
                else:
                    tvc = k.eop("vector", lambda e, dst=dst: e.tensor_copy(
                        dst, bank[7][:, 0:8]))
                k.wait("tensor", tvc)
            if p0sub < 6:
                break
            tga = clamp_row(mrow[:, 2 * D:3 * D], rowtmp[:])
            k.wait("sync", tga)
            k.dma("sync", modrow_dram[8 + b:9 + b, :], rowtmp[:])
            k.wait("sync", tga)
            t_bo = k.dma("sync", browtmp[:], bo_row)
            k.wait("vector", t_bo, tga)
            tgabo = k.eop("vector", lambda e: e.tensor_mul(rowtmp2[:], rowtmp[:], browtmp[:]))
            k.wait("sync", tgabo)
            k.dma("sync", modrow_dram[10 + b:11 + b, :], rowtmp2[:])
            tgb = clamp_row(mrow[:, 5 * D:6 * D], rowtmp[:])
            k.wait("sync", tgb)
            k.dma("sync", modrow_dram[12 + b:13 + b, :], rowtmp[:])

        if p0sub < 7:
            _stub_out(k, nc, out, es)
            k.flush()
            es.close()
            return nc
        k.barrier()
        k.flush()

    if debug:
        k.xwait("sync")
        for i_, t_ in enumerate((onepca[0], sa_s[0], onepcb[0], sbv_s[0],
                                 onepca[1], sa_s[1], onepcb[1], sbv_s[1])):
            k.dma("sync", dbg["d_mv"][:, i_ * 8:(i_ + 1) * 8], t_[:], sem="od")

    if phases < 1:
        _stub_out(k, nc, out, es)
        k.flush()
        es.close()
        return nc
    with ExitStack() as p1:
        sb1 = lambda name, shape, dt: p1.enter_context(nc.sbuf_tensor(name, shape, dt))
        wt_q = [sb1(f"wt_q{i}", [128, D], BF16) for i in range(8)]
        wt_k = [sb1(f"wt_k{i}", [128, D], BF16) for i in range(8)]
        wt_v = [sb1(f"wt_v{i}", [128, D], BF16) for i in range(8)]
        wt_u = [sb1(f"wt_u{i}", [128, D], BF16) for i in range(8)]
        wt_o = [sb1(f"wt_o{i}", [128, D], BF16) for i in range(8)]
        gw_s = sb1("gw_s", [128, 64], BF16)
        xhat = [sb1(f"xhat{i}", [128, SEG], BF16) for i in range(8)]
        xsq2 = [sb1(f"xsq{i}", [128, SEG], BF16) for i in range(2)]
        qt = [sb1(f"qt{i}", [128, SEG], BF16) for i in range(8)]
        kt = [sb1(f"kt{i}", [128, SEG], BF16) for i in range(8)]
        ut = [sb1(f"ut{i}", [128, SEG], BF16) for i in range(8)]
        ktm = [sb1(f"ktm{i}", [128, D], BF16) for i in range(4)]
        vtm = [sb1(f"vtm{i}", [128, D], BF16) for i in range(4)]
        ofm = [sb1(f"ofm{i}", [128, SEG], BF16) for i in range(8)]
        atm = sb1("atm", [128, 128], BF16)
        ssb = sb1("ssb", [128, 128], BF16)
        e_b1 = sb1("e_b1", [128, SEG], BF16)
        e_b2 = sb1("e_b2", [128, SEG], BF16)
        e_f3 = sb1("e_f3", [128, SEG], F32)
        srow = [sb1(f"srow{i}", [1, SEG], F32) for i in range(3)]
        xf32 = [sb1(f"xf32{i}", [128, D], F32) for i in range(2)]
        x2st = [sb1(f"x2st{i}", [128, D], F32) for i in range(2)]
        h2tmst = [sb1(f"h2tmst{i}", [128, D], BF16) for i in range(2)]
        lgs = sb1("lgs", [8, SEG], F32)
        bkrep = sb1("bkrep", [128, D], F32)
        bvrep = sb1("bvrep", [128, D], F32)
        garep_t = sb1("garep_t", [128, D], F32)
        gaborep_t = sb1("gaborep_t", [128, D], F32)

        t_ld = k.dma("sync", rowsrc[:], bk_row)
        t_r = replicate_row(rowsrc[:], bkrep[:], [t_ld], 0)
        k.wait("sync", t_r)
        t_ld = k.dma("sync", rowsrc[:], bv_row)
        t_r = replicate_row(rowsrc[:], bvrep[:], [t_ld], 0)
        k.wait("sync", t_r)

        k.wait("sync", t_wq, t_wk, t_wv, t_wu, t_gw)
        t_wload = None
        for i in range(8):
            t_wload = k.dma("sync", wt_q[i][:], wqT_b[i * 128:(i + 1) * 128, :])
            t_wload = k.dma("sync", wt_k[i][:], wkT_b[i * 128:(i + 1) * 128, :])
            t_wload = k.dma("sync", wt_v[i][:], wvT_b[i * 128:(i + 1) * 128, :])
            t_wload = k.dma("sync", wt_u[i][:], wuT_b[i * 128:(i + 1) * 128, :])
        t_gwl = k.dma("sync", gw_s[:], gwT_b)

        t_x2w = [None, None]
        t_xl_prev = [None, None]
        t_h2w = [None, None]
        t_h2bounce_all = []

        def ln_stats_and_norm(get_xin_tok, onep_ap_of, shift_ap_of, b):
            prev_pair = [None, None]
            tmm = None
            for dc in range(8):
                xb = xsq2[dc % 2]
                k.wait("scalar", get_xin_tok[dc], prev_pair[dc % 2])
                tsq = k.op("scalar", lambda e, dc=dc, xb=xb: e.activation(
                    xb[:], xhat[dc][:], AF.Square))
                k.wait("tensor", get_xin_tok[dc], tsq, t_small, t_const)
                k.raw("tensor", lambda e, dc=dc: e.matmul(
                    bank[0][0:1, 0:SEG], ones_s_b[:], xhat[dc][:],
                    start=(dc == 0), stop=(dc == 7)))
                tmm = k.op("tensor", lambda e, dc=dc, xb=xb: e.matmul(
                    bank[1][0:1, 0:SEG], ones_s_b[:], xb[:],
                    start=(dc == 0), stop=(dc == 7)))
                prev_pair[dc % 2] = tmm
            k.wait("vector", tmm)
            k.eop("vector", lambda e: e.tensor_scalar_mul(srow[0][:], bank[0][0:1, 0:SEG], 1.0 / D))
            k.eop("vector", lambda e: e.tensor_scalar_mul(srow[1][:], bank[1][0:1, 0:SEG], 1.0 / D))
            k.eop("vector", lambda e: e.tensor_mul(srow[2][:], srow[0][:], srow[0][:]))
            k.eop("vector", lambda e: e.tensor_sub(
                srow[1][:], srow[1][:], srow[2][:]))
            tvv = k.eop("vector", lambda e: e.tensor_scalar_add(srow[1][:], srow[1][:], 1e-5))
            k.wait("scalar", tvv)
            k.eop("scalar", lambda e: e.activation(srow[1][:], srow[1][:], AF.Ln))
            tse = k.eop("scalar", lambda e: e.activation(srow[1][:], srow[1][:], AF.Exp,
                                                         scale=-0.5))
            k.wait("vector", tse)
            tvb = k.eop("vector", lambda e: e.scalar_tensor_tensor(
                srow[2][:], srow[0][:], -1.0, srow[1][:], op0=OP.mult, op1=OP.mult))
            k.wait("tensor", tse, tvb, tmm)
            tra = k.op("tensor", lambda e: e.matmul(bank[2][:, 0:SEG], ones_r_f[:],
                                                    srow[1][:], start=True, stop=True))
            trb = k.op("tensor", lambda e: e.matmul(bank[3][:, 0:SEG], ones_r_f[:],
                                                    srow[2][:], start=True, stop=True))
            txh = None
            k.wait("vector", tra, trb)
            for dc in range(8):
                onep_ap = onep_ap_of(dc)
                shift_ap = shift_ap_of(dc)
                k.op("vector", lambda e, dc=dc: e.tensor_mul(
                    e_f3[:], xhat[dc][:], bank[2][:, 0:SEG]))
                k.op("vector", lambda e, dc=dc: e.tensor_add(
                    e_f3[:], e_f3[:], bank[3][:, 0:SEG]))
                txh = k.op("vector", lambda e, dc=dc, o=onep_ap, s=shift_ap: e.tensor_scalar(
                    xhat[dc][:], e_f3[:], o, s, op0=OP.mult, op1=OP.add))
            k.wait("tensor", txh)
            return txh

        for b in range(2):
            k.xwait("sync")
            t_ld = k.dma("sync", rowsrc[:], modrow_dram[8 + b:9 + b, :])
            t_r = replicate_row(rowsrc[:], garep_t[:], [t_ld], 0)
            k.wait("sync", t_r)
            t_ld = k.dma("sync", rowsrc[:], modrow_dram[10 + b:11 + b, :])
            t_r = replicate_row(rowsrc[:], gaborep_t[:], [t_ld], 0)
            t_wol = []
            k.wait("sync", t_wo)
            for i in range(8):
                t_wol.append(k.dma("sync", wt_o[i][:], woT_b[i * 128:(i + 1) * 128, :]))
            two = None
            garep_b = garep_t[:]
            for i in range(8):
                k.wait("vector", t_wol[i], t_r)
                two = k.op("vector", lambda e, i=i, g=garep_b: e.tensor_mul(
                    wt_o[i][:], wt_o[i][:], g))
            k.wait("tensor", two, k.tok("s"), k.tok("v"))
            k.raw("tensor", lambda e: e.matmul(bank[6][:], zeros_b[:, 0:128], zeros_b[:],
                                               start=True, stop=True))
            tz = k.op("tensor", lambda e: e.matmul(bank[7][:], zeros_b[:, 0:128], zeros_b[:],
                                                   start=True, stop=True))

            if debug and b == 0:
                k.xwait("sync")
                for i in range(8):
                    k.dma("sync", dbg["d_wto"][i * 128:(i + 1) * 128, :], wt_o[i][:],
                          sem="od")
                k.dma("sync", dbg["d_gabo"], gaborep_t[:], sem="od")
            prev_oevac = None
            prev_s_upd = [tz] * H

            for seg in range(NSEG_B):
                t0 = b * N + seg * SEG

                tx = []
                k.xwait("sync")
                k.wait("sync", t_xf)
                for dc in range(8):
                    tx.append(k.dma("sync", xhat[dc][:],
                                    xfm_b[dc * 128:(dc + 1) * 128, t0:t0 + SEG]))
                k.xwait("tensor", "vector", "scalar")
                t_xhat = ln_stats_and_norm(
                    tx, lambda dc: onepca[b][:, dc:dc + 1], lambda dc: sa_s[b][:, dc:dc + 1], b)

                if debug:
                    k.xwait("sync")
                    for dc in range(8):
                        k.dma("sync", dbg["d_xh"][dc * 128:(dc + 1) * 128, t0:t0 + SEG],
                              xhat[dc][:], sem="od")
                    k.dma("sync", dbg["d_ab"][0:1, t0:t0 + SEG], srow[1][:], sem="od")
                    k.dma("sync", dbg["d_ab"][1:2, t0:t0 + SEG], srow[2][:], sem="od")
                def fm_proj(wt, psum_i, hc):
                    k.wait("tensor", t_xhat, t_wload)
                    tret = None
                    for dc in range(8):
                        lhs = wt[dc][:, hc * 128:(hc + 1) * 128]
                        fn = lambda e, lhs=lhs, dc=dc, pi=psum_i: e.matmul(
                            bank[pi][:, 0:SEG], lhs, xhat[dc][:],
                            start=(dc == 0), stop=(dc == 7))
                        if dc < 7:
                            k.raw("tensor", fn)
                        else:
                            tret = k.op("tensor", fn)
                    return tret

                def elu_evac(psum_i, bias_ap, dst_ap):
                    tr_ = k.op("vector", lambda e, pi=psum_i, ba=bias_ap: e.tensor_scalar(
                        e_b1[:], bank[pi][:, 0:SEG], ba, 0.0, op0=OP.add, op1=OP.max))
                    tm_ = k.op("vector", lambda e, pi=psum_i, ba=bias_ap: e.tensor_scalar(
                        e_b2[:], bank[pi][:, 0:SEG], ba, 0.0, op0=OP.add, op1=OP.min))
                    k.wait("scalar", tm_)
                    te_ = k.op("scalar", lambda e: e.activation(e_b2[:], e_b2[:], AF.Exp))
                    k.wait("vector", te_)
                    k.op("vector", lambda e, d=dst_ap: e.tensor_add(d, e_b1[:], e_b2[:]))
                    return k.op("vector", lambda e, d=dst_ap: e.tensor_scalar_add(d, d, -1.0))

                k.xwait("vector", "scalar")
                for dst_list, wt, bias_sb, act in ((qt, wt_q, bq_s, "elu"),
                                                   (kt, wt_k, bk_s, "elu"),
                                                   (ut, wt_u, bu_s, "silu")):
                    for hc in range(8):
                        pi = 2 + (hc % 2)
                        tmm = fm_proj(wt, pi, hc)
                        bias_ap = bias_sb[:, hc:hc + 1]
                        if act == "elu":
                            k.wait("vector", tmm)
                            tev = elu_evac(pi, bias_ap, dst_list[hc][:])
                            k.wait("tensor", tev)
                        else:
                            k.wait("scalar", tmm)
                            tev = k.op("scalar", lambda e, hc=hc, pi=pi, ba=bias_ap,
                                       d=dst_list[hc]: e.activation(
                                d[:], bank[pi][:, 0:SEG], AF.Silu, bias=ba))
                            k.wait("tensor", tev)

                for tc in range(4):
                    for hg in range(2):
                        pi = 4 + hg
                        hsl = slice(hg * 512, (hg + 1) * 512)
                        k.wait("tensor", t_xhat)
                        tkmm = None
                        for dc in range(8):
                            lhs = xhat[dc][:, tc * 128:(tc + 1) * 128]
                            rhs = wt_k[dc][:, hsl]
                            fn = lambda e, lhs=lhs, rhs=rhs, dc=dc, pi=pi: e.matmul(
                                bank[pi][:, 0:512], lhs, rhs, start=(dc == 0), stop=(dc == 7))
                            if dc < 7:
                                k.raw("tensor", fn)
                            else:
                                tkmm = k.op("tensor", fn)
                        k.wait("vector", tkmm)
                        tv1 = k.op("vector", lambda e, pi=pi, hsl=hsl: e.tensor_add(
                            e_f3[:], bank[pi][:, 0:512], bkrep[:, hsl]))
                        tv2 = k.op("vector", lambda e: e.tensor_scalar_min(
                            e_b2[:], e_f3[:], 0.0))
                        k.wait("scalar", tv2)
                        te = k.op("scalar", lambda e: e.activation(e_b2[:], e_b2[:], AF.Exp))
                        tv3 = k.op("vector", lambda e: e.tensor_scalar_max(
                            e_b1[:], e_f3[:], 0.0))
                        k.wait("vector", te)
                        kdst = ktm[tc][:, hsl]
                        k.op("vector", lambda e, d=kdst: e.tensor_add(d, e_b1[:], e_b2[:]))
                        tv5 = k.op("vector", lambda e, d=kdst: e.tensor_scalar_add(d, d, -1.0))
                        k.wait("tensor", tv1, tv3)
                        tvmm = None
                        for dc in range(8):
                            lhs = xhat[dc][:, tc * 128:(tc + 1) * 128]
                            rhs = wt_v[dc][:, hsl]
                            fn = lambda e, lhs=lhs, rhs=rhs, dc=dc, pi=pi: e.matmul(
                                bank[pi][:, 0:512], lhs, rhs, start=(dc == 0), stop=(dc == 7))
                            if dc < 7:
                                k.raw("tensor", fn)
                            else:
                                tvmm = k.op("tensor", fn)
                        k.wait("vector", tvmm)
                        tv6 = k.op("vector", lambda e, pi=pi, hsl=hsl: e.tensor_add(
                            e_f3[:], bank[pi][:, 0:512], bvrep[:, hsl]))
                        k.wait("scalar", tv6)
                        vdst = vtm[tc][:, hsl]
                        ts7 = k.op("scalar", lambda e, d=vdst: e.activation(
                            d, e_f3[:], AF.Silu))
                        k.wait("tensor", tv6, ts7)

                k.xwait("tensor", "vector", "scalar")
                for h in range(H):
                    sview = bank[6 + h // 4][:, (h % 4) * 128:(h % 4) * 128 + 128]
                    for cc in range(4):
                        gc = seg * 4 + cc
                        csl = slice(cc * 128, cc * 128 + 128)
                        k.wait("tensor", prev_oevac)
                        tat = k.op("tensor", lambda e, h=h, csl=csl: e.matmul(
                            bank[0][:, 0:128], kt[h][:, csl], qt[h][:, csl],
                            start=True, stop=True))
                        k.wait("vector", tat)
                        tatm = k.op("vector", lambda e: e.tensor_mul(
                            atm[:], bank[0][:, 0:128], triu_f[:]))
                        tsc = None
                        if gc > 0:
                            k.wait("vector", prev_s_upd[h])
                            tsc = k.op("vector", lambda e, sview=sview: e.tensor_copy(
                                ssb[:], sview))
                        k.wait("tensor", tatm)
                        fn_intra = lambda e, cc=cc, h=h, gc=gc: e.matmul(
                            bank[1][:, 0:128], vtm[cc][:, h * 128:(h + 1) * 128], atm[:],
                            start=True, stop=(gc == 0))
                        if gc > 0:
                            k.raw("tensor", fn_intra)
                            k.wait("tensor", tsc)
                            t_o = k.op("tensor", lambda e, h=h, csl=csl: e.matmul(
                                bank[1][:, 0:128], ssb[:], qt[h][:, csl],
                                start=False, stop=True))
                        else:
                            t_o = k.op("tensor", fn_intra)
                        if tsc is not None:
                            k.wait("tensor", tsc)
                        t_s = k.op("tensor", lambda e, cc=cc, h=h, sview=sview: e.matmul(
                            sview, ktm[cc][:, h * 128:(h + 1) * 128],
                            vtm[cc][:, h * 128:(h + 1) * 128], start=False, stop=False,
                            skip_group_check=True))
                        prev_s_upd[h] = t_s
                        k.wait("scalar", t_o)
                        prev_oevac = k.op("scalar", lambda e, h=h, csl=csl: e.copy(
                            ofm[h][:, csl], bank[1][:, 0:128]))

                if debug:
                    k.xwait("sync")
                    for hc in range(8):
                        k.dma("sync", dbg["d_oraw"][hc * 128:(hc + 1) * 128, t0:t0 + SEG],
                              ofm[hc][:], sem="od")
                        k.dma("sync", dbg["d_ut"][hc * 128:(hc + 1) * 128, t0:t0 + SEG],
                              ut[hc][:], sem="od")
                        k.dma("sync", dbg["d_qt"][hc * 128:(hc + 1) * 128, t0:t0 + SEG],
                              qt[hc][:], sem="od")
                        k.dma("sync", dbg["d_kt"][hc * 128:(hc + 1) * 128, t0:t0 + SEG],
                              kt[hc][:], sem="od")
                    for tc in range(4):
                        k.dma("sync", dbg["d_vtm"][t0 + tc * 128:t0 + (tc + 1) * 128, :],
                              vtm[tc][:], sem="od")
                k.xwait("tensor", "vector", "scalar")
                k.wait("scalar", prev_oevac)
                prev_pair_r = [None, None]
                tmr = None
                for hc in range(8):
                    xb = xsq2[hc % 2]
                    k.wait("scalar", prev_pair_r[hc % 2])
                    tsq = k.op("scalar", lambda e, hc=hc, xb=xb: e.activation(
                        xb[:], ofm[hc][:], AF.Square))
                    k.wait("tensor", tsq, prev_oevac)
                    tmr = k.op("tensor", lambda e, hc=hc, xb=xb: e.matmul(
                        bank[4][0:1, 0:SEG], ones_s_b[:], xb[:],
                        start=(hc == 0), stop=(hc == 7)))
                    prev_pair_r[hc % 2] = tmr
                k.wait("vector", tmr)
                tvr = k.eop("vector", lambda e: e.tensor_scalar_mul(
                    srow[0][:], bank[4][0:1, 0:SEG], 1.0 / D))
                k.wait("scalar", tvr)
                k.eop("scalar", lambda e: e.activation(srow[0][:], srow[0][:], AF.Ln))
                tse = k.eop("scalar", lambda e: e.activation(srow[0][:], srow[0][:], AF.Exp,
                                                             scale=0.5))
                k.wait("vector", tse)
                tva = k.eop("vector", lambda e: e.tensor_scalar_add(srow[0][:], srow[0][:], 1e-8))
                k.wait("scalar", tva)
                k.eop("scalar", lambda e: e.activation(srow[0][:], srow[0][:], AF.Ln))
                tse2 = k.eop("scalar", lambda e: e.activation(srow[0][:], srow[0][:], AF.Exp,
                                                              scale=-1.0))
                if debug:
                    k.wait("sync", tse2)
                    k.dma("sync", dbg["d_frow"][:, t0:t0 + SEG], srow[0][:], sem="od")
                k.wait("tensor", tse2, tmr)
                trf = k.op("tensor", lambda e: e.matmul(bank[5][:, 0:SEG], ones_r_f[:],
                                                        srow[0][:], start=True, stop=True))
                tog = None
                k.wait("vector", trf)
                for hc in range(8):
                    k.op("vector", lambda e, hc=hc: e.tensor_mul(
                        e_f3[:], ofm[hc][:], bank[5][:, 0:SEG]))
                    tog = k.op("vector", lambda e, hc=hc: e.scalar_tensor_tensor(
                        ofm[hc][:], e_f3[:], asc_s[:, hc:hc + 1], ut[hc][:],
                        op0=OP.mult, op1=OP.mult))
                if debug:
                    k.xwait("sync")
                    for hc in range(8):
                        k.dma("sync", dbg["d_og"][hc * 128:(hc + 1) * 128, t0:t0 + SEG],
                              ofm[hc][:], sem="od")
                gaborep_b = gaborep_t[:]
                prev_eread = [None, None]
                for tc in range(4):
                    tcur = t0 + tc * 128
                    xbuf = xf32[tc % 2]
                    x2buf = x2st[tc % 2]
                    k.wait("sync", t_xl_prev[tc % 2])
                    txl = k.dma("sync", xbuf[:], x_tm[tcur:tcur + 128, :])
                    tx2 = None
                    for eg in range(2):
                        pi = 2 + eg
                        esl = slice(eg * 512, (eg + 1) * 512)
                        k.wait("tensor", tog, t_x2w[tc % 2], prev_eread[eg])
                        tpo = None
                        for hc in range(8):
                            lhs = ofm[hc][:, tc * 128:(tc + 1) * 128]
                            rhs = wt_o[hc][:, esl]
                            fn = lambda e, lhs=lhs, rhs=rhs, hc=hc, pi=pi: e.matmul(
                                bank[pi][:, 0:512], lhs, rhs, start=(hc == 0), stop=(hc == 7))
                            if hc < 7:
                                k.raw("tensor", fn)
                            else:
                                tpo = k.op("tensor", fn)
                        k.wait("vector", tpo, txl)
                        k.op("vector", lambda e, pi=pi, esl=esl, g=gaborep_b: e.tensor_add(
                            e_f3[:], bank[pi][:, 0:512], g[:, esl]))
                        tx2 = k.op("vector", lambda e, x2buf=x2buf, esl=esl, xbuf=xbuf:
                                   e.tensor_add(x2buf[:, esl], e_f3[:], xbuf[:, esl]))
                        prev_eread[eg] = tx2
                    k.wait("sync", tx2)
                    t1_ = k.dma("sync", x2_dram[tcur:tcur + 128, :], x2buf[:])
                    k.wait("gpsimd", tx2)
                    t2_ = k.dma("gpsimd", x2_btm[tcur:tcur + 128, :], x2buf[:])
                    t_x2w[tc % 2] = t2_
                    t_xl_prev[tc % 2] = tx2
                    k.wait("vector", t1_)
                t_x2_seg = (t1_, t2_)

                tx2f = []
                k.xwait("sync")
                k.wait("sync", *t_x2w)
                for dc in range(8):
                    tx2f.append(k.dma("sync", xhat[dc][:],
                                      x2_btm[t0:t0 + SEG, dc * 128:(dc + 1) * 128],
                                      transpose=True))
                k.xwait("tensor", "vector", "scalar")
                t_h2 = ln_stats_and_norm(
                    tx2f, lambda dc: onepcb[b][:, dc:dc + 1], lambda dc: sbv_s[b][:, dc:dc + 1], b)
                th2w = []
                for dc in range(8):
                    k.wait("sync", t_h2)
                    th2w.append(k.dma("sync", h2fm[dc * 128:(dc + 1) * 128, t0:t0 + SEG],
                                      xhat[dc][:]))
                for tc in range(4):
                    tcur = t0 + tc * 128
                    hbuf = h2tmst[tc % 2]
                    k.wait("sync", th2w[-1], t_h2w[tc % 2])
                    tr = k.dma("sync", hbuf[:], h2fm[:, tcur:tcur + 128], transpose=True)
                    k.wait("sync", tr)
                    tb = k.dma("sync", h2_bounce[tcur:tcur + 128, :], hbuf[:])
                    t_h2w[tc % 2] = tb
                    t_h2bounce_all.append(tb)
                k.wait("tensor", t_h2, t_gwl)
                tlm = None
                for dc in range(8):
                    lhs = gw_s[:, dc * 8:dc * 8 + 8]
                    fn = lambda e, lhs=lhs, dc=dc: e.matmul(
                        bank[2][0:8, 0:SEG], lhs, xhat[dc][:],
                        start=(dc == 0), stop=(dc == 7))
                    if dc < 7:
                        k.raw("tensor", fn)
                    else:
                        tlm = k.op("tensor", fn)
                k.wait("scalar", tlm)
                tle = k.op("scalar", lambda e: e.activation(
                    lgs[:], bank[2][0:8, 0:SEG], AF.Identity, bias=gbc_s[:]))
                k.wait("tensor", tle)
                for tc in range(4):
                    ch = (t0 + tc * 128) // 128
                    src = lgs[:, tc * 128:(tc + 1) * 128]
                    ttr = k.op("tensor", lambda e, src=src: e.transpose(
                        bank[3][0:128, 0:8], src, ident_f[0:8, 0:8]))
                    k.wait("vector", ttr)
                    dstl = logits_tm[:, ch, :]
                    tlv = k.op("vector", lambda e, d=dstl: e.tensor_copy(
                        d, bank[3][0:128, 0:8]))
                    k.wait("tensor", tlv)

        k.barrier()
        k.flush()

    if phases < 2:
        _stub_out(k, nc, out, es)
        k.flush()
        es.close()
        return nc
    t_ag_h2 = k.cc("AllGather", OP.bypass, [h2_bounce.opt()], [h2_all.opt()])

    with ExitStack() as p2:
        sb2 = lambda name, shape, dt: p2.enter_context(nc.sbuf_tensor(name, shape, dt))
        mx = sb2("mx", [128, NCH], F32)
        ex = sb2("ex", [128, NCH * 8], F32)
        iot = sb2("iot", [128, NCH * 8], F32)
        cand = sb2("cand", [128, NCH * 8], F32)
        idxf = sb2("idxf", [128, NCH], F32)
        oh_f = sb2("oh_f", [128, NCH * 8], F32)
        oh_b = sb2("oh_b", [128, NCH * 8], BF16)
        cum_sb = sb2("cum_sb", [128, NCH * 8], F32)
        tpr = sb2("tpr", [128, NCH], F32)
        ttot = sb2("ttot", [1, NCH * 8], F32)
        ttot32 = sb2("ttot32", [32, 8], F32)
        exrow = sb2("exrow", [32, 8], F32)
        offrow = sb2("offrow", [1, 8], F32)
        cmat = sb2("cmat", [8, 8], F32)
        cmatbig = sb2("cmatbig", [128, 8], F32)
        comb = sb2("comb", [32, 8], F32)
        combrow = sb2("combrow", [1, NCH * 8], F32)
        possel = sb2("possel", [128, NCH], F32)
        keep_s = sb2("keep_s", [128, NCH], F32)
        zf32 = sb2("zf32", [128, 428], F32)
        iota_p = sb2("iota_p", [128, 1], F32)
        sc_r = sb2("sc_r", [128, NCH], F32)
        sc_q = sb2("sc_q", [128, NCH], F32)
        sc_j = sb2("sc_j", [128, NCH], F32)
        sc_s = sb2("sc_s", [128, NCH], F32)
        sc_d = sb2("sc_d", [128, 1], F32)
        sc_u = sb2("sc_u", [128, NCH], U32)
        sc_t = sb2("sc_t", [128, NCH], U32)
        sc_d2 = sb2("sc_d2", [128, NCH], F32)
        payall4 = sb2("payall4", [128, NCH, 4], F32)
        tokb_s = sb2("tokb_s", [128, 1], F32)

        lg3 = logits_tm[:]
        i3 = iot[:].rearrange("p (a b) -> p a b", b=8)
        e3 = ex[:].rearrange("p (a b) -> p a b", b=8)
        o3 = oh_f[:].rearrange("p (a b) -> p a b", b=8)
        c3 = cand[:].rearrange("p (a b) -> p a b", b=8)

        tio = k.op("gpsimd", lambda e: e.iota(
            i3, [[0, NCH], [1, 8]], base=0, channel_multiplier=0,
            allow_small_or_imprecise_dtypes=True))
        tmx = k.eop("vector", lambda e: e.tensor_reduce(
            mx[:].unsqueeze(2), lg3, axis=AX.X, op=OP.max))
        mxb = mx[:].unsqueeze(2).broadcast_to([128, NCH, 8])
        k.wait("vector", tmx)
        tex = k.eop("vector", lambda e: e.tensor_tensor(e3, lg3, mxb, op=OP.subtract))
        k.wait("scalar", tex)
        tee = k.eop("scalar", lambda e: e.activation(ex[:], ex[:], AF.Exp))
        k.wait("vector", tee)
        tsum = k.eop("vector", lambda e: e.tensor_reduce(
            tpr[:].unsqueeze(2), e3, axis=AX.X, op=OP.add))
        k.wait("scalar", tsum)
        k.eop("scalar", lambda e: e.activation(tpr[:], tpr[:], AF.Ln))
        ttp = k.eop("scalar", lambda e: e.activation(tpr[:], tpr[:], AF.Exp, scale=-1.0))
        k.wait("vector", tio)
        teq = k.eop("vector", lambda e: e.tensor_tensor(o3, lg3, mxb, op=OP.is_equal))
        k.wait("vector", teq)
        k.eop("vector", lambda e: e.tensor_scalar_add(cand[:], iot[:], -1000.0))
        k.eop("vector", lambda e: e.tensor_mul(cand[:], cand[:], oh_f[:]))
        k.eop("vector", lambda e: e.tensor_scalar_add(cand[:], cand[:], 1000.0))
        tix = k.eop("vector", lambda e: e.tensor_reduce(
            idxf[:].unsqueeze(2), c3, axis=AX.X, op=OP.min))
        ixb = idxf[:].unsqueeze(2).broadcast_to([128, NCH, 8])
        k.wait("vector", tix)
        toh = k.eop("vector", lambda e: e.tensor_tensor(o3, i3, ixb, op=OP.is_equal))
        k.wait("vector", toh)
        tohb = k.eop("vector", lambda e: e.tensor_copy(oh_b[:], oh_f[:]))
        k.wait("tensor", tohb, t_const)
        tcm = None
        for ch in range(NCH):
            sl = slice(ch * 8, ch * 8 + 8)
            fn = lambda e, sl=sl, ch=ch: e.matmul(
                bank[0][:, sl], triu_b[:], oh_b[:, sl],
                start=(ch == 0), stop=True, skip_group_check=True)
            if ch < NCH - 1:
                k.raw("tensor", fn)
            else:
                tcm = k.op("tensor", fn)
        k.wait("vector", tcm)
        tcums = k.eop("vector", lambda e: e.tensor_copy(cum_sb[:], bank[0][:, 0:NCH * 8]))
        k.wait("sync", tcums)
        ttt = k.dma("sync", modrow_dram[0:1, 0:NCH * 8], cum_sb[127:128, :])
        k.wait("sync", ttt)
        trr = k.dma("sync", ttot32[:],
                    modrow_dram[0:1, 0:NCH * 8].rearrange("o (a b) -> (o a) b", b=8))
        k.wait("tensor", trr)
        tmm1 = k.op("tensor", lambda e: e.matmul(
            bank[1][0:32, 0:8], su32_f[:], ttot32[:], start=True, stop=True))
        tmm2 = k.op("tensor", lambda e: e.matmul(
            bank[2][0:1, 0:8], ones32_f[:], ttot32[:], start=True, stop=True))
        k.wait("vector", tmm1, tmm2)
        texr = k.eop("vector", lambda e: e.tensor_copy(exrow[:], bank[1][0:32, 0:8]))
        tctr = k.eop("vector", lambda e: e.tensor_copy(offrow[:], bank[2][0:1, 0:8]))
        k.wait("tensor", tctr)
        tcrepb = k.op("tensor", lambda e: e.matmul(
            bank[3][:, 0:8], ones_r_f[:], offrow[:], start=True, stop=True))
        k.wait("vector", tcrepb)
        tcmatb = k.eop("vector", lambda e: e.tensor_copy(cmatbig[:], bank[3][:, 0:8]))
        k.wait("sync", tcmatb)
        tcin = k.dma("sync", cnt_in, cmatbig[:])
        if debug:
            k.dma("sync", dbg["d_tt32"], ttot32[:], sem="od")
            k.dma("sync", dbg["d_ttot"], modrow_dram[0:1, 0:NCH * 8], sem="od")
        k.wait("gpsimd", tcin)
        t_ar = k.cc("AllGather", OP.bypass, [cnt_in.opt()], [cnt_out.opt()])
        k.wait("sync", t_ar)
        tcl = k.dma("sync", cmat[:], cnt_out.rearrange("(c r) e -> c r e", r=128)[:, 0, :])
        k.wait("tensor", tcl)
        toff = k.op("tensor", lambda e: e.matmul(
            bank[4][0:1, 0:8], cmask_s[:], cmat[:], start=True, stop=True))
        k.wait("vector", toff)
        toffr = k.eop("vector", lambda e: e.tensor_copy(offrow[:], bank[4][0:1, 0:8]))
        k.wait("tensor", toffr)
        torep = k.op("tensor", lambda e: e.matmul(
            bank[5][0:32, 0:8], ones_r_f[:, 0:32], offrow[:], start=True, stop=True))
        k.wait("vector", torep, texr)
        tcomb = k.eop("vector", lambda e: e.tensor_add(comb[:], exrow[:], bank[5][0:32, 0:8]))
        k.wait("sync", tcomb)
        tcw = k.dma("sync", modrow_dram[1:2, 0:NCH * 8].rearrange("o (a b) -> (o a) b", b=8),
                    comb[:])
        k.wait("sync", tcw)
        tcrow = k.dma("sync", combrow[:], modrow_dram[1:2, 0:NCH * 8])
        k.wait("tensor", tcrow)
        tcrep2 = k.op("tensor", lambda e: e.matmul(
            bank[5][:, 0:NCH * 8], ones_r_f[:], combrow[:], start=True, stop=True))
        k.wait("vector", tcrep2, tcums, toh)
        k.eop("vector", lambda e: e.tensor_add(cand[:], cum_sb[:], bank[5][:, 0:NCH * 8]))
        k.eop("vector", lambda e: e.tensor_mul(cand[:], cand[:], oh_f[:]))
        tps = k.eop("vector", lambda e: e.tensor_reduce(
            possel[:].unsqueeze(2), c3, axis=AX.X, op=OP.add))
        k.wait("vector", tps, ttp)
        tkp = k.eop("vector", lambda e: e.tensor_scalar(
            keep_s[:], possel[:], float(CAP), None, op0=OP.is_le))
        k.wait("vector", tkp)
        tgt = k.eop("vector", lambda e: e.tensor_mul(gate_s[:], tpr[:], keep_s[:]))
        k.eop("vector", lambda e: e.tensor_mul(cand[:, 0:NCH], possel[:], keep_s[:]))
        k.eop("vector", lambda e: e.tensor_scalar(
            cand[:, 0:NCH], cand[:, 0:NCH], -1.0, 0.0, op0=OP.add, op1=OP.max))
        k.eop("vector", lambda e: e.scalar_tensor_tensor(
            cand[:, NCH:2 * NCH], idxf[:], float(CAP), cand[:, 0:NCH],
            op0=OP.mult, op1=OP.add))
        k.eop("vector", lambda e: e.tensor_scalar_max(
            cand[:, NCH:2 * NCH], cand[:, NCH:2 * NCH], 0.0))
        tg2 = k.eop("vector", lambda e: e.tensor_scalar_min(
            cand[:, NCH:2 * NCH], cand[:, NCH:2 * NCH], float(NCORES * CAP - 1)))
        k.wait("gpsimd", tg2)
        tidxg = k.dma("gpsimd", idxg_s[:], cand[:, NCH:2 * NCH])
        tz0 = k.op("gpsimd", lambda e: e.memset(zf32[:], 0.0))
        k.wait("sync", tz0)
        tzw = None
        zview = slotmap_in.rearrange("(p c) w -> p (c w)", p=128)
        for off in range(0, 1284, 428):
            w = min(428, 1284 - off)
            tzw = k.dma("sync", zview[:, off:off + w], zf32[:, 0:w])
        tip = k.op("gpsimd", lambda e: e.iota(
            iota_p[:], [[0, 1]], base=0, channel_multiplier=1,
            allow_small_or_imprecise_dtypes=True))
        k.wait("vector", tip, tkp)
        tr1 = k.eop("vector", lambda e: e.tensor_scalar_add(sc_r[:], possel[:], -1.0))
        k.eop("vector", lambda e: e.tensor_scalar_mul(sc_j[:], sc_r[:], 1.0 / 128.0))
        k.eop("vector", lambda e: e.tensor_scalar_add(sc_j[:], sc_j[:], 8388608.0))
        k.eop("vector", lambda e: e.tensor_scalar_add(sc_j[:], sc_j[:], -8388608.0))
        k.eop("vector", lambda e: e.scalar_tensor_tensor(
            sc_q[:], sc_j[:], -128.0, sc_r[:], op0=OP.mult, op1=OP.add))
        k.eop("vector", lambda e: e.tensor_scalar(
            sc_d2[:], sc_q[:], 0.0, None, op0=OP.is_lt))
        k.eop("vector", lambda e: e.tensor_sub(sc_j[:], sc_j[:], sc_d2[:]))
        k.eop("vector", lambda e: e.scalar_tensor_tensor(
            sc_q[:], sc_d2[:], 128.0, sc_q[:], op0=OP.mult, op1=OP.add))
        k.eop("vector", lambda e: e.scalar_tensor_tensor(
            sc_s[:], sc_q[:], 40.0, sc_j[:], op0=OP.mult, op1=OP.add))
        k.eop("vector", lambda e: e.scalar_tensor_tensor(
            sc_s[:], idxf[:], float(CAP), sc_s[:], op0=OP.mult, op1=OP.add))
        k.eop("vector", lambda e: e.tensor_scalar_add(sc_d[:], iota_p[:], float(NCORES * CAP)))
        k.eop("vector", lambda e: e.tensor_sub(sc_s[:], sc_s[:], sc_d[:].broadcast_to([128, NCH])))
        k.eop("vector", lambda e: e.tensor_mul(sc_s[:], sc_s[:], keep_s[:]))
        k.eop("vector", lambda e: e.tensor_add(
            sc_s[:], sc_s[:], sc_d[:].broadcast_to([128, NCH])))
        k.eop("vector", lambda e: e.tensor_scalar_max(sc_s[:], sc_s[:], 0.0))
        tslot = k.eop("vector", lambda e: e.tensor_scalar_min(
            sc_s[:], sc_s[:], float(NCORES * CAP + 127)))
        if debug:
            k.wait("sync", tslot)
            k.dma("sync", dbg["d_scdbg"][:, 0:NCH], sc_r[:], sem="od")
            k.dma("sync", dbg["d_scdbg"][:, NCH:2 * NCH], sc_j[:], sem="od")
            k.dma("sync", dbg["d_scdbg"][:, 2 * NCH:3 * NCH], sc_q[:], sem="od")
            k.dma("sync", dbg["d_scdbg"][:, 3 * NCH:4 * NCH], sc_s[:], sem="od")
        k.wait("gpsimd", tslot)
        tsc_u = k.dma("gpsimd", sc_u[:], sc_s[:])
        tpay0 = k.op("gpsimd", lambda e: e.iota(
            payall4[:], [[128, NCH], [0, 4]], base=1, channel_multiplier=1,
            allow_small_or_imprecise_dtypes=True))
        tb_ld = k.dma("sync", tokb_s[:], tokbase)
        k.wait("vector", tpay0, tb_ld)
        tpay = k.eop("vector", lambda e: e.tensor_scalar_add(
            payall4[:].rearrange("p a b -> p (a b)"),
            payall4[:].rearrange("p a b -> p (a b)"), tokb_s[:]))
        k.wait("gpsimd", tsc_u, tzw, tpay)
        tscat = None
        for ch in range(NCH):
            tscat = k.dma16(lambda e, ch=ch: e.indirect_dma_start(
                out=slotmap_in, out_offset=IndirectOffsetOnAxis(sc_u[:, ch:ch + 1], 0),
                in_=payall4[:, ch, :], in_offset=None))
        k.wait("gpsimd", tscat)
        t_ar_slot = k.cc("AllReduce", OP.add, [slotmap_in.opt()], [slotmap_out.opt()])

        if debug:
            k.wait("sync", tgt, tidxg, tmx, tix, tps, tkp, ttp, tcums)
            k.dma("sync", dbg["d_mx"], mx[:], sem="od")
            k.dma("sync", dbg["d_idxf"], idxf[:], sem="od")
            k.dma("sync", dbg["d_possel"], possel[:], sem="od")
            k.dma("sync", dbg["d_keep"], keep_s[:], sem="od")
            k.dma("sync", dbg["d_tpr"], tpr[:], sem="od")
            k.dma("sync", dbg["d_cum"], cum_sb[:], sem="od")
            k.dma("sync", dbg["d_gate"], gate_s[:], sem="od")
            k.wait("sync", tscat)
            k.dma("sync", dbg["d_scu"], sc_u[:], sem="od")
            k.dma("sync", dbg["d_idxg"], cand[:, NCH:2 * NCH], sem="od")
            k.dma("sync", dbg["d_cnt"], cnt_out.rearrange("(c r) e -> c r e", r=128)[:, 0, :], sem="od")
            k.dma("sync", dbg["d_logits"], logits_tm[:].rearrange("p a b -> p (a b)"),
                  sem="od")
            k.dma("sync", dbg["d_x2"], x2_dram, sem="od")
            k.dma("sync", dbg["d_h2"], h2_bounce, sem="od")

        k.barrier()
        k.flush()

    if phases < 3:
        _stub_out(k, nc, out, es)
        k.flush()
        es.close()
        return nc
    with ExitStack() as p3:
        sb3 = lambda name, shape, dt: p3.enter_context(nc.sbuf_tensor(name, shape, dt))
        slot4 = sb3("slot4", [128, CAP // 128 * 4], F32)
        srowf = sb3("srowf", [128, 1], F32)
        srowu = sb3("srowu", [128, 1], U32)
        slot4u = sb3("slot4u", [128, CAP // 128 * 4], U32)
        btm = [sb3(f"btm{i}", [128, 1024], BF16) for i in range(4)]
        w1t = [sb3(f"w1t{i}", [128, FF], BF16) for i in range(8)]
        w2t = [sb3(f"w2t{i}", [128, D], BF16) for i in range(32)]
        bufg = [sb3(f"bufg{i}", [128, 8, FFG], BF16) for i in range(2)]
        y1c = [sb3(f"y1c{i}", [128, FFG], BF16) for i in range(2)]
        eost = [sb3(f"eost{i}", [128, D], BF16) for i in range(2)]
        b2rep = sb3("b2rep", [128, D], F32)

        t_ld = k.dma("sync", rowsrc[:], b2_row)
        replicate_row(rowsrc[:], b2rep[:], [t_ld], 6)

        k.wait("sync", t_w1, t_w2)
        t_wf = None
        for i in range(8):
            t_wf = k.dma("sync", w1t[i][:], w1_b[i * 128:(i + 1) * 128, :])
        for i in range(32):
            t_wf = k.dma("sync", w2t[i][:], w2_b[i * 128:(i + 1) * 128, :])

        tsr = k.dma("sync", srowf[:], slotrow128)
        k.wait("gpsimd", tsr)
        tsru = k.dma("gpsimd", srowu[:], srowf[:])
        k.wait("gpsimd", tsru, t_ar_slot)
        tsl = k.dma16(lambda e: e.indirect_dma_start(
            out=slot4[:], out_offset=None,
            in_=slotmap_out, in_offset=IndirectOffsetOnAxis(srowu[:], 0)))
        k.wait("vector", tsl)
        k.eop("vector", lambda e: e.tensor_scalar(
            slot4[:], slot4[:], -1.0, 0.0, op0=OP.add, op1=OP.max))
        tsi0 = k.eop("vector", lambda e: e.tensor_scalar_min(
            slot4[:], slot4[:], float(NCORES * TOK - 1)))
        k.wait("gpsimd", tsi0)
        tsi = k.dma("gpsimd", slot4u[:], slot4[:])

        k.wait("gpsimd", t_ag_h2, tsi)

        gather_state = {"tgi": [None, None], "tw": {}, "tlast": [None, None]}

        def gather(g, gb, waitmm=None):
            toks = []
            for half in range(2):
                col = 2 * g + half
                iu = slot4u[:, 4 * col:4 * col + 1]
                k.wait("gpsimd", tsi, gather_state["tw"].get((gb, half)))
                tgi = k.dma16(lambda e, iu=iu, gb=gb, half=half: e.indirect_dma_start(
                    out=btm[2 * gb + half][:], out_offset=None,
                    in_=h2_all, in_offset=IndirectOffsetOnAxis(iu[:], 0)))
                gather_state["tgi"][half] = tgi
                k.wait("sync", tgi, gather_state["tlast"][gb])
                tw = k.dma("sync", ffn_tm[gb][half * 128:(half + 1) * 128, :],
                           btm[2 * gb + half][:])
                gather_state["tw"][(gb, half)] = tw
                toks.append(tw)
            k.wait("sync", *toks)
            if waitmm is not None:
                k.wait("sync", waitmm)
            tlast = None
            for dc in range(8):
                tlast = k.dma("sync", bufg[gb][:, dc, :],
                              ffn_tm[gb][:, dc * 128:(dc + 1) * 128], transpose=True)
            gather_state["tlast"][gb] = tlast
            if debug and g == 0:
                k.dma("sync", dbg["d_ffng0"], ffn_tm[0], sem="od")
            return tlast

        tga = [gather(0, 0), None]
        t_eout_w = [None, None]
        ts1_prev = [None, None]
        for g in range(NFFG):
            gb = g % 2
            k.wait("tensor", tga[gb], t_wf)
            last_mm1 = None
            for fc in range(32):
                pb = fc % 2
                k.wait("tensor", ts1_prev[pb])
                tmm1 = None
                for dc in range(8):
                    lhs = w1t[dc][:, fc * 128:(fc + 1) * 128]
                    rhs = bufg[gb][:, dc, :]
                    fn = lambda e, lhs=lhs, rhs=rhs, dc=dc, pb=pb: e.matmul(
                        bank[pb][:, 0:FFG], lhs, rhs, start=(dc == 0), stop=(dc == 7))
                    if dc < 7:
                        k.raw("tensor", fn)
                    else:
                        tmm1 = k.op("tensor", fn)
                k.wait("scalar", tmm1)
                b1ap = b1_s[:, fc:fc + 1]
                ts1 = k.op("scalar", lambda e, pb=pb, b1ap=b1ap: e.activation(
                    y1c[pb][:], bank[pb][:, 0:FFG], AF.Relu, bias=b1ap))
                ts1_prev[pb] = ts1
                k.wait("tensor", ts1)
                for tc in range(2):
                    lhs = y1c[pb][:, tc * 128:(tc + 1) * 128]
                    for dg in range(2):
                        rhs = w2t[fc][:, dg * 512:(dg + 1) * 512]
                        bi = 2 + tc * 2 + dg
                        fn = lambda e, lhs=lhs, rhs=rhs, bi=bi, fc=fc: e.matmul(
                            bank[bi][:, 0:512], lhs, rhs,
                            start=(fc == 0), stop=(fc == 31))
                        if fc == 31 and tc == 1 and dg == 1:
                            tmm2 = k.op("tensor", fn)
                        else:
                            k.raw("tensor", fn)
                last_mm1 = tmm1
            if g + 1 < NFFG:
                tga[(g + 1) % 2] = gather(g + 1, (g + 1) % 2, waitmm=last_mm1)
            for tc in range(2):
                ebuf = eost[tc]
                k.wait("vector", tmm2, t_eout_w[tc])
                tve = None
                for dg in range(2):
                    bi = 2 + tc * 2 + dg
                    dsl = slice(dg * 512, (dg + 1) * 512)
                    tve = k.op("vector", lambda e, bi=bi, dsl=dsl, ebuf=ebuf: e.tensor_add(
                        ebuf[:, dsl], bank[bi][:, 0:512], b2rep[:, dsl]))
                k.wait("sync", tve)
                r0 = g * FFG + tc * 128
                tst = k.dma("sync", eout_bounce[r0:r0 + 128, :], ebuf[:])
                t_eout_w[tc] = tst
                k.wait("tensor", tve)
        k.wait("gpsimd", *t_eout_w)
        t_ag_eo = k.cc("AllGather", OP.bypass, [eout_bounce.opt()], [eout_all.opt()])

        if debug:
            k.wait("sync", *t_eout_w)
            k.dma("sync", dbg["d_eout"], eout_bounce, sem="od")
            k.dma("sync", dbg["d_slotmap"], slotmap_out, sem="od")
            k.wait("sync", tsi)
            k.dma("sync", dbg["d_slot4"], slot4[:], sem="od")

        k.barrier()
        k.flush()

    if phases < 4:
        _stub_out(k, nc, out, es)
        k.flush()
        es.close()
        return nc
    with ExitStack() as p4:
        sb4 = lambda name, shape, dt: p4.enter_context(nc.sbuf_tensor(name, shape, dt))
        egst = [sb4(f"egst{i}", [128, D], BF16) for i in range(2)]
        x2l = [sb4(f"x2l{i}", [128, D], F32) for i in range(2)]
        otile = [sb4(f"otile{i}", [128, D], F32) for i in range(2)]
        ftmp = sb4("ftmp", [128, D], F32)
        gbrep_t = [sb4(f"gbrep_t{i}", [128, D], F32) for i in range(2)]

        for b in range(2):
            t_ld = k.dma("sync", rowsrc[:], modrow_dram[12 + b:13 + b, :])
            t_r = replicate_row(rowsrc[:], gbrep_t[b][:], [t_ld], 0)
            k.wait("sync", t_r)

        k.wait("gpsimd", t_ag_eo)
        k.wait("sync", t_ag_eo)
        tprev = [None, None]
        tso_prev = [None, None]
        tout_all = []
        for ch in range(NCH):
            sl = ch % 2
            b = ch // 16
            idxcol = idxg_s[:, ch:ch + 1]
            k.wait("gpsimd", tprev[sl])
            tg_ = k.dma16(lambda e, idxcol=idxcol, sl=sl: e.indirect_dma_start(
                out=egst[sl][:], out_offset=None,
                in_=eout_all, in_offset=IndirectOffsetOnAxis(idxcol, 0)))
            k.wait("sync", tprev[sl])
            tx_ = k.dma("sync", x2l[sl][:], x2_dram[ch * 128:(ch + 1) * 128, :])
            gate_ap = gate_s[:, ch:ch + 1]
            gb_ap = gbrep_t[b][:]
            k.wait("vector", tg_, tx_, tso_prev[sl])
            k.op("vector", lambda e, sl=sl, gate_ap=gate_ap, gb_ap=gb_ap:
                 e.scalar_tensor_tensor(ftmp[:], egst[sl][:], gate_ap, gb_ap,
                                        op0=OP.mult, op1=OP.mult))
            tvo = k.op("vector", lambda e, sl=sl: e.tensor_add(
                otile[sl][:], ftmp[:], x2l[sl][:]))
            tprev[sl] = tvo
            k.wait("sync", tvo)
            tso = k.dma("sync", out[ch * 128:(ch + 1) * 128, :], otile[sl][:], sem="od")
            tso_prev[sl] = tso
            tout_all.append(tso)
        k.wait("sync", tout_all[-1])
        k.barrier()
        k.flush()

    es.close()
    return nc


_CACHE = {}


def _get_nc(debug=False):
    if debug not in _CACHE:
        _CACHE[debug] = build_nc(debug)
    return _CACHE[debug]


def kernel(**inputs):
    return _run(inputs, debug=False)[0]


def _run(inputs, debug=False):
    f32 = np.float32
    x = np.asarray(inputs["L0_attn_x"], f32)
    cvec = np.asarray(inputs["L0_c"], f32)
    Wq = np.asarray(inputs["L0_attn_Wq"], f32); bq = np.asarray(inputs["L0_attn_bq"], f32)
    Wk = np.asarray(inputs["L0_attn_Wk"], f32); bk = np.asarray(inputs["L0_attn_bk"], f32)
    Wv = np.asarray(inputs["L0_attn_Wv"], f32); bv = np.asarray(inputs["L0_attn_bv"], f32)
    Wu = np.asarray(inputs["L0_attn_Wu"], f32); bu = np.asarray(inputs["L0_attn_bu"], f32)
    Wo = np.asarray(inputs["L0_attn_Wo"], f32); bo = np.asarray(inputs["L0_attn_bo"], f32)
    ascale = np.asarray(inputs["L0_attn_scale"], f32)
    gw = np.asarray(inputs["L0_ffn_gate_w"], f32); gb_ = np.asarray(inputs["L0_ffn_gate_b"], f32)
    w1 = np.asarray(inputs["L0_ffn_w1"], f32); b1 = np.asarray(inputs["L0_ffn_b1"], f32)
    w2 = np.asarray(inputs["L0_ffn_w2"], f32); b2 = np.asarray(inputs["L0_ffn_b2"], f32)
    wmod_ = np.asarray(inputs["L0_wmod"], f32)

    C = np.ascontiguousarray

    def fmvec(v):
        return C(v.reshape(8, 128).T)

    shared = {
        "wqT": C(Wq.T), "wkT": C(Wk.T), "wvT": C(Wv.T), "wuT": C(Wu.T),
        "woT": C(Wo.T), "wmod": C(wmod_),
        "gwT": C(gw.T.reshape(8, 128, 8).transpose(1, 0, 2).reshape(128, 64)),
        "bq_fm": fmvec(bq), "bk_fm": fmvec(bk), "bu_fm": fmvec(bu),
        "bk_row": bk.reshape(1, D), "bv_row": bv.reshape(1, D),
        "b2_row": None, "bo_row": bo.reshape(1, D),
        "gb_col": gb_.reshape(8, 1),
        "ascale_fm": fmvec(ascale),
    }
    in_maps = []
    for c in range(NCORES):
        xs = x[2 * c:2 * c + 2].reshape(TOK, D)
        m = dict(shared)
        m["x_tm"] = C(xs)
        m["x_fm"] = C(xs.T)
        cc_ = cvec[2 * c:2 * c + 2]
        m["cT"] = C(cc_.T.reshape(8, 128, 2).transpose(1, 0, 2).reshape(128, 16))
        m["w1e"] = C(w1[c]); m["w2e"] = C(w2[c])
        m["b1_fm"] = C(b1[c].reshape(32, 128).T)
        m["b2_row"] = b2[c].reshape(1, D)
        cm = np.zeros((8, 1), f32); cm[:c, 0] = 1.0
        m["core_mask"] = cm
        co = np.zeros((8, 1), f32); co[c, 0] = 1.0
        m["core_oh"] = co
        m["slotrow128"] = (float(c * CAP) + np.arange(128, dtype=f32)[:, None] * 40.0)
        m["tokbase"] = np.full((128, 1), float(TOK * c), f32)
        in_maps.append(m)

    nc = _get_nc(debug)
    res = run_bass_kernel_spmd(nc, in_maps, core_ids=list(range(NCORES)))
    outs = np.concatenate([res.results[c]["out"].reshape(2, N, D)
                           for c in range(NCORES)], axis=0)
    return outs, res


# version 04 (final)
# speedup vs baseline: 1.0893x; 1.0186x over previous
import numpy as np
from contextlib import ExitStack

import concourse.bass as bass
import concourse.mybir as mybir
from concourse.bass import IndirectOffsetOnAxis
from concourse import library_config as libcfg
from concourse.bass_utils import run_bass_kernel_spmd

F32 = mybir.dt.float32
BF16 = mybir.dt.bfloat16
U32 = mybir.dt.uint32
I16 = mybir.dt.int16
AF = mybir.ActivationFunctionType
OP = mybir.AluOpType
AX = mybir.AxisListType

NCORES = 8
N = 2048
D = 1024
H = 8
TOK = 2 * N
NCH = TOK // 128
SEG = 512
NSEG_B = N // SEG
FF = 4096
CAP = 5120
FFG = 256
NFFG = CAP // FFG


def _stub_out(k, nc, out, es):
    zt = es.enter_context(nc.sbuf_tensor("stub_z", [128, 512], mybir.dt.float32))
    tz = k.op("gpsimd", lambda e: e.memset(zt[:], 0.0))
    k.wait("sync", tz)
    last = None
    for r in range(TOK // 128):
        for cg in range(2):
            last = k.dma("sync", out[r * 128:(r + 1) * 128, cg * 512:(cg + 1) * 512],
                         zt[:], sem="od")
    k.wait("sync", last)
    k.barrier()


class KB:
    ENGINES = ("sync", "tensor", "vector", "scalar", "gpsimd")

    def __init__(self, nc, es, safe=False):
        self.nc = nc
        self.es = es
        self.safe = safe
        self.items = []
        self.semh = {}
        self.semv = {}
        for s in ("ld", "gd", "od", "cc", "t", "v", "s", "g", "ph"):
            self.semh[s] = es.enter_context(nc.semaphore(s))
            self.semv[s] = 0
        self.block = es.enter_context(nc.Block())

    def on(self, eng):
        def deco(fn):
            self.items.append((eng, fn))
        return deco

    def tok(self, sem):
        return (sem, self.semv[sem])

    def dma(self, eng, out, in_, sem=None, transpose=False, slow=False):
        sem = sem or ("gd" if eng == "gpsimd" else "ld")
        self.semv[sem] += 16
        h, v = self.semh[sem], self.semv[sem]

        @self.on(eng)
        def _(e):
            if transpose:
                e.dma_start_transpose(out, in_).then_inc(h, 16)
            elif slow:
                e.dma_start(out=out, in_=in_,
                            allow_slow_non_contiguous=True).then_inc(h, 16)
            else:
                e.dma_start(out=out, in_=in_).then_inc(h, 16)
        return (sem, v)

    def dma16(self, fn):
        self.semv["gd"] += 16
        h, v = self.semh["gd"], self.semv["gd"]

        @self.on("gpsimd")
        def _(e):
            fn(e).then_inc(h, 16)
        return ("gd", v)

    def op(self, eng, fn, inc=None):
        if inc is None:
            inc = {"tensor": "t", "vector": "v", "scalar": "s", "gpsimd": "g",
                   "sync": "g"}[eng]
        self.semv[inc] += 1
        h, v = self.semh[inc], self.semv[inc]
        safe_wait = (h, v - 1) if self.safe else None

        @self.on(eng)
        def _(e):
            if safe_wait is not None and safe_wait[1] > 0:
                e.wait_ge(safe_wait[0], safe_wait[1])
            fn(e).then_inc(h, 1)
        return (inc, v)

    def eop(self, eng, fn, inc=None):
        incs = {"tensor": "t", "vector": "v", "scalar": "s", "gpsimd": "g",
                "sync": "g"}[eng]
        self.wait(eng, (incs, self.semv[incs]))
        return self.op(eng, fn, inc=inc)

    def raw(self, eng, fn):
        if self.safe:
            self.op(eng, fn)
            return

        @self.on(eng)
        def _(e):
            fn(e)

    DMA_SEMS = ("ld", "gd", "od", "cc")

    def wait(self, eng, *tokens):
        toks = []
        seen = {}
        for t in tokens:
            if t is None or t[1] <= 0:
                continue
            s, v = t
            if s in self.DMA_SEMS:
                v = self.semv[s]
            seen[s] = max(seen.get(s, 0), v)
        toks = list(seen.items())
        if not toks:
            return

        @self.on(eng)
        def _(e):
            for s, v in toks:
                e.wait_ge(self.semh[s], v)

    def cc(self, kind, op, ins, outs):
        self.wait("gpsimd", ("cc", self.semv["cc"]))
        self.semv["cc"] += 1
        h, v = self.semh["cc"], self.semv["cc"]

        @self.on("gpsimd")
        def _(e):
            e.collective_compute(kind, op, replica_groups=[list(range(NCORES))],
                                 ins=ins, outs=outs).then_inc(h, 1)
        return ("cc", v)

    def xwait(self, *engines):
        toks = [self.tok(s) for s in ("t", "v", "s", "g", "ld", "gd")]
        for eng in engines:
            self.wait(eng, *toks)

    def barrier(self):
        self.semv["ph"] += len(self.ENGINES)
        h, v = self.semh["ph"], self.semv["ph"]
        for eng in self.ENGINES:
            @self.on(eng)
            def _(e):
                e.sem_inc(h, 1)
                e.wait_ge(h, v)

    def flush(self):
        items = self.items
        self.items = []
        for eng_name, deco in (("sync", self.block.sync), ("tensor", self.block.tensor),
                               ("vector", self.block.vector), ("scalar", self.block.scalar),
                               ("gpsimd", self.block.gpsimd)):
            sub = [fn for e, fn in items if e == eng_name]
            if not sub:
                continue

            def make(sub):
                def body(engine):
                    for fn in sub:
                        fn(engine)
                return body
            deco(make(sub))


def build_nc(debug=False, phases=4, p0sub=99, safe=False):
    nc = bass.Bass("TRN2", target_bir_lowering=False, debug=False,
                   detect_race_conditions=safe)

    def inp(name, shape, dt=F32):
        return nc.dram_tensor(name, shape, dt, kind="ExternalInput").ap()

    def idram(name, shape, dt):
        return nc.dram_tensor(name, shape, dt, kind="Internal").ap()

    x_tm = inp("x_tm", [TOK, D])
    x_fm = inp("x_fm", [D, TOK])
    cT = inp("cT", [128, 16])
    wqT = inp("wqT", [D, D]); wkT = inp("wkT", [D, D])
    wvT = inp("wvT", [D, D]); wuT = inp("wuT", [D, D])
    woT = inp("woT", [D, D])
    wmod = inp("wmod", [D, 6 * D])
    gwT = inp("gwT", [128, 64])
    w1e = inp("w1e", [D, FF]); w2e = inp("w2e", [FF, D])
    bq_fm = inp("bq_fm", [128, 8]); bk_fm = inp("bk_fm", [128, 8])
    bu_fm = inp("bu_fm", [128, 8])
    b1_fm = inp("b1_fm", [128, 32])
    bk_row = inp("bk_row", [1, D]); bv_row = inp("bv_row", [1, D])
    b2_row = inp("b2_row", [1, D]); bo_row = inp("bo_row", [1, D])
    gb_col = inp("gb_col", [8, 1])
    ascale_fm = inp("ascale_fm", [128, 8])
    core_mask = inp("core_mask", [8, 1])
    core_oh = inp("core_oh", [8, 1])
    slotrow128 = inp("slotrow128", [128, 1])
    tokbase = inp("tokbase", [128, 1])

    wqT_b = idram("wqT_b", [D, D], BF16); wkT_b = idram("wkT_b", [D, D], BF16)
    wvT_b = idram("wvT_b", [D, D], BF16); wuT_b = idram("wuT_b", [D, D], BF16)
    woT_b = idram("woT_b", [D, D], BF16)
    gwT_b = idram("gwT_b", [128, 64], BF16)
    w1_b = idram("w1_b", [D, FF], BF16); w2_b = idram("w2_b", [FF, D], BF16)
    xfm_b = idram("xfm_b", [D, TOK], BF16)
    x2_dram = idram("x2_dram", [TOK, D], F32)
    x2_btm = idram("x2_btm", [TOK, D], BF16)
    h2fm = idram("h2fm", [D, TOK], BF16)
    h2_bounce = idram("h2_bounce", [TOK, D], BF16)
    h2_all = idram("h2_all", [NCORES * TOK, D], BF16)
    SLOTS = NCORES * CAP + 128
    slotmap_in = idram("slotmap_in", [SLOTS, 4], F32)
    slotmap_out = idram("slotmap_out", [SLOTS, 4], F32)
    cnt_in = idram("cnt_in", [128, 8], F32)
    cnt_out = idram("cnt_out", [1024, 8], F32)
    eout_bounce = idram("eout_bounce", [CAP, D], BF16)
    modrow_dram = idram("modrow_dram", [16, D], F32)
    ffn_tm = [idram(f"ffn_tm{i}", [FFG, D], BF16) for i in range(2)]
    eout_all = idram("eout_all", [NCORES * CAP, D], BF16)
    out = nc.dram_tensor("out", [TOK, D], F32, kind="ExternalOutput").ap()
    dbg = {}
    if debug:
        for nm, shape, dt in (("d_x2", [TOK, D], F32), ("d_h2", [TOK, D], BF16),
                              ("d_gate", [128, NCH], F32), ("d_idxg", [128, NCH], F32),
                              ("d_eout", [CAP, D], BF16), ("d_cnt", [8, 8], F32),
                              ("d_slotmap", [NCORES * CAP + 128, 4], F32),
                              ("d_oraw", [D, TOK], BF16), ("d_frow", [1, TOK], F32),
                              ("d_qt", [D, TOK], BF16), ("d_kt", [D, TOK], BF16),
                              ("d_vtm", [TOK, D], BF16), ("d_ut", [D, TOK], BF16),
                              ("d_og", [D, TOK], BF16), ("d_wto", [D, D], BF16),
                              ("d_gabo", [128, D], F32), ("d_mx", [128, NCH], F32),
                              ("d_idxf", [128, NCH], F32), ("d_possel", [128, NCH], F32),
                              ("d_keep", [128, NCH], F32), ("d_tpr", [128, NCH], F32),
                              ("d_cum", [128, NCH * 8], F32), ("d_comb", [128, NCH * 8], F32),
                              ("d_scu", [128, NCH], U32), ("d_xh", [D, TOK], BF16),
                              ("d_ab", [2, TOK], F32), ("d_cmat", [8, 8], F32),
                              ("d_tt32", [32, 8], F32), ("d_ttot", [1, NCH * 8], F32),
                              ("d_mv", [128, 64], F32), ("d_scdbg", [128, NCH * 4], F32),
                              ("d_ffng0", [FFG, D], BF16), ("d_slot4", [128, CAP // 128 * 4], F32),
                              ("d_logits", [128, NCH * 8], F32),
                              ("d_csil", [128, 16], F32),
                              ("d_modsb", [2, 6 * D], F32)):
            dbg[nm] = nc.dram_tensor(nm, shape, dt, kind="ExternalOutput").ap()

    es = ExitStack()
    k = KB(nc, es, safe=safe)
    sbg = lambda name, shape, dt: es.enter_context(nc.sbuf_tensor(name, shape, dt))
    bank = [es.enter_context(nc.psum_tensor(f"bank{i}", [128, 512], F32))
            for i in range(8)]

    ones_r_f = sbg("ones_r_f", [1, 128], F32)
    ones_s_b = sbg("ones_s_b", [128, 1], BF16)
    zeros_b = sbg("zeros_b", [128, 512], BF16)
    triu_f = sbg("triu_f", [128, 128], F32)
    triu_b = sbg("triu_b", [128, 128], BF16)
    su32_f = sbg("su32_f", [32, 32], F32)
    ident_f = sbg("ident_f", [128, 128], F32)
    ident_b = sbg("ident_b", [128, 128], BF16)
    ones32_f = sbg("ones32_f", [32, 1], F32)
    ones_1x1 = sbg("ones_1x1", [1, 1], F32)
    bq_s = sbg("bq_s", [128, 8], F32); bk_s = sbg("bk_s", [128, 8], F32)
    bu_s = sbg("bu_s", [128, 8], F32); b1_s = sbg("b1_s", [128, 32], F32)
    asc_s = sbg("asc_s", [128, 8], F32)
    gbc_s = sbg("gbc_s", [8, 1], F32)
    cmask_s = sbg("cmask_s", [8, 1], F32); coh_s = sbg("coh_s", [8, 1], F32)
    onepca = [sbg(f"onepca{b}", [128, 8], F32) for b in range(2)]
    sa_s = [sbg(f"sa_s{b}", [128, 8], F32) for b in range(2)]
    onepcb = [sbg(f"onepcb{b}", [128, 8], F32) for b in range(2)]
    sbv_s = [sbg(f"sbv_s{b}", [128, 8], F32) for b in range(2)]
    logits_tm = sbg("logits_tm", [128, NCH, 8], F32)
    gate_s = sbg("gate_s", [128, NCH], F32)
    idxg_s = sbg("idxg_s", [128, NCH], U32)

    t_wq = k.dma("gpsimd", wqT_b, wqT); t_wk = k.dma("gpsimd", wkT_b, wkT)
    t_wv = k.dma("gpsimd", wvT_b, wvT); t_wu = k.dma("gpsimd", wuT_b, wuT)
    t_wo = k.dma("gpsimd", woT_b, woT)
    t_gw = k.dma("gpsimd", gwT_b, gwT)
    t_xf = k.dma("gpsimd", xfm_b, x_fm)
    t_w1 = k.dma("gpsimd", w1_b, w1e)
    t_w2 = k.dma("gpsimd", w2_b, w2e)

    if p0sub < 1:
        _stub_out(k, nc, out, es)
        k.flush()
        es.close()
        return nc
    t_small = None
    for dst, src in ((bq_s[:], bq_fm), (bk_s[:], bk_fm), (bu_s[:], bu_fm),
                     (b1_s[:], b1_fm), (asc_s[:], ascale_fm), (gbc_s[:], gb_col),
                     (cmask_s[:], core_mask), (coh_s[:], core_oh)):
        t_small = k.dma("sync", dst, src)

    for t_ in (ones_r_f, ones_s_b, zeros_b, ones32_f, ones_1x1):
        k.op("gpsimd", lambda e, t_=t_: e.memset(t_[:], 1.0 if t_ is not zeros_b else 0.0))
    tg0 = k.op("gpsimd", lambda e: e.memset(triu_f[:], 1.0))
    k.wait("gpsimd", tg0)
    k.op("gpsimd", lambda e: e.affine_select(triu_f[:], triu_f[:], [[1, 128]],
                                             compare_op=OP.is_ge, fill=0.0, base=0,
                                             channel_multiplier=-1))
    tg0 = k.op("gpsimd", lambda e: e.memset(triu_b[:], 1.0))
    k.wait("gpsimd", tg0)
    k.op("gpsimd", lambda e: e.affine_select(triu_b[:], triu_b[:], [[1, 128]],
                                             compare_op=OP.is_ge, fill=0.0, base=0,
                                             channel_multiplier=-1))
    tg0 = k.op("gpsimd", lambda e: e.memset(su32_f[:], 1.0))
    k.wait("gpsimd", tg0)
    k.op("gpsimd", lambda e: e.affine_select(su32_f[:], su32_f[:], [[1, 32]],
                                             compare_op=OP.is_ge, fill=0.0, base=-1,
                                             channel_multiplier=-1))
    tg0 = k.op("gpsimd", lambda e: e.memset(ident_f[:], 1.0))
    k.wait("gpsimd", tg0)
    k.op("gpsimd", lambda e: e.affine_select(ident_f[:], ident_f[:], [[1, 128]],
                                             compare_op=OP.is_equal, fill=0.0, base=0,
                                             channel_multiplier=-1))
    tg0 = k.op("gpsimd", lambda e: e.memset(ident_b[:], 1.0))
    k.wait("gpsimd", tg0)
    t_const = k.op("gpsimd", lambda e: e.affine_select(ident_b[:], ident_b[:], [[1, 128]],
                                                       compare_op=OP.is_equal, fill=0.0,
                                                       base=0, channel_multiplier=-1))

    if p0sub < 2:
        _stub_out(k, nc, out, es)
        k.flush()
        es.close()
        return nc
    rowsrc = sbg("rowsrc", [1, D], F32)

    def replicate_row(row_ap, dst_sb, wait_toks, bank_i):
        last = None
        for g in range(D // 512):
            dslice = dst_sb[:, g * 512:(g + 1) * 512]
            rslice = row_ap[:, g * 512:(g + 1) * 512]
            k.wait("tensor", *wait_toks, t_const)
            tmm = k.op("tensor", lambda e, rs=rslice, bi=bank_i: e.matmul(
                bank[bi][:, 0:512], ones_r_f[:], rs, start=True, stop=True))
            k.wait("vector", tmm)
            last = k.op("vector", lambda e, ds=dslice, bi=bank_i: e.tensor_copy(
                ds, bank[bi][:, 0:512]))
            k.wait("tensor", last)
        return last

    if p0sub < 3:
        _stub_out(k, nc, out, es)
        k.flush()
        es.close()
        return nc
    csil = sbg("csil", [128, 16], F32)
    cT_s = sbg("cT_s", [128, 16], F32)
    t_c = k.dma("sync", cT_s[:], cT)
    k.wait("scalar", t_c)
    t_cs = k.op("scalar", lambda e: e.activation(csil[:], cT_s[:], AF.Silu))

    if p0sub < 4:
        _stub_out(k, nc, out, es)
        k.flush()
        es.close()
        return nc
    with ExitStack() as p0es:
        wmt = [p0es.enter_context(nc.sbuf_tensor(f"wmt{i}", [128, 6 * D], F32))
               for i in range(2)]
        mod_sb = p0es.enter_context(nc.sbuf_tensor("mod_sb", [2, 6 * D], F32))
        mrow_sb = p0es.enter_context(nc.sbuf_tensor("mrow_sb", [1, 6 * D], F32))
        rowtmp = p0es.enter_context(nc.sbuf_tensor("rowtmp", [1, D], F32))
        rowtmp2 = p0es.enter_context(nc.sbuf_tensor("rowtmp2", [1, D], F32))
        browtmp = p0es.enter_context(nc.sbuf_tensor("browtmp", [1, D], F32))

        k.barrier()
        t_pe = [None, None]
        for half in range(2):
            t_wml = [None, None]
            tmm6 = [None] * 6
            for dc in range(8):
                k.wait("sync", t_wml[dc % 2], t_pe[dc % 2])
                t_wml[dc % 2] = k.dma("sync", wmt[dc % 2][:],
                                      wmod[dc * 128:(dc + 1) * 128, :])
                k.wait("tensor", t_wml[dc % 2], t_cs, t_const)
                lhs = csil[:, dc * 2:dc * 2 + 2]
                for mg6 in range(6):
                    mg = half * 6 + mg6
                    rhs = wmt[dc % 2][:, mg * 512:(mg + 1) * 512]
                    fn = lambda e, lhs=lhs, rhs=rhs, dc=dc, mg6=mg6: e.matmul(
                        bank[mg6][0:2, 0:512], lhs, rhs,
                        start=(dc == 0), stop=(dc == 7))
                    tmm6[mg6] = k.op("tensor", fn)
                    if mg6 == 5:
                        t_pe[dc % 2] = tmm6[5]
            tv = None
            for mg6 in range(6):
                mg = half * 6 + mg6
                k.wait("vector", tmm6[mg6])
                ds = mod_sb[:, mg * 512:(mg + 1) * 512]
                tv = k.eop("vector", lambda e, ds=ds, mg6=mg6: e.tensor_copy(
                    ds, bank[mg6][0:2, 0:512]))
            k.wait("tensor", tv)
        if debug:
            k.wait("sync", tv, t_cs)
            k.dma("sync", dbg["d_modsb"][:, half * 3 * D:(half + 1) * 3 * D],
                  mod_sb[:, half * 3 * D:(half + 1) * 3 * D], sem="od")
            k.dma("sync", dbg["d_csil"], csil[:], sem="od")
        if p0sub < 5:
            _stub_out(k, nc, out, es)
            k.flush()
            es.close()
            return nc

        def clamp_row(src_ap, dst_ap):
            k.wait("vector", k.tok("t"), k.tok("s"), k.tok("ld"))
            k.wait("scalar", k.tok("v"), k.tok("t"), k.tok("ld"))
            t0 = k.op("scalar", lambda e, s=src_ap: e.activation(rowtmp2[:], s, AF.Abs))
            k.wait("vector", t0)
            t1 = k.eop("vector", lambda e: e.tensor_scalar_max(rowtmp2[:], rowtmp2[:], 0.25))
            t2 = k.op("scalar", lambda e, s=src_ap: e.sign(browtmp[:], s))
            k.wait("vector", t1, t2)
            return k.eop("vector", lambda e, d=dst_ap: e.tensor_mul(d, rowtmp2[:], browtmp[:]))

        for b in range(2):
            k.xwait("sync")
            tmr = k.dma("sync", mrow_sb[:], mod_sb[b:b + 1, :])
            k.wait("vector", tmr)
            k.wait("scalar", tmr)
            mrow = mrow_sb[0:1, :]
            for jj, (vec_i, dst, add1) in enumerate((
                    (0, sa_s[b][:], False), (3, sbv_s[b][:], False),
                    (1, onepca[b][:], True), (4, onepcb[b][:], True))):
                k.wait("tensor", tmr, k.tok("v"), t_const)
                tcol = None
                for dc in range(8):
                    sl = mrow[:, vec_i * D + dc * 128: vec_i * D + (dc + 1) * 128]
                    tcol = k.op("tensor", lambda e, sl=sl, dc=dc: e.matmul(
                        bank[7][:, dc:dc + 1], sl, ones_1x1[:],
                        start=True, stop=True, skip_group_check=True))
                k.wait("vector", tcol)
                if add1:
                    tvc = k.eop("vector", lambda e, dst=dst: e.tensor_scalar_add(
                        dst, bank[7][:, 0:8], 1.0))
                else:
                    tvc = k.eop("vector", lambda e, dst=dst: e.tensor_copy(
                        dst, bank[7][:, 0:8]))
                k.wait("tensor", tvc)
            if p0sub < 6:
                break
            tga = clamp_row(mrow[:, 2 * D:3 * D], rowtmp[:])
            k.wait("sync", tga)
            k.dma("sync", modrow_dram[8 + b:9 + b, :], rowtmp[:])
            k.wait("sync", tga)
            t_bo = k.dma("sync", browtmp[:], bo_row)
            k.wait("vector", t_bo, tga)
            tgabo = k.eop("vector", lambda e: e.tensor_mul(rowtmp2[:], rowtmp[:], browtmp[:]))
            k.wait("sync", tgabo)
            k.dma("sync", modrow_dram[10 + b:11 + b, :], rowtmp2[:])
            tgb = clamp_row(mrow[:, 5 * D:6 * D], rowtmp[:])
            k.wait("sync", tgb)
            k.dma("sync", modrow_dram[12 + b:13 + b, :], rowtmp[:])

        if p0sub < 7:
            _stub_out(k, nc, out, es)
            k.flush()
            es.close()
            return nc
        k.barrier()
        k.flush()

    if debug:
        k.xwait("sync")
        for i_, t_ in enumerate((onepca[0], sa_s[0], onepcb[0], sbv_s[0],
                                 onepca[1], sa_s[1], onepcb[1], sbv_s[1])):
            k.dma("sync", dbg["d_mv"][:, i_ * 8:(i_ + 1) * 8], t_[:], sem="od")

    if phases < 1:
        _stub_out(k, nc, out, es)
        k.flush()
        es.close()
        return nc
    with ExitStack() as p1:
        sb1 = lambda name, shape, dt: p1.enter_context(nc.sbuf_tensor(name, shape, dt))
        wt_q = [sb1(f"wt_q{i}", [128, D], BF16) for i in range(8)]
        wt_k = [sb1(f"wt_k{i}", [128, D], BF16) for i in range(8)]
        wt_v = [sb1(f"wt_v{i}", [128, D], BF16) for i in range(8)]
        wt_u = [sb1(f"wt_u{i}", [128, D], BF16) for i in range(8)]
        wt_o = [sb1(f"wt_o{i}", [128, D], BF16) for i in range(8)]
        gw_s = sb1("gw_s", [128, 64], BF16)
        xhat = [sb1(f"xhat{i}", [128, SEG], BF16) for i in range(8)]
        xsq2 = [sb1(f"xsq{i}", [128, SEG], BF16) for i in range(2)]
        qt = [sb1(f"qt{i}", [128, SEG], BF16) for i in range(8)]
        kt = [sb1(f"kt{i}", [128, SEG], BF16) for i in range(8)]
        ut = [sb1(f"ut{i}", [128, SEG], BF16) for i in range(8)]
        ktm = [sb1(f"ktm{i}", [128, D], BF16) for i in range(4)]
        vtm = [sb1(f"vtm{i}", [128, D], BF16) for i in range(4)]
        ofm = [sb1(f"ofm{i}", [128, SEG], BF16) for i in range(8)]
        atm = sb1("atm", [128, 128], BF16)
        ssb = sb1("ssb", [128, 128], BF16)
        e_b1 = sb1("e_b1", [128, SEG], BF16)
        e_b2 = sb1("e_b2", [128, SEG], BF16)
        e_f3 = sb1("e_f3", [128, SEG], F32)
        srow = [sb1(f"srow{i}", [1, SEG], F32) for i in range(3)]
        xf32 = [sb1(f"xf32{i}", [128, D], F32) for i in range(2)]
        x2st = [sb1(f"x2st{i}", [128, D], F32) for i in range(2)]
        h2tmst = [sb1(f"h2tmst{i}", [128, D], BF16) for i in range(2)]
        lgs = sb1("lgs", [8, SEG], F32)
        bkrep = sb1("bkrep", [128, D], F32)
        bvrep = sb1("bvrep", [128, D], F32)
        garep_t = sb1("garep_t", [128, D], F32)
        gaborep_t = sb1("gaborep_t", [128, D], F32)

        t_ld = k.dma("sync", rowsrc[:], bk_row)
        t_r = replicate_row(rowsrc[:], bkrep[:], [t_ld], 0)
        k.wait("sync", t_r)
        t_ld = k.dma("sync", rowsrc[:], bv_row)
        t_r = replicate_row(rowsrc[:], bvrep[:], [t_ld], 0)
        k.wait("sync", t_r)

        k.wait("sync", t_wq, t_wk, t_wv, t_wu, t_gw)
        t_wload = None
        for i in range(8):
            t_wload = k.dma("sync", wt_q[i][:], wqT_b[i * 128:(i + 1) * 128, :])
            t_wload = k.dma("sync", wt_k[i][:], wkT_b[i * 128:(i + 1) * 128, :])
            t_wload = k.dma("sync", wt_v[i][:], wvT_b[i * 128:(i + 1) * 128, :])
            t_wload = k.dma("sync", wt_u[i][:], wuT_b[i * 128:(i + 1) * 128, :])
        t_gwl = k.dma("sync", gw_s[:], gwT_b)

        t_x2w = [None, None]
        t_xl_prev = [None, None]
        t_h2w = [None, None]
        t_h2bounce_all = []

        def ln_stats_and_norm(get_xin_tok, onep_ap_of, shift_ap_of, b):
            prev_pair = [None, None]
            tmm = None
            for dc in range(8):
                xb = xsq2[dc % 2]
                k.wait("scalar", get_xin_tok[dc], prev_pair[dc % 2])
                tsq = k.op("scalar", lambda e, dc=dc, xb=xb: e.activation(
                    xb[:], xhat[dc][:], AF.Square))
                k.wait("tensor", get_xin_tok[dc], tsq, t_small, t_const)
                k.raw("tensor", lambda e, dc=dc: e.matmul(
                    bank[0][0:1, 0:SEG], ones_s_b[:], xhat[dc][:],
                    start=(dc == 0), stop=(dc == 7)))
                tmm = k.op("tensor", lambda e, dc=dc, xb=xb: e.matmul(
                    bank[1][0:1, 0:SEG], ones_s_b[:], xb[:],
                    start=(dc == 0), stop=(dc == 7)))
                prev_pair[dc % 2] = tmm
            k.wait("vector", tmm)
            k.eop("vector", lambda e: e.tensor_scalar_mul(srow[0][:], bank[0][0:1, 0:SEG], 1.0 / D))
            k.eop("vector", lambda e: e.tensor_scalar_mul(srow[1][:], bank[1][0:1, 0:SEG], 1.0 / D))
            k.eop("vector", lambda e: e.tensor_mul(srow[2][:], srow[0][:], srow[0][:]))
            k.eop("vector", lambda e: e.tensor_sub(
                srow[1][:], srow[1][:], srow[2][:]))
            tvv = k.eop("vector", lambda e: e.tensor_scalar_add(srow[1][:], srow[1][:], 1e-5))
            k.wait("scalar", tvv)
            k.eop("scalar", lambda e: e.activation(srow[1][:], srow[1][:], AF.Ln))
            tse = k.eop("scalar", lambda e: e.activation(srow[1][:], srow[1][:], AF.Exp,
                                                         scale=-0.5))
            k.wait("vector", tse)
            tvb = k.eop("vector", lambda e: e.scalar_tensor_tensor(
                srow[2][:], srow[0][:], -1.0, srow[1][:], op0=OP.mult, op1=OP.mult))
            k.wait("tensor", tse, tvb, tmm)
            tra = k.op("tensor", lambda e: e.matmul(bank[2][:, 0:SEG], ones_r_f[:],
                                                    srow[1][:], start=True, stop=True))
            trb = k.op("tensor", lambda e: e.matmul(bank[3][:, 0:SEG], ones_r_f[:],
                                                    srow[2][:], start=True, stop=True))
            txh = None
            k.wait("vector", tra, trb)
            for dc in range(8):
                onep_ap = onep_ap_of(dc)
                shift_ap = shift_ap_of(dc)
                k.op("vector", lambda e, dc=dc: e.tensor_mul(
                    e_f3[:], xhat[dc][:], bank[2][:, 0:SEG]))
                k.op("vector", lambda e, dc=dc: e.tensor_add(
                    e_f3[:], e_f3[:], bank[3][:, 0:SEG]))
                txh = k.op("vector", lambda e, dc=dc, o=onep_ap, s=shift_ap: e.tensor_scalar(
                    xhat[dc][:], e_f3[:], o, s, op0=OP.mult, op1=OP.add))
            k.wait("tensor", txh)
            return txh

        for b in range(2):
            k.xwait("sync")
            t_ld = k.dma("sync", rowsrc[:], modrow_dram[8 + b:9 + b, :])
            t_r = replicate_row(rowsrc[:], garep_t[:], [t_ld], 0)
            k.wait("sync", t_r)
            t_ld = k.dma("sync", rowsrc[:], modrow_dram[10 + b:11 + b, :])
            t_r = replicate_row(rowsrc[:], gaborep_t[:], [t_ld], 0)
            t_wol = []
            k.wait("sync", t_wo)
            for i in range(8):
                t_wol.append(k.dma("sync", wt_o[i][:], woT_b[i * 128:(i + 1) * 128, :]))
            two = None
            garep_b = garep_t[:]
            for i in range(8):
                k.wait("vector", t_wol[i], t_r)
                two = k.op("vector", lambda e, i=i, g=garep_b: e.tensor_mul(
                    wt_o[i][:], wt_o[i][:], g))
            k.wait("tensor", two, k.tok("s"), k.tok("v"))
            k.raw("tensor", lambda e: e.matmul(bank[6][:], zeros_b[:, 0:128], zeros_b[:],
                                               start=True, stop=True))
            tz = k.op("tensor", lambda e: e.matmul(bank[7][:], zeros_b[:, 0:128], zeros_b[:],
                                                   start=True, stop=True))

            if debug and b == 0:
                k.xwait("sync")
                for i in range(8):
                    k.dma("sync", dbg["d_wto"][i * 128:(i + 1) * 128, :], wt_o[i][:],
                          sem="od")
                k.dma("sync", dbg["d_gabo"], gaborep_t[:], sem="od")
            prev_oevac = None
            prev_s_upd = [tz] * H

            for seg in range(NSEG_B):
                t0 = b * N + seg * SEG

                tx = []
                k.xwait("sync")
                k.wait("sync", t_xf)
                for dc in range(8):
                    tx.append(k.dma("sync", xhat[dc][:],
                                    xfm_b[dc * 128:(dc + 1) * 128, t0:t0 + SEG]))
                k.xwait("tensor", "vector", "scalar")
                t_xhat = ln_stats_and_norm(
                    tx, lambda dc: onepca[b][:, dc:dc + 1], lambda dc: sa_s[b][:, dc:dc + 1], b)

                if debug:
                    k.xwait("sync")
                    for dc in range(8):
                        k.dma("sync", dbg["d_xh"][dc * 128:(dc + 1) * 128, t0:t0 + SEG],
                              xhat[dc][:], sem="od")
                    k.dma("sync", dbg["d_ab"][0:1, t0:t0 + SEG], srow[1][:], sem="od")
                    k.dma("sync", dbg["d_ab"][1:2, t0:t0 + SEG], srow[2][:], sem="od")
                def fm_proj(wt, psum_i, hc):
                    k.wait("tensor", t_xhat, t_wload)
                    tret = None
                    for dc in range(8):
                        lhs = wt[dc][:, hc * 128:(hc + 1) * 128]
                        fn = lambda e, lhs=lhs, dc=dc, pi=psum_i: e.matmul(
                            bank[pi][:, 0:SEG], lhs, xhat[dc][:],
                            start=(dc == 0), stop=(dc == 7))
                        if dc < 7:
                            k.raw("tensor", fn)
                        else:
                            tret = k.op("tensor", fn)
                    return tret

                def elu_evac(psum_i, bias_ap, dst_ap):
                    tr_ = k.op("vector", lambda e, pi=psum_i, ba=bias_ap: e.tensor_scalar(
                        e_b1[:], bank[pi][:, 0:SEG], ba, 0.0, op0=OP.add, op1=OP.max))
                    tm_ = k.op("vector", lambda e, pi=psum_i, ba=bias_ap: e.tensor_scalar(
                        e_b2[:], bank[pi][:, 0:SEG], ba, 0.0, op0=OP.add, op1=OP.min))
                    k.wait("scalar", tm_)
                    te_ = k.op("scalar", lambda e: e.activation(e_b2[:], e_b2[:], AF.Exp))
                    k.wait("vector", te_)
                    k.op("vector", lambda e, d=dst_ap: e.tensor_add(d, e_b1[:], e_b2[:]))
                    return k.op("vector", lambda e, d=dst_ap: e.tensor_scalar_add(d, d, -1.0))

                k.xwait("vector", "scalar")
                for dst_list, wt, bias_sb, act in ((qt, wt_q, bq_s, "elu"),
                                                   (kt, wt_k, bk_s, "elu"),
                                                   (ut, wt_u, bu_s, "silu")):
                    for hc in range(8):
                        pi = 2 + (hc % 2)
                        tmm = fm_proj(wt, pi, hc)
                        bias_ap = bias_sb[:, hc:hc + 1]
                        if act == "elu":
                            k.wait("vector", tmm)
                            tev = elu_evac(pi, bias_ap, dst_list[hc][:])
                            k.wait("tensor", tev)
                        else:
                            k.wait("scalar", tmm)
                            tev = k.op("scalar", lambda e, hc=hc, pi=pi, ba=bias_ap,
                                       d=dst_list[hc]: e.activation(
                                d[:], bank[pi][:, 0:SEG], AF.Silu, bias=ba))
                            k.wait("tensor", tev)

                for tc in range(4):
                    for hg in range(2):
                        pi = 4 + hg
                        hsl = slice(hg * 512, (hg + 1) * 512)
                        k.wait("tensor", t_xhat)
                        tkmm = None
                        for dc in range(8):
                            lhs = xhat[dc][:, tc * 128:(tc + 1) * 128]
                            rhs = wt_k[dc][:, hsl]
                            fn = lambda e, lhs=lhs, rhs=rhs, dc=dc, pi=pi: e.matmul(
                                bank[pi][:, 0:512], lhs, rhs, start=(dc == 0), stop=(dc == 7))
                            if dc < 7:
                                k.raw("tensor", fn)
                            else:
                                tkmm = k.op("tensor", fn)
                        k.wait("vector", tkmm)
                        tv1 = k.op("vector", lambda e, pi=pi, hsl=hsl: e.tensor_add(
                            e_f3[:], bank[pi][:, 0:512], bkrep[:, hsl]))
                        tv2 = k.op("vector", lambda e: e.tensor_scalar_min(
                            e_b2[:], e_f3[:], 0.0))
                        k.wait("scalar", tv2)
                        te = k.op("scalar", lambda e: e.activation(e_b2[:], e_b2[:], AF.Exp))
                        tv3 = k.op("vector", lambda e: e.tensor_scalar_max(
                            e_b1[:], e_f3[:], 0.0))
                        k.wait("vector", te)
                        kdst = ktm[tc][:, hsl]
                        k.op("vector", lambda e, d=kdst: e.tensor_add(d, e_b1[:], e_b2[:]))
                        tv5 = k.op("vector", lambda e, d=kdst: e.tensor_scalar_add(d, d, -1.0))
                        k.wait("tensor", tv1, tv3)
                        tvmm = None
                        for dc in range(8):
                            lhs = xhat[dc][:, tc * 128:(tc + 1) * 128]
                            rhs = wt_v[dc][:, hsl]
                            fn = lambda e, lhs=lhs, rhs=rhs, dc=dc, pi=pi: e.matmul(
                                bank[pi][:, 0:512], lhs, rhs, start=(dc == 0), stop=(dc == 7))
                            if dc < 7:
                                k.raw("tensor", fn)
                            else:
                                tvmm = k.op("tensor", fn)
                        k.wait("vector", tvmm)
                        tv6 = k.op("vector", lambda e, pi=pi, hsl=hsl: e.tensor_add(
                            e_f3[:], bank[pi][:, 0:512], bvrep[:, hsl]))
                        k.wait("scalar", tv6)
                        vdst = vtm[tc][:, hsl]
                        ts7 = k.op("scalar", lambda e, d=vdst: e.activation(
                            d, e_f3[:], AF.Silu))
                        k.wait("tensor", tv6, ts7)

                k.xwait("tensor", "vector", "scalar")
                for h in range(H):
                    sview = bank[6 + h // 4][:, (h % 4) * 128:(h % 4) * 128 + 128]
                    for cc in range(4):
                        gc = seg * 4 + cc
                        csl = slice(cc * 128, cc * 128 + 128)
                        k.wait("tensor", prev_oevac)
                        tat = k.op("tensor", lambda e, h=h, csl=csl: e.matmul(
                            bank[0][:, 0:128], kt[h][:, csl], qt[h][:, csl],
                            start=True, stop=True))
                        k.wait("vector", tat)
                        tatm = k.op("vector", lambda e: e.tensor_mul(
                            atm[:], bank[0][:, 0:128], triu_f[:]))
                        tsc = None
                        if gc > 0:
                            k.wait("vector", prev_s_upd[h])
                            tsc = k.op("vector", lambda e, sview=sview: e.tensor_copy(
                                ssb[:], sview))
                        k.wait("tensor", tatm)
                        fn_intra = lambda e, cc=cc, h=h, gc=gc: e.matmul(
                            bank[1][:, 0:128], vtm[cc][:, h * 128:(h + 1) * 128], atm[:],
                            start=True, stop=(gc == 0))
                        if gc > 0:
                            k.raw("tensor", fn_intra)
                            k.wait("tensor", tsc)
                            t_o = k.op("tensor", lambda e, h=h, csl=csl: e.matmul(
                                bank[1][:, 0:128], ssb[:], qt[h][:, csl],
                                start=False, stop=True))
                        else:
                            t_o = k.op("tensor", fn_intra)
                        if tsc is not None:
                            k.wait("tensor", tsc)
                        t_s = k.op("tensor", lambda e, cc=cc, h=h, sview=sview: e.matmul(
                            sview, ktm[cc][:, h * 128:(h + 1) * 128],
                            vtm[cc][:, h * 128:(h + 1) * 128], start=False, stop=False,
                            skip_group_check=True))
                        prev_s_upd[h] = t_s
                        k.wait("scalar", t_o)
                        prev_oevac = k.op("scalar", lambda e, h=h, csl=csl: e.copy(
                            ofm[h][:, csl], bank[1][:, 0:128]))

                if debug:
                    k.xwait("sync")
                    for hc in range(8):
                        k.dma("sync", dbg["d_oraw"][hc * 128:(hc + 1) * 128, t0:t0 + SEG],
                              ofm[hc][:], sem="od")
                        k.dma("sync", dbg["d_ut"][hc * 128:(hc + 1) * 128, t0:t0 + SEG],
                              ut[hc][:], sem="od")
                        k.dma("sync", dbg["d_qt"][hc * 128:(hc + 1) * 128, t0:t0 + SEG],
                              qt[hc][:], sem="od")
                        k.dma("sync", dbg["d_kt"][hc * 128:(hc + 1) * 128, t0:t0 + SEG],
                              kt[hc][:], sem="od")
                    for tc in range(4):
                        k.dma("sync", dbg["d_vtm"][t0 + tc * 128:t0 + (tc + 1) * 128, :],
                              vtm[tc][:], sem="od")
                k.xwait("tensor", "vector", "scalar")
                k.wait("scalar", prev_oevac)
                prev_pair_r = [None, None]
                tmr = None
                for hc in range(8):
                    xb = xsq2[hc % 2]
                    k.wait("scalar", prev_pair_r[hc % 2])
                    tsq = k.op("scalar", lambda e, hc=hc, xb=xb: e.activation(
                        xb[:], ofm[hc][:], AF.Square))
                    k.wait("tensor", tsq, prev_oevac)
                    tmr = k.op("tensor", lambda e, hc=hc, xb=xb: e.matmul(
                        bank[4][0:1, 0:SEG], ones_s_b[:], xb[:],
                        start=(hc == 0), stop=(hc == 7)))
                    prev_pair_r[hc % 2] = tmr
                k.wait("vector", tmr)
                tvr = k.eop("vector", lambda e: e.tensor_scalar_mul(
                    srow[0][:], bank[4][0:1, 0:SEG], 1.0 / D))
                k.wait("scalar", tvr)
                k.eop("scalar", lambda e: e.activation(srow[0][:], srow[0][:], AF.Ln))
                tse = k.eop("scalar", lambda e: e.activation(srow[0][:], srow[0][:], AF.Exp,
                                                             scale=0.5))
                k.wait("vector", tse)
                tva = k.eop("vector", lambda e: e.tensor_scalar_add(srow[0][:], srow[0][:], 1e-8))
                k.wait("scalar", tva)
                k.eop("scalar", lambda e: e.activation(srow[0][:], srow[0][:], AF.Ln))
                tse2 = k.eop("scalar", lambda e: e.activation(srow[0][:], srow[0][:], AF.Exp,
                                                              scale=-1.0))
                if debug:
                    k.wait("sync", tse2)
                    k.dma("sync", dbg["d_frow"][:, t0:t0 + SEG], srow[0][:], sem="od")
                k.wait("tensor", tse2, tmr)
                trf = k.op("tensor", lambda e: e.matmul(bank[5][:, 0:SEG], ones_r_f[:],
                                                        srow[0][:], start=True, stop=True))
                tog = None
                k.wait("vector", trf)
                for hc in range(8):
                    k.op("vector", lambda e, hc=hc: e.tensor_mul(
                        e_f3[:], ofm[hc][:], bank[5][:, 0:SEG]))
                    tog = k.op("vector", lambda e, hc=hc: e.scalar_tensor_tensor(
                        ofm[hc][:], e_f3[:], asc_s[:, hc:hc + 1], ut[hc][:],
                        op0=OP.mult, op1=OP.mult))
                if debug:
                    k.xwait("sync")
                    for hc in range(8):
                        k.dma("sync", dbg["d_og"][hc * 128:(hc + 1) * 128, t0:t0 + SEG],
                              ofm[hc][:], sem="od")
                gaborep_b = gaborep_t[:]
                prev_eread = [None, None]
                for tc in range(4):
                    tcur = t0 + tc * 128
                    xbuf = xf32[tc % 2]
                    x2buf = x2st[tc % 2]
                    k.wait("sync", t_xl_prev[tc % 2])
                    txl = k.dma("sync", xbuf[:], x_tm[tcur:tcur + 128, :])
                    tx2 = None
                    for eg in range(2):
                        pi = 2 + eg
                        esl = slice(eg * 512, (eg + 1) * 512)
                        k.wait("tensor", tog, t_x2w[tc % 2], prev_eread[eg])
                        tpo = None
                        for hc in range(8):
                            lhs = ofm[hc][:, tc * 128:(tc + 1) * 128]
                            rhs = wt_o[hc][:, esl]
                            fn = lambda e, lhs=lhs, rhs=rhs, hc=hc, pi=pi: e.matmul(
                                bank[pi][:, 0:512], lhs, rhs, start=(hc == 0), stop=(hc == 7))
                            if hc < 7:
                                k.raw("tensor", fn)
                            else:
                                tpo = k.op("tensor", fn)
                        k.wait("vector", tpo, txl)
                        k.op("vector", lambda e, pi=pi, esl=esl, g=gaborep_b: e.tensor_add(
                            e_f3[:], bank[pi][:, 0:512], g[:, esl]))
                        tx2 = k.op("vector", lambda e, x2buf=x2buf, esl=esl, xbuf=xbuf:
                                   e.tensor_add(x2buf[:, esl], e_f3[:], xbuf[:, esl]))
                        prev_eread[eg] = tx2
                    k.wait("sync", tx2)
                    t1_ = k.dma("sync", x2_dram[tcur:tcur + 128, :], x2buf[:])
                    k.wait("gpsimd", tx2)
                    t2_ = k.dma("gpsimd", x2_btm[tcur:tcur + 128, :], x2buf[:])
                    t_x2w[tc % 2] = t2_
                    t_xl_prev[tc % 2] = tx2
                    k.wait("vector", t1_)
                t_x2_seg = (t1_, t2_)

                tx2f = []
                k.xwait("sync")
                k.wait("sync", *t_x2w)
                for dc in range(8):
                    tx2f.append(k.dma("sync", xhat[dc][:],
                                      x2_btm[t0:t0 + SEG, dc * 128:(dc + 1) * 128],
                                      transpose=True))
                k.xwait("tensor", "vector", "scalar")
                t_h2 = ln_stats_and_norm(
                    tx2f, lambda dc: onepcb[b][:, dc:dc + 1], lambda dc: sbv_s[b][:, dc:dc + 1], b)
                th2w = []
                for dc in range(8):
                    k.wait("sync", t_h2)
                    th2w.append(k.dma("sync", h2fm[dc * 128:(dc + 1) * 128, t0:t0 + SEG],
                                      xhat[dc][:]))
                for tc in range(4):
                    tcur = t0 + tc * 128
                    hbuf = h2tmst[tc % 2]
                    k.wait("sync", th2w[-1], t_h2w[tc % 2])
                    tr = k.dma("sync", hbuf[:], h2fm[:, tcur:tcur + 128], transpose=True)
                    k.wait("sync", tr)
                    tb = k.dma("sync", h2_bounce[tcur:tcur + 128, :], hbuf[:])
                    t_h2w[tc % 2] = tb
                    t_h2bounce_all.append(tb)
                k.wait("tensor", t_h2, t_gwl)
                tlm = None
                for dc in range(8):
                    lhs = gw_s[:, dc * 8:dc * 8 + 8]
                    fn = lambda e, lhs=lhs, dc=dc: e.matmul(
                        bank[2][0:8, 0:SEG], lhs, xhat[dc][:],
                        start=(dc == 0), stop=(dc == 7))
                    if dc < 7:
                        k.raw("tensor", fn)
                    else:
                        tlm = k.op("tensor", fn)
                k.wait("scalar", tlm)
                tle = k.op("scalar", lambda e: e.activation(
                    lgs[:], bank[2][0:8, 0:SEG], AF.Identity, bias=gbc_s[:]))
                k.wait("tensor", tle)
                for tc in range(4):
                    ch = (t0 + tc * 128) // 128
                    src = lgs[:, tc * 128:(tc + 1) * 128]
                    ttr = k.op("tensor", lambda e, src=src: e.transpose(
                        bank[3][0:128, 0:8], src, ident_f[0:8, 0:8]))
                    k.wait("vector", ttr)
                    dstl = logits_tm[:, ch, :]
                    tlv = k.op("vector", lambda e, d=dstl: e.tensor_copy(
                        d, bank[3][0:128, 0:8]))
                    k.wait("tensor", tlv)

        k.barrier()
        k.flush()

    if phases < 2:
        _stub_out(k, nc, out, es)
        k.flush()
        es.close()
        return nc
    t_ag_h2 = k.cc("AllGather", OP.bypass, [h2_bounce.opt()], [h2_all.opt()])

    with ExitStack() as p2:
        sb2 = lambda name, shape, dt: p2.enter_context(nc.sbuf_tensor(name, shape, dt))
        mx = sb2("mx", [128, NCH], F32)
        ex = sb2("ex", [128, NCH * 8], F32)
        iot = sb2("iot", [128, NCH * 8], F32)
        cand = sb2("cand", [128, NCH * 8], F32)
        idxf = sb2("idxf", [128, NCH], F32)
        oh_f = sb2("oh_f", [128, NCH * 8], F32)
        oh_b = sb2("oh_b", [128, NCH * 8], BF16)
        cum_sb = sb2("cum_sb", [128, NCH * 8], F32)
        tpr = sb2("tpr", [128, NCH], F32)
        ttot = sb2("ttot", [1, NCH * 8], F32)
        ttot32 = sb2("ttot32", [32, 8], F32)
        exrow = sb2("exrow", [32, 8], F32)
        offrow = sb2("offrow", [1, 8], F32)
        cmat = sb2("cmat", [8, 8], F32)
        cmatbig = sb2("cmatbig", [128, 8], F32)
        comb = sb2("comb", [32, 8], F32)
        combrow = sb2("combrow", [1, NCH * 8], F32)
        possel = sb2("possel", [128, NCH], F32)
        keep_s = sb2("keep_s", [128, NCH], F32)
        zf32 = sb2("zf32", [128, 428], F32)
        iota_p = sb2("iota_p", [128, 1], F32)
        sc_r = sb2("sc_r", [128, NCH], F32)
        sc_q = sb2("sc_q", [128, NCH], F32)
        sc_j = sb2("sc_j", [128, NCH], F32)
        sc_s = sb2("sc_s", [128, NCH], F32)
        sc_d = sb2("sc_d", [128, 1], F32)
        sc_u = sb2("sc_u", [128, NCH], U32)
        sc_t = sb2("sc_t", [128, NCH], U32)
        sc_d2 = sb2("sc_d2", [128, NCH], F32)
        payall4 = sb2("payall4", [128, NCH, 4], F32)
        tokb_s = sb2("tokb_s", [128, 1], F32)

        lg3 = logits_tm[:]
        i3 = iot[:].rearrange("p (a b) -> p a b", b=8)
        e3 = ex[:].rearrange("p (a b) -> p a b", b=8)
        o3 = oh_f[:].rearrange("p (a b) -> p a b", b=8)
        c3 = cand[:].rearrange("p (a b) -> p a b", b=8)

        tio = k.op("gpsimd", lambda e: e.iota(
            i3, [[0, NCH], [1, 8]], base=0, channel_multiplier=0,
            allow_small_or_imprecise_dtypes=True))
        tmx = k.eop("vector", lambda e: e.tensor_reduce(
            mx[:].unsqueeze(2), lg3, axis=AX.X, op=OP.max))
        mxb = mx[:].unsqueeze(2).broadcast_to([128, NCH, 8])
        k.wait("vector", tmx)
        tex = k.eop("vector", lambda e: e.tensor_tensor(e3, lg3, mxb, op=OP.subtract))
        k.wait("scalar", tex)
        tee = k.eop("scalar", lambda e: e.activation(ex[:], ex[:], AF.Exp))
        k.wait("vector", tee)
        tsum = k.eop("vector", lambda e: e.tensor_reduce(
            tpr[:].unsqueeze(2), e3, axis=AX.X, op=OP.add))
        k.wait("scalar", tsum)
        k.eop("scalar", lambda e: e.activation(tpr[:], tpr[:], AF.Ln))
        ttp = k.eop("scalar", lambda e: e.activation(tpr[:], tpr[:], AF.Exp, scale=-1.0))
        k.wait("vector", tio)
        teq = k.eop("vector", lambda e: e.tensor_tensor(o3, lg3, mxb, op=OP.is_equal))
        k.wait("vector", teq)
        k.eop("vector", lambda e: e.tensor_scalar_add(cand[:], iot[:], -1000.0))
        k.eop("vector", lambda e: e.tensor_mul(cand[:], cand[:], oh_f[:]))
        k.eop("vector", lambda e: e.tensor_scalar_add(cand[:], cand[:], 1000.0))
        tix = k.eop("vector", lambda e: e.tensor_reduce(
            idxf[:].unsqueeze(2), c3, axis=AX.X, op=OP.min))
        ixb = idxf[:].unsqueeze(2).broadcast_to([128, NCH, 8])
        k.wait("vector", tix)
        toh = k.eop("vector", lambda e: e.tensor_tensor(o3, i3, ixb, op=OP.is_equal))
        k.wait("vector", toh)
        tohb = k.eop("vector", lambda e: e.tensor_copy(oh_b[:], oh_f[:]))
        k.wait("tensor", tohb, t_const)
        tcm = None
        for ch in range(NCH):
            sl = slice(ch * 8, ch * 8 + 8)
            fn = lambda e, sl=sl, ch=ch: e.matmul(
                bank[0][:, sl], triu_b[:], oh_b[:, sl],
                start=(ch == 0), stop=True, skip_group_check=True)
            if ch < NCH - 1:
                k.raw("tensor", fn)
            else:
                tcm = k.op("tensor", fn)
        k.wait("vector", tcm)
        tcums = k.eop("vector", lambda e: e.tensor_copy(cum_sb[:], bank[0][:, 0:NCH * 8]))
        k.wait("sync", tcums)
        ttt = k.dma("sync", modrow_dram[0:1, 0:NCH * 8], cum_sb[127:128, :])
        k.wait("sync", ttt)
        trr = k.dma("sync", ttot32[:],
                    modrow_dram[0:1, 0:NCH * 8].rearrange("o (a b) -> (o a) b", b=8))
        k.wait("tensor", trr)
        tmm1 = k.op("tensor", lambda e: e.matmul(
            bank[1][0:32, 0:8], su32_f[:], ttot32[:], start=True, stop=True))
        tmm2 = k.op("tensor", lambda e: e.matmul(
            bank[2][0:1, 0:8], ones32_f[:], ttot32[:], start=True, stop=True))
        k.wait("vector", tmm1, tmm2)
        texr = k.eop("vector", lambda e: e.tensor_copy(exrow[:], bank[1][0:32, 0:8]))
        tctr = k.eop("vector", lambda e: e.tensor_copy(offrow[:], bank[2][0:1, 0:8]))
        k.wait("tensor", tctr)
        tcrepb = k.op("tensor", lambda e: e.matmul(
            bank[3][:, 0:8], ones_r_f[:], offrow[:], start=True, stop=True))
        k.wait("vector", tcrepb)
        tcmatb = k.eop("vector", lambda e: e.tensor_copy(cmatbig[:], bank[3][:, 0:8]))
        k.wait("sync", tcmatb)
        tcin = k.dma("sync", cnt_in, cmatbig[:])
        if debug:
            k.dma("sync", dbg["d_tt32"], ttot32[:], sem="od")
            k.dma("sync", dbg["d_ttot"], modrow_dram[0:1, 0:NCH * 8], sem="od")
        k.wait("gpsimd", tcin)
        t_ar = k.cc("AllGather", OP.bypass, [cnt_in.opt()], [cnt_out.opt()])
        k.wait("sync", t_ar)
        tcl = k.dma("sync", cmat[:], cnt_out.rearrange("(c r) e -> c r e", r=128)[:, 0, :])
        k.wait("tensor", tcl)
        toff = k.op("tensor", lambda e: e.matmul(
            bank[4][0:1, 0:8], cmask_s[:], cmat[:], start=True, stop=True))
        k.wait("vector", toff)
        toffr = k.eop("vector", lambda e: e.tensor_copy(offrow[:], bank[4][0:1, 0:8]))
        k.wait("tensor", toffr)
        torep = k.op("tensor", lambda e: e.matmul(
            bank[5][0:32, 0:8], ones_r_f[:, 0:32], offrow[:], start=True, stop=True))
        k.wait("vector", torep, texr)
        tcomb = k.eop("vector", lambda e: e.tensor_add(comb[:], exrow[:], bank[5][0:32, 0:8]))
        k.wait("sync", tcomb)
        tcw = k.dma("sync", modrow_dram[1:2, 0:NCH * 8].rearrange("o (a b) -> (o a) b", b=8),
                    comb[:])
        k.wait("sync", tcw)
        tcrow = k.dma("sync", combrow[:], modrow_dram[1:2, 0:NCH * 8])
        k.wait("tensor", tcrow)
        tcrep2 = k.op("tensor", lambda e: e.matmul(
            bank[5][:, 0:NCH * 8], ones_r_f[:], combrow[:], start=True, stop=True))
        k.wait("vector", tcrep2, tcums, toh)
        k.eop("vector", lambda e: e.tensor_add(cand[:], cum_sb[:], bank[5][:, 0:NCH * 8]))
        k.eop("vector", lambda e: e.tensor_mul(cand[:], cand[:], oh_f[:]))
        tps = k.eop("vector", lambda e: e.tensor_reduce(
            possel[:].unsqueeze(2), c3, axis=AX.X, op=OP.add))
        k.wait("vector", tps, ttp)
        tkp = k.eop("vector", lambda e: e.tensor_scalar(
            keep_s[:], possel[:], float(CAP), None, op0=OP.is_le))
        k.wait("vector", tkp)
        tgt = k.eop("vector", lambda e: e.tensor_mul(gate_s[:], tpr[:], keep_s[:]))
        k.eop("vector", lambda e: e.tensor_mul(cand[:, 0:NCH], possel[:], keep_s[:]))
        k.eop("vector", lambda e: e.tensor_scalar(
            cand[:, 0:NCH], cand[:, 0:NCH], -1.0, 0.0, op0=OP.add, op1=OP.max))
        k.eop("vector", lambda e: e.scalar_tensor_tensor(
            cand[:, NCH:2 * NCH], idxf[:], float(CAP), cand[:, 0:NCH],
            op0=OP.mult, op1=OP.add))
        k.eop("vector", lambda e: e.tensor_scalar_max(
            cand[:, NCH:2 * NCH], cand[:, NCH:2 * NCH], 0.0))
        tg2 = k.eop("vector", lambda e: e.tensor_scalar_min(
            cand[:, NCH:2 * NCH], cand[:, NCH:2 * NCH], float(NCORES * CAP - 1)))
        k.wait("gpsimd", tg2)
        tidxg = k.dma("gpsimd", idxg_s[:], cand[:, NCH:2 * NCH])
        tz0 = k.op("gpsimd", lambda e: e.memset(zf32[:], 0.0))
        k.wait("sync", tz0)
        tzw = None
        zview = slotmap_in.rearrange("(p c) w -> p (c w)", p=128)
        for off in range(0, 1284, 428):
            w = min(428, 1284 - off)
            tzw = k.dma("sync", zview[:, off:off + w], zf32[:, 0:w])
        tip = k.op("gpsimd", lambda e: e.iota(
            iota_p[:], [[0, 1]], base=0, channel_multiplier=1,
            allow_small_or_imprecise_dtypes=True))
        k.wait("vector", tip, tkp)
        tr1 = k.eop("vector", lambda e: e.tensor_scalar_add(sc_r[:], possel[:], -1.0))
        k.eop("vector", lambda e: e.tensor_scalar_mul(sc_j[:], sc_r[:], 1.0 / 128.0))
        k.eop("vector", lambda e: e.tensor_scalar_add(sc_j[:], sc_j[:], 8388608.0))
        k.eop("vector", lambda e: e.tensor_scalar_add(sc_j[:], sc_j[:], -8388608.0))
        k.eop("vector", lambda e: e.scalar_tensor_tensor(
            sc_q[:], sc_j[:], -128.0, sc_r[:], op0=OP.mult, op1=OP.add))
        k.eop("vector", lambda e: e.tensor_scalar(
            sc_d2[:], sc_q[:], 0.0, None, op0=OP.is_lt))
        k.eop("vector", lambda e: e.tensor_sub(sc_j[:], sc_j[:], sc_d2[:]))
        k.eop("vector", lambda e: e.scalar_tensor_tensor(
            sc_q[:], sc_d2[:], 128.0, sc_q[:], op0=OP.mult, op1=OP.add))
        k.eop("vector", lambda e: e.scalar_tensor_tensor(
            sc_s[:], sc_q[:], 40.0, sc_j[:], op0=OP.mult, op1=OP.add))
        k.eop("vector", lambda e: e.scalar_tensor_tensor(
            sc_s[:], idxf[:], float(CAP), sc_s[:], op0=OP.mult, op1=OP.add))
        k.eop("vector", lambda e: e.tensor_scalar_add(sc_d[:], iota_p[:], float(NCORES * CAP)))
        k.eop("vector", lambda e: e.tensor_sub(sc_s[:], sc_s[:], sc_d[:].broadcast_to([128, NCH])))
        k.eop("vector", lambda e: e.tensor_mul(sc_s[:], sc_s[:], keep_s[:]))
        k.eop("vector", lambda e: e.tensor_add(
            sc_s[:], sc_s[:], sc_d[:].broadcast_to([128, NCH])))
        k.eop("vector", lambda e: e.tensor_scalar_max(sc_s[:], sc_s[:], 0.0))
        tslot = k.eop("vector", lambda e: e.tensor_scalar_min(
            sc_s[:], sc_s[:], float(NCORES * CAP + 127)))
        if debug:
            k.wait("sync", tslot)
            k.dma("sync", dbg["d_scdbg"][:, 0:NCH], sc_r[:], sem="od")
            k.dma("sync", dbg["d_scdbg"][:, NCH:2 * NCH], sc_j[:], sem="od")
            k.dma("sync", dbg["d_scdbg"][:, 2 * NCH:3 * NCH], sc_q[:], sem="od")
            k.dma("sync", dbg["d_scdbg"][:, 3 * NCH:4 * NCH], sc_s[:], sem="od")
        k.wait("gpsimd", tslot)
        tsc_u = k.dma("gpsimd", sc_u[:], sc_s[:])
        tpay0 = k.op("gpsimd", lambda e: e.iota(
            payall4[:], [[128, NCH], [0, 4]], base=1, channel_multiplier=1,
            allow_small_or_imprecise_dtypes=True))
        tb_ld = k.dma("sync", tokb_s[:], tokbase)
        k.wait("vector", tpay0, tb_ld)
        tpay = k.eop("vector", lambda e: e.tensor_scalar_add(
            payall4[:].rearrange("p a b -> p (a b)"),
            payall4[:].rearrange("p a b -> p (a b)"), tokb_s[:]))
        k.wait("gpsimd", tsc_u, tzw, tpay)
        tscat = None
        for ch in range(NCH):
            tscat = k.dma16(lambda e, ch=ch: e.indirect_dma_start(
                out=slotmap_in, out_offset=IndirectOffsetOnAxis(sc_u[:, ch:ch + 1], 0),
                in_=payall4[:, ch, :], in_offset=None))
        k.wait("gpsimd", tscat)
        t_ar_slot = k.cc("AllReduce", OP.add, [slotmap_in.opt()], [slotmap_out.opt()])

        if debug:
            k.wait("sync", tgt, tidxg, tmx, tix, tps, tkp, ttp, tcums)
            k.dma("sync", dbg["d_mx"], mx[:], sem="od")
            k.dma("sync", dbg["d_idxf"], idxf[:], sem="od")
            k.dma("sync", dbg["d_possel"], possel[:], sem="od")
            k.dma("sync", dbg["d_keep"], keep_s[:], sem="od")
            k.dma("sync", dbg["d_tpr"], tpr[:], sem="od")
            k.dma("sync", dbg["d_cum"], cum_sb[:], sem="od")
            k.dma("sync", dbg["d_gate"], gate_s[:], sem="od")
            k.wait("sync", tscat)
            k.dma("sync", dbg["d_scu"], sc_u[:], sem="od")
            k.dma("sync", dbg["d_idxg"], cand[:, NCH:2 * NCH], sem="od")
            k.dma("sync", dbg["d_cnt"], cnt_out.rearrange("(c r) e -> c r e", r=128)[:, 0, :], sem="od")
            k.dma("sync", dbg["d_logits"], logits_tm[:].rearrange("p a b -> p (a b)"),
                  sem="od")
            k.dma("sync", dbg["d_x2"], x2_dram, sem="od")
            k.dma("sync", dbg["d_h2"], h2_bounce, sem="od")

        k.barrier()
        k.flush()

    if phases < 3:
        _stub_out(k, nc, out, es)
        k.flush()
        es.close()
        return nc
    with ExitStack() as p3:
        sb3 = lambda name, shape, dt: p3.enter_context(nc.sbuf_tensor(name, shape, dt))
        slot4 = sb3("slot4", [128, CAP // 128 * 4], F32)
        srowf = sb3("srowf", [128, 1], F32)
        srowu = sb3("srowu", [128, 1], U32)
        slot4u = sb3("slot4u", [128, CAP // 128 * 4], U32)
        btm = [sb3(f"btm{i}", [128, 1024], BF16) for i in range(4)]
        w1t = [sb3(f"w1t{i}", [128, FF], BF16) for i in range(8)]
        w2t = [sb3(f"w2t{i}", [128, D], BF16) for i in range(32)]
        bufg = [sb3(f"bufg{i}", [128, 8, FFG], BF16) for i in range(2)]
        y1c = [sb3(f"y1c{i}", [128, FFG], BF16) for i in range(2)]
        eost = [sb3(f"eost{i}", [128, D], BF16) for i in range(2)]
        b2rep = sb3("b2rep", [128, D], F32)

        t_ld = k.dma("sync", rowsrc[:], b2_row)
        replicate_row(rowsrc[:], b2rep[:], [t_ld], 6)

        k.wait("sync", t_w1, t_w2)
        t_wf = None
        for i in range(8):
            t_wf = k.dma("sync", w1t[i][:], w1_b[i * 128:(i + 1) * 128, :])
        for i in range(32):
            t_wf = k.dma("sync", w2t[i][:], w2_b[i * 128:(i + 1) * 128, :])

        tsr = k.dma("sync", srowf[:], slotrow128)
        k.wait("gpsimd", tsr)
        tsru = k.dma("gpsimd", srowu[:], srowf[:])
        k.wait("gpsimd", tsru, t_ar_slot)
        tsl = k.dma16(lambda e: e.indirect_dma_start(
            out=slot4[:], out_offset=None,
            in_=slotmap_out, in_offset=IndirectOffsetOnAxis(srowu[:], 0)))
        k.wait("vector", tsl)
        k.eop("vector", lambda e: e.tensor_scalar(
            slot4[:], slot4[:], -1.0, 0.0, op0=OP.add, op1=OP.max))
        tsi0 = k.eop("vector", lambda e: e.tensor_scalar_min(
            slot4[:], slot4[:], float(NCORES * TOK - 1)))
        k.wait("gpsimd", tsi0)
        tsi = k.dma("gpsimd", slot4u[:], slot4[:])

        k.wait("gpsimd", t_ag_h2, tsi)

        gather_state = {"tgi": [None, None], "tw": {}, "tlast": [None, None]}

        def gather(g, gb, waitmm=None):
            toks = []
            for half in range(2):
                col = 2 * g + half
                iu = slot4u[:, 4 * col:4 * col + 1]
                k.wait("gpsimd", tsi, gather_state["tw"].get((gb, half)))
                tgi = k.dma16(lambda e, iu=iu, gb=gb, half=half: e.indirect_dma_start(
                    out=btm[2 * gb + half][:], out_offset=None,
                    in_=h2_all, in_offset=IndirectOffsetOnAxis(iu[:], 0)))
                gather_state["tgi"][half] = tgi
                k.wait("sync", tgi, gather_state["tlast"][gb])
                tw = k.dma("sync", ffn_tm[gb][half * 128:(half + 1) * 128, :],
                           btm[2 * gb + half][:])
                gather_state["tw"][(gb, half)] = tw
                toks.append(tw)
            k.wait("sync", *toks)
            if waitmm is not None:
                k.wait("sync", waitmm)
            tlast = None
            for dc in range(8):
                tlast = k.dma("sync", bufg[gb][:, dc, :],
                              ffn_tm[gb][:, dc * 128:(dc + 1) * 128], transpose=True)
            gather_state["tlast"][gb] = tlast
            if debug and g == 0:
                k.dma("sync", dbg["d_ffng0"], ffn_tm[0], sem="od")
            return tlast

        tga = [gather(0, 0), None]
        t_eout_w = [None, None]
        ts1_prev = [None, None]
        for g in range(NFFG):
            gb = g % 2
            k.wait("tensor", tga[gb], t_wf)
            last_mm1 = None
            for fc in range(32):
                pb = fc % 2
                k.wait("tensor", ts1_prev[pb])
                tmm1 = None
                for dc in range(8):
                    lhs = w1t[dc][:, fc * 128:(fc + 1) * 128]
                    rhs = bufg[gb][:, dc, :]
                    fn = lambda e, lhs=lhs, rhs=rhs, dc=dc, pb=pb: e.matmul(
                        bank[pb][:, 0:FFG], lhs, rhs, start=(dc == 0), stop=(dc == 7))
                    if dc < 7:
                        k.raw("tensor", fn)
                    else:
                        tmm1 = k.op("tensor", fn)
                k.wait("scalar", tmm1)
                b1ap = b1_s[:, fc:fc + 1]
                ts1 = k.op("scalar", lambda e, pb=pb, b1ap=b1ap: e.activation(
                    y1c[pb][:], bank[pb][:, 0:FFG], AF.Relu, bias=b1ap))
                ts1_prev[pb] = ts1
                k.wait("tensor", ts1)
                for tc in range(2):
                    lhs = y1c[pb][:, tc * 128:(tc + 1) * 128]
                    for dg in range(2):
                        rhs = w2t[fc][:, dg * 512:(dg + 1) * 512]
                        bi = 2 + tc * 2 + dg
                        fn = lambda e, lhs=lhs, rhs=rhs, bi=bi, fc=fc: e.matmul(
                            bank[bi][:, 0:512], lhs, rhs,
                            start=(fc == 0), stop=(fc == 31))
                        if fc == 31 and tc == 1 and dg == 1:
                            tmm2 = k.op("tensor", fn)
                        else:
                            k.raw("tensor", fn)
                last_mm1 = tmm1
            if g + 1 < NFFG:
                tga[(g + 1) % 2] = gather(g + 1, (g + 1) % 2, waitmm=last_mm1)
            for tc in range(2):
                ebuf = eost[tc]
                k.wait("vector", tmm2, t_eout_w[tc])
                tve = None
                for dg in range(2):
                    bi = 2 + tc * 2 + dg
                    dsl = slice(dg * 512, (dg + 1) * 512)
                    tve = k.op("vector", lambda e, bi=bi, dsl=dsl, ebuf=ebuf: e.tensor_add(
                        ebuf[:, dsl], bank[bi][:, 0:512], b2rep[:, dsl]))
                k.wait("sync", tve)
                r0 = g * FFG + tc * 128
                tst = k.dma("sync", eout_bounce[r0:r0 + 128, :], ebuf[:])
                t_eout_w[tc] = tst
                k.wait("tensor", tve)
        k.wait("gpsimd", *t_eout_w)
        t_ag_eo = k.cc("AllGather", OP.bypass, [eout_bounce.opt()], [eout_all.opt()])

        if debug:
            k.wait("sync", *t_eout_w)
            k.dma("sync", dbg["d_eout"], eout_bounce, sem="od")
            k.dma("sync", dbg["d_slotmap"], slotmap_out, sem="od")
            k.wait("sync", tsi)
            k.dma("sync", dbg["d_slot4"], slot4[:], sem="od")

        k.barrier()
        k.flush()

    if phases < 4:
        _stub_out(k, nc, out, es)
        k.flush()
        es.close()
        return nc
    with ExitStack() as p4:
        sb4 = lambda name, shape, dt: p4.enter_context(nc.sbuf_tensor(name, shape, dt))
        egst = [sb4(f"egst{i}", [128, D], BF16) for i in range(2)]
        x2l = [sb4(f"x2l{i}", [128, D], F32) for i in range(2)]
        otile = [sb4(f"otile{i}", [128, D], F32) for i in range(2)]
        ftmp = sb4("ftmp", [128, D], F32)
        gbrep_t = [sb4(f"gbrep_t{i}", [128, D], F32) for i in range(2)]

        for b in range(2):
            t_ld = k.dma("sync", rowsrc[:], modrow_dram[12 + b:13 + b, :])
            t_r = replicate_row(rowsrc[:], gbrep_t[b][:], [t_ld], 0)
            k.wait("sync", t_r)

        k.wait("gpsimd", t_ag_eo)
        k.wait("sync", t_ag_eo)
        tprev = [None, None]
        tso_prev = [None, None]
        tout_all = []
        for ch in range(NCH):
            sl = ch % 2
            b = ch // 16
            idxcol = idxg_s[:, ch:ch + 1]
            k.wait("gpsimd", tprev[sl])
            tg_ = k.dma16(lambda e, idxcol=idxcol, sl=sl: e.indirect_dma_start(
                out=egst[sl][:], out_offset=None,
                in_=eout_all, in_offset=IndirectOffsetOnAxis(idxcol, 0)))
            k.wait("sync", tprev[sl])
            tx_ = k.dma("sync", x2l[sl][:], x2_dram[ch * 128:(ch + 1) * 128, :])
            gate_ap = gate_s[:, ch:ch + 1]
            gb_ap = gbrep_t[b][:]
            k.wait("vector", tg_, tx_, tso_prev[sl])
            k.op("vector", lambda e, sl=sl, gate_ap=gate_ap, gb_ap=gb_ap:
                 e.scalar_tensor_tensor(ftmp[:], egst[sl][:], gate_ap, gb_ap,
                                        op0=OP.mult, op1=OP.mult))
            tvo = k.op("vector", lambda e, sl=sl: e.tensor_add(
                otile[sl][:], ftmp[:], x2l[sl][:]))
            tprev[sl] = tvo
            k.wait("sync", tvo)
            tso = k.dma("sync", out[ch * 128:(ch + 1) * 128, :], otile[sl][:], sem="od")
            tso_prev[sl] = tso
            tout_all.append(tso)
        k.wait("sync", tout_all[-1])
        k.barrier()
        k.flush()

    es.close()
    return nc


_CACHE = {}


def _get_nc(debug=False):
    if debug not in _CACHE:
        _CACHE[debug] = build_nc(debug)
    return _CACHE[debug]


def kernel(**inputs):
    return _run(inputs, debug=False)[0]


def _run(inputs, debug=False):
    f32 = np.float32
    x = np.asarray(inputs["L0_attn_x"], f32)
    cvec = np.asarray(inputs["L0_c"], f32)
    Wq = np.asarray(inputs["L0_attn_Wq"], f32); bq = np.asarray(inputs["L0_attn_bq"], f32)
    Wk = np.asarray(inputs["L0_attn_Wk"], f32); bk = np.asarray(inputs["L0_attn_bk"], f32)
    Wv = np.asarray(inputs["L0_attn_Wv"], f32); bv = np.asarray(inputs["L0_attn_bv"], f32)
    Wu = np.asarray(inputs["L0_attn_Wu"], f32); bu = np.asarray(inputs["L0_attn_bu"], f32)
    Wo = np.asarray(inputs["L0_attn_Wo"], f32); bo = np.asarray(inputs["L0_attn_bo"], f32)
    ascale = np.asarray(inputs["L0_attn_scale"], f32)
    gw = np.asarray(inputs["L0_ffn_gate_w"], f32); gb_ = np.asarray(inputs["L0_ffn_gate_b"], f32)
    w1 = np.asarray(inputs["L0_ffn_w1"], f32); b1 = np.asarray(inputs["L0_ffn_b1"], f32)
    w2 = np.asarray(inputs["L0_ffn_w2"], f32); b2 = np.asarray(inputs["L0_ffn_b2"], f32)
    wmod_ = np.asarray(inputs["L0_wmod"], f32)

    C = np.ascontiguousarray

    def fmvec(v):
        return C(v.reshape(8, 128).T)

    shared = {
        "wqT": C(Wq.T), "wkT": C(Wk.T), "wvT": C(Wv.T), "wuT": C(Wu.T),
        "woT": C(Wo.T), "wmod": C(wmod_),
        "gwT": C(gw.T.reshape(8, 128, 8).transpose(1, 0, 2).reshape(128, 64)),
        "bq_fm": fmvec(bq), "bk_fm": fmvec(bk), "bu_fm": fmvec(bu),
        "bk_row": bk.reshape(1, D), "bv_row": bv.reshape(1, D),
        "b2_row": None, "bo_row": bo.reshape(1, D),
        "gb_col": gb_.reshape(8, 1),
        "ascale_fm": fmvec(ascale),
    }
    in_maps = []
    for c in range(NCORES):
        xs = x[2 * c:2 * c + 2].reshape(TOK, D)
        m = dict(shared)
        m["x_tm"] = C(xs)
        m["x_fm"] = C(xs.T)
        cc_ = cvec[2 * c:2 * c + 2]
        m["cT"] = C(cc_.T.reshape(8, 128, 2).transpose(1, 0, 2).reshape(128, 16))
        m["w1e"] = C(w1[c]); m["w2e"] = C(w2[c])
        m["b1_fm"] = C(b1[c].reshape(32, 128).T)
        m["b2_row"] = b2[c].reshape(1, D)
        cm = np.zeros((8, 1), f32); cm[:c, 0] = 1.0
        m["core_mask"] = cm
        co = np.zeros((8, 1), f32); co[c, 0] = 1.0
        m["core_oh"] = co
        m["slotrow128"] = (float(c * CAP) + np.arange(128, dtype=f32)[:, None] * 40.0)
        m["tokbase"] = np.full((128, 1), float(TOK * c), f32)
        in_maps.append(m)

    nc = _get_nc(debug)
    res = run_bass_kernel_spmd(nc, in_maps, core_ids=list(range(NCORES)))
    outs = np.concatenate([res.results[c]["out"].reshape(2, N, D)
                           for c in range(NCORES)], axis=0)
    return outs, res
